# version 01
# baseline (speedup 1.0000x reference)
import contextlib
import numpy as np
import concourse.bass as bass
import concourse.mybir as mybir

F32 = mybir.dt.float32
BF16 = mybir.dt.bfloat16
AF = mybir.ActivationFunctionType
ALU = mybir.AluOpType
AX = mybir.AxisListType

S, D, H, HD = 2048, 1024, 16, 64
NT, NF, NP = 16, 8, 8
C = 128
EPS = 1e-5


class Stepper:
    def __init__(self, eng, sem, base):
        self.e, self.sem, self.v = eng, sem, base

    def __call__(self, ins):
        ins.then_inc(self.sem, 1)
        self.v += 1
        self.e.wait_ge(self.sem, self.v)
        return ins

    def last(self, ins):
        ins.then_inc(self.sem, 1)
        self.v += 1
        return ins


class MK:
    ENGINES = ("sync", "gpsimd", "vector", "scalar", "tensor")
    SEMMAP = {"vector": "V", "scalar": "S", "tensor": "T", "gpsimd": "G",
              "sync": "Y"}

    def __init__(self, upto="l1f"):
        self.upto = upto
        self.nc = bass.Bass("TRN2", target_bir_lowering=False)
        self.ops = {e: [] for e in self.ENGINES}
        self.semc = {}
        self.semh = {}
        self.es = contextlib.ExitStack()
        self.sb = {}
        self.ps = {}
        self.dram = {}

    def sem(self, name):
        if name not in self.semh:
            self.semh[name] = self.es.enter_context(self.nc.semaphore(name))
            self.semc[name] = 0
        return self.semh[name]

    def val(self, name):
        self.sem(name)
        return self.semc[name]

    def block(self, eng, nsteps, waits=(), tag=""):
        semname = self.SEMMAP[eng]
        self.sem(semname)
        base = self.semc[semname]
        self.semc[semname] += nsteps
        tot = self.semc[semname]
        wl = [(w[0], w[1]) for w in waits]

        class Res:
            total = tot

        def deco(fn):
            def run(e, fn=fn, wl=wl, base=base, tot=tot, semname=semname):
                for s, v in wl:
                    if v > 0:
                        e.wait_ge(self.semh[s], v)
                st = Stepper(e, self.semh[semname], base)
                fn(e, st)
                assert st.v == tot, f"{tag}: steps {st.v-base} != {tot-base}"
            self.ops[eng].append(run)
            return Res

        deco.total = tot
        return deco

    def sbuf(self, name, shape, dtype, scope=None):
        es = scope if scope is not None else getattr(self, "cur_es", None) or self.es
        h = es.enter_context(self.nc.sbuf_tensor(name, list(shape), dtype))
        self.sb[name] = h
        return h

    def push_scope(self):
        self.cur_es = contextlib.ExitStack()
        return self.cur_es

    def pop_scope(self):
        self.cur_es.close()
        self.cur_es = None

    def barrier(self, tag, extra=()):
        tv = {s: self.val(s) for s in ("T", "V", "S", "G")}
        ex = list(extra)
        for eng in ("tensor", "vector", "scalar", "gpsimd"):
            @self.block(eng, 0, waits=list(tv.items()) + ex,
                        tag=f"barw{tag}{eng}")
            def _(e, st):
                pass

    def psum(self, name, shape, dtype):
        h = self.es.enter_context(self.nc.psum_tensor(name, list(shape), dtype))
        self.ps[name] = h
        return h

    def din(self, name, shape, dtype=F32):
        self.dram[name] = self.nc.dram_tensor(name, list(shape), dtype,
                                              kind="ExternalInput")

    def dout(self, name, shape, dtype=F32):
        self.dram[name] = self.nc.dram_tensor(name, list(shape), dtype,
                                              kind="ExternalOutput")

    def declare_io(self):
        d = self.din
        d("x", [S, D])
        d("l0a_wqkv", [3 * D, D]); d("l0a_bqkv", [1, 3 * D])
        d("l0a_wo", [D, D]); d("l0a_bo", [1, D]); d("l0a_gamma", [1, H])
        order = ["l0a", "l0f", "l1a", "l1f"]
        self.nblocks = order.index(self.upto) + 1
        if self.nblocks >= 2:
            d("l0f_w1", [4 * D, D]); d("l0f_b1", [1, 4 * D])
            d("l0f_w2", [D, 4 * D]); d("l0f_b2", [1, D])
        if self.nblocks >= 3:
            d("l1a_wg", [D, D]); d("l1a_bg", [1, D])
            d("l1a_wv", [D, D]); d("l1a_bv", [1, D])
            d("l1a_wd", [D, D]); d("l1a_bd", [1, D])
        if self.nblocks >= 4:
            d("l1f_wqkv", [D, 3 * D]); d("l1f_g", [1, D])
            d("l1f_wproj", [D, D]); d("l1f_bproj", [1, D])
            self.dram["cc_in"] = self.nc.dram_tensor("cc_in", [1, 2], F32)
            self.dram["cc_out"] = self.nc.dram_tensor("cc_out", [1, 2], F32)
        self.dout("out", [S, D])

    def build(self):
        self.declare_io()
        with self.es:
            self.sbuf("XS", [128, NT * D], F32)
            self.sbuf("IDENT", [128, 128], BF16)
            self.sbuf("ONESROW", [1, 128], BF16)
            self.sbuf("SQJ", [128, D], BF16)
            self.sb["SQJ_global"] = self.sb["SQJ"]
            self.sbuf("STAT", [128, 6 * NT], F32)
            self.sbuf("COLS", [128, 32], F32)
            self.sbuf("TMPF", [128, D], F32)
            self.sbuf("ONESFR", [1, 128], F32)
            self.sbuf("ONESFC", [128, 1], F32)
            self.sbuf("CFH", [128, 1], F32)
            self.sbuf("CFK", [128, 2], F32)
            self.psum("PA", [128, 2048], F32)
            self.psum("PB", [128, 1024], F32)
            self.psum("PD", [128, 2048], BF16)

            self.phase_init()
            self.phase_l0a()
            if self.nblocks >= 2:
                self.phase_l0f()
            if self.nblocks >= 3:
                self.phase_l1a()
            if self.nblocks >= 4:
                self.phase_l1f()
            self.phase_store()

            block = self.es.enter_context(self.nc.Block())
            for eng in self.ENGINES:
                fns = self.ops[eng]
                getattr(block, eng)(lambda e, fns=fns: [f(e) for f in fns])
        return self.nc

    def prep_weight_ocmajor(self, dram_name, nrows, dst, pref, stgname=None):
        noc = nrows // 128
        stgname = stgname or f"stg_{dram_name}"
        stg = self.sb[stgname] if stgname in self.sb and \
            self.sb[stgname].shape[1] == 2 * D else \
            self.sbuf(stgname, [128, 2 * D], BF16)
        w = self.dram[dram_name].ap()
        for oc in range(noc):
            buf = oc % 2
            semn = f"{pref}{buf}"
            self.sem(semn)
            if not hasattr(self, "dma_load_sems"):
                self.dma_load_sems = set()
            prev_waits = ([(s, self.semc[s]) for s in self.dma_load_sems
                           if s != semn and self.semc[s] > 0]
                          if oc < 2 else [])
            self.dma_load_sems.add(semn)
            self.semc[semn] += 16
            lw = self.semc[semn]

            @self.block("gpsimd", 0,
                        waits=[("V", self.val("V") if oc >= 2 else 0),
                               ("S", self.val("S") if oc < 2 else 0),
                               ("T", self.val("T") if oc < 2 else 0)] + prev_waits,
                        tag=f"prep{dram_name}{oc}g")
            def _(g, st, oc=oc, buf=buf, semn=semn):
                g.dma_start(out=stg[:, buf * D:(buf + 1) * D],
                            in_=w[oc * 128:(oc + 1) * 128, :]
                            ).then_inc(self.semh[semn], 16)

            @self.block("tensor", 1,
                        waits=[(semn, lw), ("V", self.val("V")),
                               ("G", 5 if oc == 0 else 0)],
                        tag=f"prep{dram_name}{oc}t")
            def _(t, st, oc=oc, buf=buf):
                for fc in range(NF):
                    ins = t.transpose(self.ps["PD"][:, fc * 128:(fc + 1) * 128],
                                      stg[:, buf * D + fc * 128:buf * D + (fc + 1) * 128],
                                      self.sb["IDENT"][:, :])
                st.last(ins)

            @self.block("vector", 1, waits=[("T", self.val("T"))],
                        tag=f"prep{dram_name}{oc}v")
            def _(v, st, oc=oc):
                st.last(v.tensor_copy(dst[:, oc * 1024:(oc + 1) * 1024],
                                      self.ps["PD"][:, 0:1024]))

    def bulk_ln_stats(self, src_name, wait_semL, wait_valL, tag):
        ST = self.sb["STAT"]
        XS = self.sb[src_name]
        SQJ = self.sb["SQJ"]

        @self.block("vector", 1, waits=[(wait_semL, wait_valL)], tag=tag + "v1")
        def bv1(v, st):
            st.last(v.reduce_sum(ST[:, 0:NT],
                                 XS[:, :].rearrange("p (t d) -> p t d", d=D),
                                 axis=AX.X))

        @self.block("scalar", NT, waits=[(wait_semL, wait_valL)], tag=tag + "s1")
        def bs1(s, st):
            for i in range(NT):
                st(s.activation(SQJ[:, 0:D], XS[:, i * D:(i + 1) * D],
                                AF.Square, accum_out=ST[:, NT + i:NT + i + 1]))

        @self.block("vector", 4, waits=[("V", bv1.total), ("S", bs1.total)],
                    tag=tag + "v2")
        def bv2(v, st):
            mean, sq = ST[:, 0:NT], ST[:, NT:2 * NT]
            var, rcp = ST[:, 2 * NT:3 * NT], ST[:, 3 * NT:4 * NT]
            st(v.tensor_tensor(var, mean, mean, ALU.mult))
            st(v.scalar_tensor_tensor(var, var, -1.0 / D, sq, ALU.mult, ALU.add))
            st(v.tensor_scalar(var, var, 1.0 / D, EPS, ALU.mult, ALU.add))
            st.last(v.reciprocal(rcp, var))

        @self.block("scalar", 1, waits=[("V", bv2.total)], tag=tag + "s2")
        def bs2(s, st):
            st.last(s.activation(ST[:, 3 * NT:4 * NT], ST[:, 3 * NT:4 * NT],
                                 AF.Sqrt))

        @self.block("vector", 1, waits=[("S", bs2.total)], tag=tag + "v3")
        def bv3(v, st):
            st.last(v.scalar_tensor_tensor(ST[:, 4 * NT:5 * NT], ST[:, 0:NT],
                                           -1.0 / D, ST[:, 3 * NT:4 * NT],
                                           ALU.mult, ALU.mult))
        return bv3.total

    def postln_residual(self, c, psrc, width, wT, tag, scale_col=None, wV=0):
        CO = self.sb["COLS"]
        su, sq, var = CO[:, 8:9], CO[:, 9:10], CO[:, 10:11]
        istd, nm = CO[:, 11:12], CO[:, 12:13]
        XS = self.sb["XS"]
        TMP = self.sb["TMPF"]
        SQJ = self.sb["SQJ"]

        @self.block("vector", 2, waits=[("T", wT), ("V", wV)], tag=tag + "v1")
        def pv1(v, st):
            if scale_col is None:
                st(v.tensor_copy(TMP[:, 0:width], psrc[:, 0:width]))
            else:
                st(v.tensor_scalar(TMP[:, 0:width], psrc[:, 0:width],
                                   scale_col, None, ALU.mult))
            st.last(v.reduce_sum(su, TMP[:, 0:width], axis=AX.X))

        @self.block("scalar", 1, waits=[("V", pv1.total)], tag=tag + "s1")
        def ps1(s, st):
            st(s.activation(SQJ[:, 0:width], TMP[:, 0:width],
                            AF.Square, accum_out=sq))

        @self.block("vector", 4, waits=[("S", ps1.total)], tag=tag + "v2")
        def pv2(v, st):
            st(v.tensor_tensor(var, su, su, ALU.mult))
            st(v.scalar_tensor_tensor(var, var, -1.0 / width, sq, ALU.mult, ALU.add))
            st(v.tensor_scalar(var, var, 1.0 / width, EPS, ALU.mult, ALU.add))
            st.last(v.reciprocal(istd, var))

        @self.block("scalar", 1, waits=[("V", pv2.total)], tag=tag + "s2")
        def ps2(s, st):
            st.last(s.activation(istd, istd, AF.Sqrt))

        @self.block("vector", 3, waits=[("S", ps2.total)], tag=tag + "v3")
        def pv3(v, st, c=c):
            st(v.scalar_tensor_tensor(nm, su, -1.0 / width, istd, ALU.mult,
                                      ALU.mult))
            st(v.tensor_scalar(TMP[:, 0:width], TMP[:, 0:width], istd, nm,
                               ALU.mult, ALU.add))
            st.last(v.tensor_tensor(XS[:, c * D:(c + 1) * D],
                                    XS[:, c * D:(c + 1) * D], TMP[:, 0:width],
                                    ALU.add))
        return pv3.total

    def phase_init(self):
        xin = self.dram["x"].ap().rearrange("(t p) d -> p t d", p=128)
        self.sem("LX")
        self.semc["LX"] = 16

        @self.block("sync", 0, tag="loadx")
        def _(e, st, xin=xin):
            e.dma_start(out=self.sb["XS"][:, :].rearrange("p (t d) -> p t d", d=D),
                        in_=xin).then_inc(self.semh["LX"], 16)

        @self.block("vector", 4, tag="init_cfh")
        def icf(v, st):
            CFK, CFH = self.sb["CFK"], self.sb["CFH"]
            st(v.memset(CFK[:, 0:1], 0.3))
            st(v.tensor_scalar(CFK[:, 1:2], CFK[:, 0:1], 1.5 * 2.0 ** 23,
                               -(1.5 * 2.0 ** 23), ALU.add, ALU.add))
            st(v.tensor_tensor(CFH[:, :], CFK[:, 0:1], CFK[:, 1:2], ALU.is_equal))
            st.last(v.tensor_scalar(CFH[:, :], CFH[:, :], 0.5, None, ALU.mult))

        @self.block("gpsimd", 5, tag="init")
        def _(g, st):
            st(g.memset(self.sb["IDENT"][:, :], 1.0))
            st(g.memset(self.sb["ONESROW"][:, :], 1.0))
            st(g.memset(self.sb["ONESFR"][:, :], 1.0))
            st(g.memset(self.sb["ONESFC"][:, :], 1.0))
            st.last(g.affine_select(self.sb["IDENT"][:, :], self.sb["IDENT"][:, :],
                                    pattern=[[-1, 128]], base=0,
                                    channel_multiplier=1,
                                    compare_op=ALU.is_equal, fill=0.0))

    def phase_l0a(self):
        import os
        CUT = int(os.environ.get("MK_CUT", "9"))
        if CUT == 0:
            return
        scope = self.push_scope()
        WQKVT = self.sbuf("WQKVT", [128, 24 * 1024], BF16)
        WOT = self.sbuf("WOT", [128, 8 * 1024], BF16)
        QB = self.sbuf("QB", [128, 24], F32)
        KB8 = self.sbuf("KB8", [128, 8], F32)
        VBROW = self.sbuf("VBROW", [1, 1024], BF16)
        OBROW = self.sbuf("OBROW", [1, 1024], BF16)
        GRB = self.sbuf("GRB", [128, 16], F32)
        LNG = self.sbuf("LNG", [128, 16], F32)
        GCB = self.sbuf("GCB", [128, 16], F32)
        IJT = self.sbuf("IJT", [128, 128], F32)
        IP1 = self.sbuf("IP1", [128, 128], F32)
        JT64 = self.sbuf("JT64", [128, 64], F32)
        DECAY = self.sbuf("DECAY", [128, 2048], BF16)
        GAMT = self.sbuf("GAMT", [128, 1024], BF16)
        GKT = self.sbuf("GKT", [128, 1024], BF16)
        GIDT = self.sbuf("GIDT", [128, 512], BF16)
        XN = self.sbuf("XN", [128, D], BF16)
        XNT = self.sbuf("XNT", [128, 1024], BF16)
        QBUF = self.sbuf("QBUF", [128, 1024], BF16)
        KBUF = self.sbuf("KBUF", [128, 1024], BF16)
        QT = self.sbuf("QT", [128, 1024], BF16)
        VBUF = self.sbuf("VBUF", [128, 1024], BF16)
        SMT = self.sbuf("SMT", [128, 2048], BF16)
        KTB = self.sbuf("KTB", [128, 1024], BF16)
        RBUF = self.sbuf("RBUF", [128, 512], BF16)
        YBUF = self.sbuf("YBUF", [128, 1024], BF16)
        PA, PB, PD = (self.ps[k] for k in ("PA", "PB", "PD"))
        IDENT, ONESROW = self.sb["IDENT"], self.sb["ONESROW"]
        XS, ST = self.sb["XS"], self.sb["STAT"]

        self.sem("LG0")
        self.semc["LG0"] += 64
        lg0 = self.semc["LG0"]

        @self.block("gpsimd", 0, tag="l0a_smallloads")
        def _(g, st):
            bq = self.dram["l0a_bqkv"].ap()
            g.dma_start(out=QB[:, :],
                        in_=bq.rearrange("o (j p) -> p (o j)", p=128),
                        allow_slow_non_contiguous=True
                        ).then_inc(self.semh["LG0"], 16)
            g.dma_start(out=VBROW[:, :], in_=bq[:, 2048:3072]
                        ).then_inc(self.semh["LG0"], 16)
            g.dma_start(out=OBROW[:, :], in_=self.dram["l0a_bo"].ap()
                        ).then_inc(self.semh["LG0"], 16)
            g.dma_start(out=GRB[:, :],
                        in_=self.dram["l0a_gamma"].ap().partition_broadcast(128),
                        allow_slow_non_contiguous=True
                        ).then_inc(self.semh["LG0"], 16)

        @self.block("scalar", 3, waits=[("LG0", lg0)], tag="l0a_gamma")
        def gs(s, st):
            st(s.activation(GRB[:, :], GRB[:, :], AF.Sigmoid))
            st(s.activation(LNG[:, :], GRB[:, :], AF.Ln))
            st.last(s.activation(GCB[:, :], LNG[:, :], AF.Exp, scale=float(C)))
        gb = gs

        @self.block("gpsimd", 4, tag="l0a_iotas")
        def gi(g, st):
            st(g.iota(IJT[:, :], pattern=[[1, 128]], base=0, channel_multiplier=-1,
                      allow_small_or_imprecise_dtypes=True))
            st(g.affine_select(IJT[:, :], IJT[:, :], pattern=[[1, 128]], base=0,
                               channel_multiplier=-1, compare_op=ALU.is_ge,
                               fill=1e30))
            st(g.iota(IP1[:, :], pattern=[[1, 128]], base=1, channel_multiplier=0,
                      allow_small_or_imprecise_dtypes=True))
            st.last(g.iota(JT64[:, :], pattern=[[0, 64]], base=C - 1,
                           channel_multiplier=-1,
                           allow_small_or_imprecise_dtypes=True))

        @self.block("scalar", 1, waits=[("G", gi.total)], tag="l0a_decays")
        def ds(s, st):
            for h in range(H):
                bI = (h % 2) * 8 + (h // 2)
                s.activation(DECAY[:, bI * 128:(bI + 1) * 128], IJT[:, :], AF.Exp,
                             scale=LNG[:, h:h + 1])
            for p in range(NP):
                s.activation(GAMT[0:64, p * 128:(p + 1) * 128], IP1[0:64, :],
                             AF.Exp, scale=LNG[0:64, 2 * p:2 * p + 1])
                s.activation(GAMT[64:128, p * 128:(p + 1) * 128], IP1[64:128, :],
                             AF.Exp, scale=LNG[64:128, 2 * p + 1:2 * p + 2])
            for h in range(H):
                kc = 64 * (h // 2) + 512 * (h % 2)
                ins = s.activation(GKT[:, kc:kc + 64], JT64[:, 0:64],
                                   AF.Exp, scale=LNG[:, h:h + 1])
            st.last(ins)

        @self.block("vector", 2, waits=[("LG0", lg0), ("G", 5), ("S", ds.total)], tag="l0a_misc")
        def vm(v, st):
            st(v.tensor_scalar(KB8[:, :], QB[:, 8:16], 0.125, None, ALU.mult))
            for p in range(NP):
                v.tensor_scalar(GIDT[0:64, p * 64:(p + 1) * 64],
                                IDENT[0:64, 0:64],
                                GCB[0:64, 2 * p:2 * p + 1], None, ALU.mult)
                ins = v.tensor_scalar(GIDT[64:128, p * 64:(p + 1) * 64],
                                      IDENT[0:64, 0:64],
                                      GCB[64:128, 2 * p + 1:2 * p + 2], None,
                                      ALU.mult)
            st.last(ins)

        @self.block("vector", 1, tag="l0a_rinit")
        def rz(v, st):
            st.last(v.memset(RBUF[:, :], 0.0))

        if CUT >= 2:
            self.prep_weight_ocmajor("l0a_wqkv", 3 * D, WQKVT, "LWA")
            self.prep_weight_ocmajor("l0a_wo", D, WOT, "LWB")

        if CUT < 3:
            return
        stats_done = self.bulk_ln_stats("XS", "LX", 16, "l0a_st")

        def ISTD(c):
            return ST[:, 3 * NT + c:3 * NT + c + 1]

        def NM(c):
            return ST[:, 4 * NT + c:4 * NT + c + 1]

        if CUT < 4:
            return
        for c in range(NT):
            @self.block("vector", 1,
                        waits=[("V", stats_done), ("T", self.val("T"))],
                        tag=f"l0aA{c}")
            def sA(v, st, c=c):
                st.last(v.tensor_scalar(XN[:, :], XS[:, c * D:(c + 1) * D],
                                        ISTD(c), NM(c), ALU.mult, ALU.add))

            @self.block("tensor", 1, waits=[("V", sA.total)], tag=f"l0aB{c}")
            def sB(t, st):
                for j in range(NF):
                    ins = t.transpose(PD[:, j * 128:(j + 1) * 128],
                                      XN[:, j * 128:(j + 1) * 128], IDENT[:, :])
                st.last(ins)

            @self.block("vector", 1, waits=[("T", sB.total)], tag=f"l0aBv{c}")
            def sBv(v, st):
                st.last(v.tensor_copy(XNT[:, :], PD[:, 0:1024]))

            if CUT < 5:
                continue
            @self.block("tensor", 1, waits=[("V", sBv.total)], tag=f"l0aC{c}")
            def sC(t, st):
                for oc in range(16):
                    for fc in range(NF):
                        ins = t.matmul(
                            PA[:, oc * 128:(oc + 1) * 128],
                            WQKVT[:, (oc * 8 + fc) * 128:(oc * 8 + fc + 1) * 128],
                            XNT[:, fc * 128:(fc + 1) * 128],
                            start=(oc % 4 == 0 and fc == 0),
                            stop=(oc % 4 == 3 and fc == NF - 1))
                st.last(ins)

            @self.block("vector", 1, waits=[("T", sC.total)], tag=f"l0aCv{c}")
            def sCv(v, st):
                for p in range(NP):
                    v.tensor_scalar(QBUF[:, p * 128:(p + 1) * 128],
                                    PA[:, p * 128:(p + 1) * 128],
                                    QB[:, p:p + 1], None, ALU.add)
                for p in range(NP):
                    ins = v.tensor_scalar(KBUF[:, p * 128:(p + 1) * 128],
                                          PA[:, (8 + p) * 128:(9 + p) * 128],
                                          0.125, KB8[:, p:p + 1], ALU.mult,
                                          ALU.add)
                st.last(ins)

            if CUT < 6:
                continue
            @self.block("tensor", 1, waits=[("V", sCv.total)], tag=f"l0aD{c}")
            def sD(t, st):
                for voc in range(8):
                    for fc in range(NF):
                        t.matmul(PB[:, voc * 128:(voc + 1) * 128],
                                 XNT[:, fc * 128:(fc + 1) * 128],
                                 WQKVT[:, ((16 + voc) * 8 + fc) * 128:
                                       ((16 + voc) * 8 + fc + 1) * 128],
                                 start=(voc % 4 == 0 and fc == 0), stop=False)
                for voc in range(8):
                    ins = t.matmul(PB[:, voc * 128:(voc + 1) * 128],
                                   ONESROW[:, :],
                                   VBROW[:, voc * 128:(voc + 1) * 128],
                                   start=False, stop=(voc % 4 == 3))
                st.last(ins)

            @self.block("vector", 1, waits=[("T", sD.total)], tag=f"l0aDv{c}")
            def sDv(v, st):
                st.last(v.tensor_copy(VBUF[:, :], PB[:, :]))

            if CUT < 7:
                continue
            @self.block("tensor", 1, waits=[("V", sCv.total)], tag=f"l0aE{c}")
            def sE(t, st):
                for p in range(NP):
                    t.matmul(PA[:, p * 128:(p + 1) * 128],
                             KBUF[0:64, p * 128:(p + 1) * 128],
                             QBUF[0:64, p * 128:(p + 1) * 128],
                             start=(p % 4 == 0), stop=(p % 4 == 3),
                             tile_position=(0, 0))
                    ins = t.matmul(PA[:, 1024 + p * 128:1024 + (p + 1) * 128],
                                   KBUF[64:128, p * 128:(p + 1) * 128],
                                   QBUF[64:128, p * 128:(p + 1) * 128],
                                   start=(p % 4 == 0), stop=(p % 4 == 3),
                                   tile_position=(64, 0))
                st.last(ins)

            @self.block("vector", 2, waits=[("T", sE.total)], tag=f"l0aEv{c}")
            def sEv(v, st):
                st(v.tensor_tensor(SMT[:, :], PA[:, :], DECAY[:, :], ALU.mult))
                st.last(v.tensor_tensor(QT[:, :], QBUF[:, :], GAMT[:, :],
                                        ALU.mult))

            @self.block("tensor", 1, waits=[("V", sBv.total)], tag=f"l0aF{c}")
            def sF(t, st):
                for h in range(H):
                    rh = (h % 2) * 64
                    p = h // 2
                    kc = 64 * p + 1024 * (h % 2)
                    ins = t.transpose(PD[:, kc:kc + 64],
                                      KBUF[rh:rh + 64, p * 128:(p + 1) * 128],
                                      IDENT[rh:rh + 64, rh:rh + 64])
                st.last(ins)

            @self.block("vector", 1, waits=[("T", sF.total)], tag=f"l0aFv{c}")
            def sFv(v, st):
                pdv = PD[:, :].rearrange("p (b q) -> p b q", q=1024)[:, :, 0:512]
                ktv = KTB[:, :].rearrange("p (b q) -> p b q", q=512)
                gkv = GKT[:, :].rearrange("p (b q) -> p b q", q=512)
                st.last(v.tensor_tensor(ktv, pdv, gkv, ALU.mult))

            if CUT < 8:
                continue
            @self.block("tensor", 2,
                        waits=[("V", max(sEv.total, sFv.total, sDv.total))],
                        tag=f"l0aG{c}")
            def sG(t, st):
                for p in range(NP):
                    t.matmul(PA[0:64, p * 128:(p + 1) * 128],
                             VBUF[:, (2 * p) * 64:(2 * p + 1) * 64],
                             SMT[:, p * 128:(p + 1) * 128],
                             start=(p % 4 == 0), stop=False,
                             tile_position=(0, 0), skip_group_check=True)
                    t.matmul(PA[64:128, 1024 + p * 128:1024 + (p + 1) * 128],
                             VBUF[:, (2 * p + 1) * 64:(2 * p + 2) * 64],
                             SMT[:, 1024 + p * 128:1024 + (p + 1) * 128],
                             start=(p % 4 == 0), stop=False,
                             tile_position=(0, 64), skip_group_check=True)
                    t.matmul(PA[0:64, p * 128:(p + 1) * 128],
                             RBUF[0:64, p * 64:(p + 1) * 64],
                             QT[0:64, p * 128:(p + 1) * 128],
                             start=False, stop=(p % 4 == 3),
                             tile_position=(0, 0), skip_group_check=True)
                    ins = t.matmul(PA[64:128, 1024 + p * 128:1024 + (p + 1) * 128],
                                   RBUF[64:128, p * 64:(p + 1) * 64],
                                   QT[64:128, p * 128:(p + 1) * 128],
                                   start=False, stop=(p % 4 == 3),
                                   tile_position=(64, 64), skip_group_check=True)
                st(ins)
                for p in range(NP):
                    t.matmul(PB[0:64, p * 64:(p + 1) * 64],
                             GIDT[0:64, p * 64:(p + 1) * 64],
                             RBUF[0:64, p * 64:(p + 1) * 64],
                             start=(p == 0), stop=False, tile_position=(0, 0),
                             skip_group_check=True)
                    t.matmul(PB[64:128, 512 + p * 64:512 + (p + 1) * 64],
                             GIDT[64:128, p * 64:(p + 1) * 64],
                             RBUF[64:128, p * 64:(p + 1) * 64],
                             start=(p == 0), stop=False, tile_position=(64, 64),
                             skip_group_check=True)
                    t.matmul(PB[0:64, p * 64:(p + 1) * 64],
                             KTB[:, p * 64:(p + 1) * 64],
                             VBUF[:, (2 * p) * 64:(2 * p + 1) * 64],
                             start=False, stop=(p == 7), tile_position=(0, 0),
                             skip_group_check=True)
                    ins = t.matmul(PB[64:128, 512 + p * 64:512 + (p + 1) * 64],
                                   KTB[:, 512 + p * 64:512 + (p + 1) * 64],
                                   VBUF[:, (2 * p + 1) * 64:(2 * p + 2) * 64],
                                   start=False, stop=(p == 7),
                                   tile_position=(0, 64), skip_group_check=True)
                st.last(ins)

            @self.block("vector", 4, waits=[("T", sG.total)], tag=f"l0aGv{c}")
            def sGv(v, st):
                st(v.tensor_copy(YBUF[0:64, :], PA[0:64, 0:1024]))
                st(v.tensor_copy(YBUF[64:128, :], PA[64:128, 1024:2048]))
                st(v.tensor_copy(RBUF[0:64, :], PB[0:64, 0:512]))
                st.last(v.tensor_copy(RBUF[64:128, :], PB[64:128, 512:1024]))

            if CUT < 9:
                continue
            @self.block("tensor", 1, waits=[("V", sGv.total)], tag=f"l0aH{c}")
            def sH(t, st):
                for ooc in range(8):
                    for fc in range(NF):
                        t.matmul(PA[:, ooc * 128:(ooc + 1) * 128],
                                 YBUF[:, fc * 128:(fc + 1) * 128],
                                 WOT[:, (ooc * 8 + fc) * 128:
                                     (ooc * 8 + fc + 1) * 128],
                                 start=(ooc % 4 == 0 and fc == 0), stop=False)
                for ooc in range(8):
                    ins = t.matmul(PA[:, ooc * 128:(ooc + 1) * 128],
                                   ONESROW[:, :],
                                   OBROW[:, ooc * 128:(ooc + 1) * 128],
                                   start=False, stop=(ooc % 4 == 3))
                st.last(ins)

            self.postln_residual(c, PA, D, sH.total, f"l0aI{c}")

        self.barrier("l0a")
        self.pop_scope()

    def prep_ternary(self, dram_name, nrows, ncols, dst, pref, mwb, mwi, tag):
        noc = nrows // 128
        nfc = ncols // 128
        dbuf = 2 if ncols <= 1024 else 1
        stg = self.sbuf(f"stgA_{dram_name}", [128, dbuf * ncols], F32)
        stgq = self.sbuf(f"stgQ_{dram_name}", [128, ncols], BF16)
        stgi = self.sbuf(f"stgI_{dram_name}", [128, ncols], mybir.dt.int8)
        SQJ = self.sb["SQJ"]
        CO = self.sb["COLS"]
        pab, acc = CO[:, 20:21], CO[:, 21:22]
        w = self.dram[dram_name].ap()
        PD, PB = self.ps["PD"], self.ps["PB"]

        @self.block("vector", 1, tag=tag + "z")
        def vz(v, st):
            st.last(v.memset(acc, 0.0))

        lastA = None
        for oc in range(noc):
            buf = oc % dbuf
            semn = f"{pref}{buf}"
            self.sem(semn)
            if not hasattr(self, "dma_load_sems"):
                self.dma_load_sems = set()
            prev_waits = ([(s, self.semc[s]) for s in self.dma_load_sems
                           if s != semn and self.semc[s] > 0]
                          if oc < dbuf else [])
            self.dma_load_sems.add(semn)
            self.semc[semn] += 16
            lw = self.semc[semn]

            @self.block("gpsimd", 0,
                        waits=[("S", self.val("S")),
                               ("V", self.val("V") if oc < dbuf else 0),
                               ("T", self.val("T") if oc < dbuf else 0)] + prev_waits,
                        tag=f"{tag}A{oc}g")
            def _(g, st, oc=oc, buf=buf, semn=semn):
                g.dma_start(out=stg[:, buf * ncols:(buf + 1) * ncols],
                            in_=w[oc * 128:(oc + 1) * 128, :]
                            ).then_inc(self.semh[semn], 16)

            @self.block("scalar", 1,
                        waits=[(semn, lw), ("S", self.val("S")),
                               ("V", self.val("V"))],
                        tag=f"{tag}A{oc}s")
            def sA(s, st, oc=oc, buf=buf):
                st.last(s.activation(SQJ[:, 0:ncols],
                                     stg[:, buf * ncols:(buf + 1) * ncols],
                                     AF.Abs, accum_out=pab))

            @self.block("vector", 1,
                        waits=[("S", sA.total), ("V", self.val("V"))],
                        tag=f"{tag}A{oc}v")
            def vA(v, st):
                st.last(v.tensor_add(acc, acc, pab))
            lastA = vA

        @self.block("vector", 1, waits=[("V", lastA.total)], tag=tag + "c")
        def vc(v, st):
            st.last(v.tensor_copy(CO[:, 22:23], acc))

        @self.block("tensor", 1, waits=[("V", vc.total)], tag=tag + "mm")
        def tmm(t, st):
            st.last(t.matmul(PB[0:1, 0:1], CO[:, 22:23], self.sb["ONESFC"][:, :],
                             start=True, stop=True))

        @self.block("vector", 1, waits=[("T", tmm.total)], tag=tag + "mv")
        def vmv(v, st):
            st.last(v.tensor_scalar(CO[0:1, 23:24], PB[0:1, 0:1],
                                    1.0 / (nrows * ncols), 1e-5, ALU.mult,
                                    ALU.max))

        @self.block("tensor", 1, waits=[("V", vmv.total)], tag=tag + "mb")
        def tmb(t, st):
            st.last(t.matmul(PB[:, 4:5], self.sb["ONESFR"][:, :],
                             CO[0:1, 23:24], start=True, stop=True))

        @self.block("vector", 2, waits=[("T", tmb.total)], tag=tag + "mw")
        def vmw(v, st):
            st(v.tensor_copy(mwb[:, :], PB[:, 4:5]))
            st.last(v.reciprocal(mwi[:, :], mwb[:, :]))

        nrounds = (nfc + 15) // 16
        for oc in range(noc):
            buf = oc % dbuf
            semn = f"{pref}{buf}"
            self.sem(semn)
            self.semc[semn] += 16
            lw = self.semc[semn]

            @self.block("gpsimd", 0,
                        waits=[("V", max(vmw.total, self.val("V") if oc >= dbuf else 0)),
                               ("S", self.val("S"))],
                        tag=f"{tag}B{oc}g")
            def _(g, st, oc=oc, buf=buf, semn=semn):
                g.dma_start(out=stg[:, buf * ncols:(buf + 1) * ncols],
                            in_=w[oc * 128:(oc + 1) * 128, :]
                            ).then_inc(self.semh[semn], 16)

            @self.block("scalar", 1,
                        waits=[(semn, lw), ("S", self.val("S")),
                               ("V", self.val("V"))],
                        tag=f"{tag}B{oc}sg")
            def sB(s, st, oc=oc, buf=buf):
                st.last(s.activation(SQJ[:, 0:ncols],
                                     stg[:, buf * ncols:(buf + 1) * ncols],
                                     AF.Sign))

            @self.block("vector", 5,
                        waits=[(semn, lw), ("V", max(vmw.total, self.val("V"))),
                               ("T", self.val("T")), ("S", sB.total)],
                        tag=f"{tag}B{oc}v")
            def vB(v, st, oc=oc, buf=buf):
                sl = stg[:, buf * ncols:(buf + 1) * ncols]
                sgn = SQJ[:, 0:ncols]
                st(v.tensor_scalar(sl, sl, mwi[:, :], 1.0, ALU.mult, ALU.min))
                st(v.tensor_scalar(sl, sl, -1.0, None, ALU.max))
                st(v.tensor_scalar(sgn, sgn, self.sb["CFH"][:, :], None,
                                   ALU.mult))
                st(v.scalar_tensor_tensor(stgi[:, :], sl, 0.0, sgn, ALU.add,
                                          ALU.add))
                st.last(v.tensor_copy(stgq[:, :], stgi[:, :]))

            for r in range(nrounds):
                f0 = 16 * r
                f1 = min(nfc, f0 + 16)

                @self.block("tensor", 1,
                            waits=[("V", vB.total if r == 0 else self.val("V"))],
                            tag=f"{tag}B{oc}t{r}")
                def tB(t, st, oc=oc, f0=f0, f1=f1):
                    for fc in range(f0, f1):
                        ins = t.transpose(PD[:, (fc - f0) * 128:(fc - f0 + 1) * 128],
                                          stgq[:, fc * 128:(fc + 1) * 128],
                                          self.sb["IDENT"][:, :])
                    st.last(ins)

                @self.block("vector", 1, waits=[("T", tB.total)],
                            tag=f"{tag}B{oc}e{r}")
                def vE(v, st, oc=oc, f0=f0, f1=f1):
                    st.last(v.tensor_copy(
                        dst[:, (oc * nfc + f0) * 128:(oc * nfc + f1) * 128],
                        PD[:, 0:(f1 - f0) * 128]))

    def phase_l0f(self):
        NC2 = 4 * D
        XS, CO, ST = self.sb["XS"], self.sb["COLS"], self.sb["STAT"]
        IDENT, ONESROW = self.sb["IDENT"], self.sb["ONESROW"]
        PA, PB, PD = (self.ps[k] for k in ("PA", "PB", "PD"))
        TMP = self.sb["TMPF"]
        qi2d = self.nc.dram_tensor("qi2_scratch", [S, NC2], mybir.dt.int8)
        import os
        DBG = int(os.environ.get("MK_DBG", "0"))
        if DBG:
            self.dout("dbg_h", [128, NC2])
            self.dout("dbg_t1", [128, 128])
            self.dout("dbg_mw", [1, 4])
            self.dout("dbg_qi", [128, D])
            self.dout("dbg_q2", [128, NC2])
        SCL2 = self.sbuf("SCL2", [128, NT], F32)
        MW2B = self.sbuf("MW2B", [128, 1], F32)
        MW2I = self.sbuf("MW2I", [128, 1], F32)
        _oldsqj = self.sb["SQJ"]
        self.sbuf("SQJA", [128, 4 * D], BF16)
        self.sb["SQJ"] = self.sb["SQJA"]
        SQJP = self.sb["SQJA"]

        self.push_scope()
        TERN1T = self.sbuf("TERN1T", [128, 32 * 8 * 128], BF16)
        MW1B = self.sbuf("MW1B", [128, 1], F32)
        MW1I = self.sbuf("MW1I", [128, 1], F32)
        B1ROW = self.sbuf("B1ROW", [1, NC2], BF16)
        QI = self.sbuf("QI", [128, D], BF16)
        QIT = self.sbuf("QIT", [128, D], BF16)
        SGN = self.sbuf("SGN", [128, NC2], BF16)
        HBUF = self.sbuf("HBUF", [128, NC2], BF16)
        T4K = self.sbuf("T4K", [128, NC2], F32)
        QI2C = self.sbuf("QI2C", [128, NC2], mybir.dt.int8)
        RSCB = self.sbuf("RSCB", [128, 1], BF16)
        RSROW = self.sbuf("RSROW", [1, 128], BF16)

        self.sem("LG1")
        self.semc["LG1"] += 16
        lg1 = self.semc["LG1"]

        @self.block("gpsimd", 0, tag="l0f_b1load")
        def _(g, st):
            g.dma_start(out=B1ROW[:, :], in_=self.dram["l0f_b1"].ap()
                        ).then_inc(self.semh["LG1"], 16)

        self.prep_ternary("l0f_w1", 4 * D, D, TERN1T, "LW1", MW1B, MW1I, "w1p")
        QI32 = self.sb["stgI_l0f_w1"]


        c_idx = {"su": 0, "sq": 1, "mean": 2, "sqd": 3, "m2": 4, "var": 5,
                 "veps": 6, "rcp1": 7, "istd": 13, "rv": 14, "rcp2": 15,
                 "r": 16, "d1": 17, "d2": 18, "mm": 19, "mx": 24, "mn": 25,
                 "t3": 26, "scale": 27, "alpha": 28, "beta": 29, "scol": 30,
                 "rscol": 31}

        def CC(n):
            i = c_idx[n]
            return CO[:, i:i + 1]

        for c in range(NT):
            @self.block("vector", 3, waits=[("V", self.val("V")),
                                            ("S", self.val("S"))],
                        tag=f"l0f1s{c}")
            def fv1(v, st, c=c):
                xt = XS[:, c * D:(c + 1) * D]
                st(v.reduce_sum(CC("su"), xt, axis=AX.X))
                st(v.reduce_max(CC("mx"), xt, axis=AX.X))
                st.last(v.tensor_reduce(CC("mn"), xt, axis=AX.X, op=ALU.min))

            @self.block("scalar", 1, waits=[("V", fv1.total)], tag=f"l0f1q{c}")
            def fs1(s, st, c=c):
                st.last(s.activation(SQJP[:, 0:D],
                                     XS[:, c * D:(c + 1) * D], AF.Square,
                                     accum_out=CC("sq")))

            @self.block("vector", 8, waits=[("S", fs1.total)], tag=f"l0f1c{c}")
            def fv2(v, st):
                st(v.tensor_scalar(CC("mean"), CC("su"), 1.0 / D, None, ALU.mult))
                st(v.tensor_scalar(CC("sqd"), CC("sq"), 1.0 / D, None, ALU.mult))
                st(v.tensor_tensor(CC("m2"), CC("mean"), CC("mean"), ALU.mult))
                st(v.tensor_tensor(CC("var"), CC("sqd"), CC("m2"), ALU.subtract))
                st(v.tensor_scalar(CC("veps"), CC("var"), EPS, None, ALU.add))
                st(v.reciprocal(CC("rcp1"), CC("veps")))
                st(v.tensor_tensor(CC("rv"), CC("var"), CC("rcp1"), ALU.mult))
                st.last(v.tensor_scalar(CC("rv"), CC("rv"), EPS, None, ALU.add))

            @self.block("scalar", 2, waits=[("V", fv2.total)], tag=f"l0f1r{c}")
            def fs2(s, st):
                st(s.activation(CC("istd"), CC("rcp1"), AF.Sqrt))
                st.last(s.activation(CC("rcp2"), CC("rv"), AF.Sqrt))

            @self.block("vector", 10, waits=[("S", fs2.total)], tag=f"l0f1d{c}")
            def fv3(v, st):
                st(v.reciprocal(CC("r"), CC("rcp2")))
                st(v.tensor_tensor(CC("d1"), CC("mx"), CC("mean"), ALU.subtract))
                st(v.tensor_tensor(CC("d2"), CC("mean"), CC("mn"), ALU.subtract))
                st(v.tensor_tensor(CC("mm"), CC("d1"), CC("d2"), ALU.max))
                st(v.tensor_tensor(CC("t3"), CC("mm"), CC("istd"), ALU.mult))
                st(v.tensor_tensor(CC("t3"), CC("t3"), CC("r"), ALU.mult))
                st(v.tensor_scalar(CC("t3"), CC("t3"), 1e-5, None, ALU.max))
                st(v.reciprocal(CC("scale"), CC("t3")))
                st(v.tensor_scalar(CC("scale"), CC("scale"), 127.0, None,
                                   ALU.mult))
                st.last(v.tensor_tensor(CC("alpha"), CC("istd"), CC("r"),
                                        ALU.mult))

            @self.block("vector", 1, waits=[("V", fv3.total)], tag=f"l0f1nm{c}")
            def fvnm(v, st):
                st.last(v.tensor_scalar(CC("d1"), CC("mean"), -1.0, None,
                                        ALU.mult))

            @self.block("scalar", 1, waits=[("V", fvnm.total)], tag=f"l0f1sg{c}")
            def fssg(s, st, c=c):
                st.last(s.activation(SGN[:, 0:D], XS[:, c * D:(c + 1) * D],
                                     AF.Sign, bias=CC("d1")))

            @self.block("vector", 7,
                        waits=[("S", fssg.total),
                               ("SDB", 64 if DBG and c == 1 else 0)],
                        tag=f"l0f1e{c}")
            def fv4(v, st, c=c):
                st(v.tensor_tensor(CC("alpha"), CC("alpha"), CC("scale"),
                                   ALU.mult))
                st(v.scalar_tensor_tensor(CC("beta"), CC("mean"), -1.0,
                                          CC("alpha"), ALU.mult, ALU.mult))
                st(v.tensor_scalar(TMP[:, :], XS[:, c * D:(c + 1) * D],
                                   CC("alpha"), CC("beta"), ALU.mult, ALU.add))
                st(v.tensor_scalar(SGN[:, 0:D], SGN[:, 0:D],
                                   self.sb["CFH"][:, :], None, ALU.mult))
                st(v.scalar_tensor_tensor(QI32[:, :], TMP[:, :], 0.0,
                                          SGN[:, 0:D], ALU.add, ALU.add))
                st(v.tensor_copy(QI[:, :], QI32[:, :]))
                st.last(v.scalar_tensor_tensor(CC("rscol"), CC("scale"), 1.0,
                                               MW1I[:, :], ALU.mult, ALU.mult))

            @self.block("tensor", 1, waits=[("V", fv4.total)], tag=f"l0f1t{c}")
            def ft1(t, st):
                for j in range(NF):
                    ins = t.transpose(PD[:, j * 128:(j + 1) * 128],
                                      QI[:, j * 128:(j + 1) * 128], IDENT[:, :])
                st.last(ins)

            @self.block("vector", 2, waits=[("T", ft1.total)], tag=f"l0f1u{c}")
            def fv5(v, st):
                st(v.tensor_copy(QIT[:, :], PD[:, 0:1024]))
                st.last(v.tensor_copy(RSCB[:, :], CC("rscol")))

            @self.block("tensor", 1, waits=[("V", fv5.total)], tag=f"l0f1w{c}")
            def ft2(t, st):
                st.last(t.matmul(PB[0:1, 8:136], RSCB[:, :], IDENT[:, :],
                                 start=True, stop=True))

            @self.block("vector", 1, waits=[("T", ft2.total)], tag=f"l0f1x{c}")
            def fv6(v, st):
                st.last(v.tensor_copy(RSROW[:, :], PB[0:1, 8:136]))

            @self.block("vector", 2, waits=[("V", fv6.total)], tag=f"l0f1y{c}")
            def fv7(v, st):
                st(v.reciprocal(CC("t3"), CC("scale")))
                st.last(v.tensor_tensor(CC("scol"), MW1B[:, :], CC("t3"), ALU.mult))

            for half in range(2):
                @self.block("tensor", 1,
                            waits=[("V", fv7.total), ("LG1", lg1),
                                   ("S", self.val("S") if half == 1 else 0)],
                            tag=f"l0f1m{c}h{half}")
                def ft3(t, st, half=half):
                    for o16 in range(16):
                        oc = half * 16 + o16
                        for fc in range(NF):
                            t.matmul(PA[:, o16 * 128:(o16 + 1) * 128],
                                     QIT[:, fc * 128:(fc + 1) * 128],
                                     TERN1T[:, (oc * 8 + fc) * 128:
                                            (oc * 8 + fc + 1) * 128],
                                     start=(o16 % 4 == 0 and fc == 0), stop=False)
                    for o16 in range(16):
                        oc = half * 16 + o16
                        ins = t.matmul(PA[:, o16 * 128:(o16 + 1) * 128],
                                       RSROW[:, :],
                                       B1ROW[:, oc * 128:(oc + 1) * 128],
                                       start=False, stop=(o16 % 4 == 3))
                    st.last(ins)

                @self.block("scalar", 4,
                            waits=[("T", ft3.total),
                                   ("SDB", 64 if DBG and c == 1 else 0)],
                            tag=f"l0f1g{c}h{half}")
                def fs3(s, st, half=half):
                    for q in range(4):
                        st(s.activation(
                            HBUF[:, half * 2048 + q * 512:half * 2048 + (q + 1) * 512],
                            PA[:, q * 512:(q + 1) * 512], AF.Gelu,
                            scale=CC("scol")))

            @self.block("scalar", 1, waits=[("S", self.val("S"))],
                        tag=f"l0f1h{c}")
            def fs4(s, st):
                st.last(s.activation(SQJP[:, 0:NC2], HBUF[:, :],
                                     AF.Square, accum_out=CC("sq")))

            @self.block("vector", 3, waits=[("S", fs4.total)], tag=f"l0f1i{c}")
            def fv8(v, st):
                st(v.reduce_max(CC("mx"), HBUF[:, :], axis=AX.X,
                                apply_absolute_value=True))
                st(v.tensor_scalar(CC("rv"), CC("sq"), 1.0 / NC2, EPS,
                                   ALU.mult, ALU.add))
                st.last(v.reciprocal(CC("rcp2"), CC("rv")))

            @self.block("scalar", 1, waits=[("V", fv8.total)], tag=f"l0f1j{c}")
            def fs5(s, st):
                st.last(s.activation(CC("rcp2"), CC("rcp2"), AF.Sqrt))

            @self.block("scalar", 1, waits=[("S", fs5.total)], tag=f"l0f1sg2{c}")
            def fssg2(s, st):
                st.last(s.activation(SGN[:, 0:NC2], HBUF[:, :], AF.Sign))

            @self.block("vector", 7, waits=[("S", fssg2.total)], tag=f"l0f1k{c}")
            def fv9(v, st, c=c):
                st(v.tensor_tensor(CC("t3"), CC("mx"), CC("rcp2"), ALU.mult))
                st(v.tensor_scalar(CC("t3"), CC("t3"), 1e-5, None, ALU.max))
                st(v.reciprocal(CC("scale"), CC("t3")))
                st(v.tensor_scalar(CC("scale"), CC("scale"), 127.0, None,
                                   ALU.mult))
                st(v.tensor_copy(SCL2[:, c:c + 1], CC("scale")))
                st(v.tensor_tensor(CC("alpha"), CC("rcp2"), CC("scale"),
                                   ALU.mult))
                st.last(v.tensor_scalar(T4K[:, :], HBUF[:, :], CC("alpha"),
                                        None, ALU.mult))

            @self.block("vector", 2,
                        waits=[("V", fv9.total), ("SQ2", self.val("SQ2")),
                               ("SDB", 64 if DBG and c == 1 else 0)],
                        tag=f"l0f1l{c}")
            def fv10(v, st):
                st(v.tensor_scalar(SGN[:, 0:NC2], SGN[:, 0:NC2],
                                   self.sb["CFH"][:, :], None, ALU.mult))
                st.last(v.scalar_tensor_tensor(QI2C[:, :], T4K[:, :], 0.0,
                                               SGN[:, 0:NC2], ALU.add, ALU.add))

            self.sem("SQ2")
            self.semc["SQ2"] += 16
            sq2t = self.semc["SQ2"]

            @self.block("sync", 0, waits=[("V", fv10.total)], tag=f"l0f1z{c}")
            def _(e, st, c=c):
                e.dma_start(out=qi2d.ap()[c * 128:(c + 1) * 128, :],
                            in_=QI2C[:, :]).then_inc(self.semh["SQ2"], 16)

            if DBG and c == 0:
                self.sem("SDB")
                self.sem("SDC")
                self.semc["SDB"] += 64
                self.semc["SDC"] += 16

                @self.block("gpsimd", 0, waits=[("V", fv10.total)], tag="dbg0")
                def _(g, st):
                    g.dma_start(out=self.dram["dbg_h"].ap(), in_=HBUF[:, :]
                                ).then_inc(self.semh["SDB"], 16)
                    g.dma_start(out=self.dram["dbg_t1"].ap(),
                                in_=TERN1T[:, 0:128]).then_inc(self.semh["SDB"], 16)
                    g.dma_start(out=self.dram["dbg_qi"].ap(), in_=QI[:, :]
                                ).then_inc(self.semh["SDB"], 16)
                    g.dma_start(out=self.dram["dbg_q2"].ap(),
                                in_=QI2C[:, :]).then_inc(self.semh["SDB"], 16)

                @self.block("vector", 1, waits=[], tag="dbgmw")
                def dmw(v, st):
                    st.last(v.tensor_copy(self.sb["COLS"][0:1, 26:27],
                                          MW1B[0:1, 0:1]))

                @self.block("sync", 0, waits=[("V", dmw.total)], tag="dbg1")
                def _(e, st):
                    e.dma_start(out=self.dram["dbg_mw"].ap()[:, 0:1],
                                in_=MW1B[0:1, 0:1]).then_inc(self.semh["SDC"], 16)

                @self.block("gpsimd", 0, waits=[("SDB", 64)], tag="dbg2")
                def _(g, st):
                    pass

        self.barrier("l0f1", extra=[("SQ2", self.val("SQ2"))] +
                     ([("SDB", self.val("SDB")), ("SDC", self.val("SDC"))]
                      if DBG else []))
        self.pop_scope()

        self.push_scope()
        TERN2T = self.sbuf("TERN2T", [128, 8 * 32 * 128], BF16)
        self.sbuf("SQJB", [128, 4 * D], BF16)
        self.sb["SQJ"] = self.sb["SQJB"]
        B2ROW = self.sbuf("B2ROW", [1, D], BF16)
        QB2 = self.sbuf("QB2", [128, NC2], BF16)
        QIT2 = self.sbuf("QIT2", [128, NC2], BF16)
        QI2L = self.sbuf("QI2L", [128, NC2], mybir.dt.int8)
        RSCB2 = self.sbuf("RSCB2", [128, 1], BF16)
        RSROW2 = self.sbuf("RSROW2", [1, 128], BF16)

        self.sem("LG2")
        self.semc["LG2"] += 16
        lg2 = self.semc["LG2"]

        @self.block("gpsimd", 0, tag="l0f_b2load")
        def _(g, st):
            g.dma_start(out=B2ROW[:, :], in_=self.dram["l0f_b2"].ap()
                        ).then_inc(self.semh["LG2"], 16)

        self.prep_ternary("l0f_w2", D, NC2, TERN2T, "LW2", MW2B, MW2I, "w2p")

        for c in range(NT):
            self.sem("LQ2")
            self.semc["LQ2"] += 16
            lq2 = self.semc["LQ2"]

            @self.block("gpsimd", 0, waits=[("V", self.val("V"))],
                        tag=f"l0f2l{c}")
            def _(g, st, c=c):
                g.dma_start(out=QI2L[:, :],
                            in_=qi2d.ap()[c * 128:(c + 1) * 128, :]
                            ).then_inc(self.semh["LQ2"], 16)

            @self.block("vector", 4, waits=[("LQ2", lq2), ("SQ2", self.val("SQ2"))],
                        tag=f"l0f2c{c}")
            def gv1(v, st, c=c):
                st(v.tensor_copy(QB2[:, :], QI2L[:, :]))
                st(v.reciprocal(CC("t3"), SCL2[:, c:c + 1]))
                st(v.tensor_tensor(CC("scol"), MW2B[:, :], CC("t3"), ALU.mult))
                st.last(v.scalar_tensor_tensor(CC("rscol"), SCL2[:, c:c + 1],
                                               1.0, MW2I[:, :], ALU.mult, ALU.mult))

            for r in range(2):
                @self.block("tensor", 1,
                            waits=[("V", gv1.total if r == 0 else self.val("V"))],
                            tag=f"l0f2t{c}r{r}")
                def gt1(t, st, r=r):
                    for j in range(16):
                        ins = t.transpose(PD[:, j * 128:(j + 1) * 128],
                                          QB2[:, (16 * r + j) * 128:
                                              (16 * r + j + 1) * 128],
                                          IDENT[:, :])
                    st.last(ins)

                @self.block("vector", 1, waits=[("T", gt1.total)],
                            tag=f"l0f2u{c}r{r}")
                def gv2(v, st, r=r):
                    st.last(v.tensor_copy(QIT2[:, r * 2048:(r + 1) * 2048],
                                          PD[:, 0:2048]))

            @self.block("vector", 1, waits=[("V", gv2.total)], tag=f"l0f2v{c}")
            def gv3(v, st):
                st.last(v.tensor_copy(RSCB2[:, :], CC("rscol")))

            @self.block("tensor", 1, waits=[("V", gv3.total)], tag=f"l0f2w{c}")
            def gt2(t, st):
                st.last(t.matmul(PB[0:1, 8:136], RSCB2[:, :], IDENT[:, :],
                                 start=True, stop=True))

            @self.block("vector", 1, waits=[("T", gt2.total)], tag=f"l0f2x{c}")
            def gv4(v, st):
                st.last(v.tensor_copy(RSROW2[:, :], PB[0:1, 8:136]))

            @self.block("tensor", 1, waits=[("V", gv4.total), ("LG2", lg2)],
                        tag=f"l0f2m{c}")
            def gt3(t, st):
                for o2c in range(8):
                    for fc2 in range(32):
                        t.matmul(PA[:, o2c * 128:(o2c + 1) * 128],
                                 QIT2[:, fc2 * 128:(fc2 + 1) * 128],
                                 TERN2T[:, (o2c * 32 + fc2) * 128:
                                        (o2c * 32 + fc2 + 1) * 128],
                                 start=(o2c % 4 == 0 and fc2 == 0), stop=False)
                for o2c in range(8):
                    ins = t.matmul(PA[:, o2c * 128:(o2c + 1) * 128],
                                   RSROW2[:, :],
                                   B2ROW[:, o2c * 128:(o2c + 1) * 128],
                                   start=False, stop=(o2c % 4 == 3))
                st.last(ins)

            self.postln_residual(c, PA, D, gt3.total, f"l0f2I{c}",
                                 scale_col=CC("scol"))

        self.barrier("l0f2")
        self.pop_scope()

    def phase_l1a(self):
        self.sb["SQJ"] = self.sb.get("SQJ_global", self.sb["SQJ"])
        XS, CO, ST = self.sb["XS"], self.sb["COLS"], self.sb["STAT"]
        IDENT, ONESROW = self.sb["IDENT"], self.sb["ONESROW"]
        PA, PB, PD = (self.ps[k] for k in ("PA", "PB", "PD"))
        TMP = self.sb["TMPF"]

        self.push_scope()
        ABUF = self.sbuf("ABUF", [128, NF * 2048], BF16)
        XSC = self.sbuf("XSC", [128, NF * 2048], BF16)
        GB = self.sbuf("GB", [128, 24], F32)
        SGT = self.sbuf("SGT", [128, 2048], BF16)

        self.sem("LG3")
        self.semc["LG3"] += 48
        lg3 = self.semc["LG3"]

        @self.block("gpsimd", 0, tag="l1a_bload")
        def _(g, st):
            for i, nm in enumerate(("l1a_bg", "l1a_bv", "l1a_bd")):
                g.dma_start(out=GB[:, 8 * i:8 * (i + 1)],
                            in_=self.dram[nm].ap().rearrange("o (j p) -> p (o j)", p=128),
                            allow_slow_non_contiguous=True
                            ).then_inc(self.semh["LG3"], 16)

        sub1 = contextlib.ExitStack()
        old_es = self.cur_es
        self.cur_es = sub1
        XNT = self.sbuf("XNT1a", [128, NF * 2048], BF16)
        XN1 = self.sbuf("XN1a", [128, D], BF16)
        WOC = self.sbuf("WOC1a", [128, 8 * 1024], BF16)

        stats_done = self.bulk_ln_stats("XS", "V", self.val("V"), "l1a_st")

        def ISTD(c):
            return ST[:, 3 * NT + c:3 * NT + c + 1]

        def NM(c):
            return ST[:, 4 * NT + c:4 * NT + c + 1]

        for c in range(NT):
            @self.block("vector", 1,
                        waits=[("V", stats_done), ("T", self.val("T"))],
                        tag=f"l1aA{c}")
            def aA(v, st, c=c):
                st.last(v.tensor_scalar(XN1[:, :], XS[:, c * D:(c + 1) * D],
                                        ISTD(c), NM(c), ALU.mult, ALU.add))

            @self.block("tensor", 1, waits=[("V", aA.total)], tag=f"l1aB{c}")
            def aB(t, st, c=c):
                for j in range(NF):
                    ins = t.transpose(PD[:, j * 128:(j + 1) * 128],
                                      XN1[:, j * 128:(j + 1) * 128], IDENT[:, :])
                st.last(ins)

            @self.block("vector", 1, waits=[("T", aB.total)], tag=f"l1aBv{c}")
            def aBv(v, st, c=c):
                pdv = PD[:, 0:1024].rearrange("p (f q) -> p f q", q=128)
                xv = XNT[:, :].rearrange("p (f q) -> p f q", q=2048)[
                    :, :, c * 128:(c + 1) * 128]
                st.last(v.tensor_copy(xv, pdv))

        for mi, (wnm, bcol0, kind) in enumerate(
                (("l1a_wv", 8, "tanh"), ("l1a_wg", 0, "sigmul"),
                 ("l1a_wd", 16, "sigaff"))):
            self.prep_weight_ocmajor(wnm, D, WOC, f"LWg{mi}", stgname="stg_l1a")
            for oc in range(NF):
                @self.block("tensor", 1,
                            waits=[("V", self.val("V")), ("S", self.val("S")),
                                   ("LG3", lg3)],
                            tag=f"l1aP{mi}o{oc}")
                def aP(t, st, oc=oc):
                    for fc in range(NF):
                        for q in range(4):
                            ins = t.matmul(
                                PA[:, q * 512:(q + 1) * 512],
                                WOC[:, (oc * 8 + fc) * 128:(oc * 8 + fc + 1) * 128],
                                XNT[:, fc * 2048 + q * 512:fc * 2048 + (q + 1) * 512],
                                start=(fc == 0), stop=(fc == NF - 1))
                    st.last(ins)

                @self.block("scalar", 1, waits=[("T", aP.total)],
                            tag=f"l1aE{mi}o{oc}")
                def aE(s, st, oc=oc, mi=mi, bcol0=bcol0, kind=kind):
                    fn = AF.Tanh if kind == "tanh" else AF.Sigmoid
                    dstb = XSC if kind == "tanh" else (SGT if kind == "sigmul"
                                                       else ABUF)
                    dst = (dstb[:, oc * 2048:(oc + 1) * 2048] if kind != "sigmul"
                           else SGT[:, :])
                    st.last(s.activation(dst, PA[:, :], fn,
                                         bias=GB[:, bcol0 + oc:bcol0 + oc + 1]))

                if kind == "sigmul":
                    @self.block("vector", 1, waits=[("S", aE.total)],
                                tag=f"l1aM{mi}o{oc}")
                    def aM(v, st, oc=oc):
                        sl = XSC[:, oc * 2048:(oc + 1) * 2048]
                        st.last(v.tensor_tensor(sl, sl, SGT[:, :], ALU.mult))
                elif kind == "sigaff":
                    @self.block("vector", 1, waits=[("S", aE.total)],
                                tag=f"l1aM{mi}o{oc}")
                    def aM(v, st, oc=oc):
                        sl = ABUF[:, oc * 2048:(oc + 1) * 2048]
                        st.last(v.tensor_scalar(sl, sl, 0.998, 0.001, ALU.mult,
                                                ALU.add))

        self.barrier("l1a_sub1")
        sub1.close()
        self.cur_es = old_es

        sub2 = contextlib.ExitStack()
        self.cur_es = sub2
        HTM = self.sbuf("HTM", [128, NT * D], BF16)
        HFC = self.sbuf("HFC", [128, 2048], BF16)

        for fc in range(NF):
            @self.block("vector", 1, waits=[("T", self.val("T"))],
                        tag=f"l1aS{fc}")
            def aS(v, st, fc=fc):
                st.last(v.tensor_tensor_scan(HFC[:, :],
                                             ABUF[:, fc * 2048:(fc + 1) * 2048],
                                             XSC[:, fc * 2048:(fc + 1) * 2048],
                                             0.0, ALU.mult, ALU.add))

            @self.block("tensor", 1, waits=[("V", aS.total)], tag=f"l1aT{fc}")
            def aT(t, st, fc=fc):
                for tt_ in range(NT):
                    ins = t.transpose(PD[:, tt_ * 128:(tt_ + 1) * 128],
                                      HFC[:, tt_ * 128:(tt_ + 1) * 128],
                                      IDENT[:, :])
                st.last(ins)

            @self.block("vector", 1, waits=[("T", aT.total)], tag=f"l1aTv{fc}")
            def aTv(v, st, fc=fc):
                pdv = PD[:, 0:2048].rearrange("p (t q) -> p t q", q=128)
                hv = HTM[:, :].rearrange("p (t q) -> p t q", q=1024)[
                    :, :, fc * 128:(fc + 1) * 128]
                st.last(v.tensor_copy(hv, pdv))

        for c in range(NT):
            self.postln_residual(c, HTM[:, c * D:(c + 1) * D], D,
                                 self.val("T"), f"l1aI{c}", wV=self.val("V"))

        self.barrier("l1a")
        sub2.close()
        self.cur_es = old_es
        self.pop_scope()

    def phase_l1f(self):
        REG = 5
        XS, CO, ST = self.sb["XS"], self.sb["COLS"], self.sb["STAT"]
        IDENT, ONESROW = self.sb["IDENT"], self.sb["ONESROW"]
        ONESFR = self.sb["ONESFR"]
        PA, PB, PD = (self.ps[k] for k in ("PA", "PB", "PD"))
        TMP = self.sb["TMPF"]
        SQJ = self.sb["SQJ_global"]
        self.sb["SQJ"] = SQJ
        nc = self.nc
        qf_d = nc.dram_tensor("qf_scratch", [D, 2048], BF16)
        kf_d = nc.dram_tensor("kf_scratch", [D, 2048], BF16)
        oc_d = nc.dram_tensor("ocf_scratch", [D, 2048], BF16)
        wq_d = nc.dram_tensor("wq_scratch", [128, 16], BF16)
        cc_in = self.dram["cc_in"]
        cc_out = self.dram["cc_out"]
        wqkv = self.dram["l1f_wqkv"].ap()

        self.push_scope()
        VPM = self.sbuf("VPM", [128, 16 * 1040], BF16)
        VAM = self.sbuf("VAM", [128, 16 * 1040], BF16)
        SIMS = self.sbuf("SIMS", [128, NT], F32)
        PMK = self.sbuf("PMK", [128, NT], F32)
        WQCL = self.sbuf("WQCL", [128, NT], F32)
        RGW = self.sbuf("RGW", [128, NT], F32)
        MAXN2 = self.sbuf("MAXN2", [128, 16], F32)
        GHATB = self.sbuf("GHATB", [128, D], BF16)
        BCOLS = self.sbuf("BCOLS", [128, 16], F32)
        THRB = self.sbuf("THRB", [128, 1], F32)
        CCS = self.sbuf("CCS", [1, 4], F32)
        SCL = self.sbuf("SCL", [128, 16], F32)
        DTC = self.sbuf("DTC", [128, 16], F32)
        KMXC = self.sbuf("KMXC", [128, 8], F32)
        SCB = self.sbuf("SCB", [128, 16], BF16)
        CCB = self.sbuf("CCB", [2, 1], F32)
        SRW = self.sbuf("SRW", [16, 1], F32)
        SRB = self.sbuf("SRB", [16, 1], BF16)
        SRR = self.sbuf("SRR", [1, 16], F32)

        self.sem("LG4")
        self.semc["LG4"] += 16
        lg4 = self.semc["LG4"]

        @self.block("gpsimd", 0, waits=[("V", self.val("V"))], tag="l1f_gload")
        def _(g, st):
            g.dma_start(out=GHATB[:, :],
                        in_=self.dram["l1f_g"].ap().partition_broadcast(128),
                        allow_slow_non_contiguous=True
                        ).then_inc(self.semh["LG4"], 16)

        @self.block("vector", 3, waits=[("LG4", lg4), ("S", self.val("S")),
                                        ("T", self.val("T"))], tag="l1f_gn1")
        def gn1(v, st):
            st(v.tensor_tensor(TMP[:, :], GHATB[:, :], GHATB[:, :], ALU.mult))
            st(v.reduce_sum(SCL[:, :].rearrange("p (h o) -> p h o", o=1),
                            TMP[:, :].rearrange("p (h e) -> p h e", e=64),
                            axis=AX.X))
            st.last(v.reciprocal(SCL[:, :], SCL[:, :]))

        @self.block("scalar", 1, waits=[("V", gn1.total)], tag="l1f_gn2")
        def gn2(s, st):
            st.last(s.activation(SCL[:, :], SCL[:, :], AF.Sqrt))

        @self.block("vector", 2, waits=[("S", gn2.total)], tag="l1f_gn3")
        def gn3(v, st):
            gv = GHATB[:, :].rearrange("p (h e) -> p h e", e=64)
            sv = SCL[:, :].rearrange("p (h o) -> p h o", o=1)
            st(v.memset(MAXN2[:, :], 0.0))
            st.last(v.tensor_tensor(gv, gv, sv.broadcast_to([128, 16, 64]),
                                    ALU.mult))

        sub1 = contextlib.ExitStack()
        outer1 = self.cur_es
        self.cur_es = sub1
        XNT = self.sbuf("XNT1f", [128, NF * 2048], BF16)
        XN1 = self.sbuf("XN1f", [128, D], BF16)
        WPART = self.sbuf("WPART", [128, 8 * 1024], BF16)
        WQS = self.sbuf("WQS", [128, 2 * 1024], BF16)
        QKE = self.sbuf("QKE", [128, 2048], BF16)
        QTM = self.sbuf("QTM", [128, D], BF16)
        QSQ = TMP

        stats_done = self.bulk_ln_stats("XS", "V", self.val("V"), "l1f_st")

        def ISTD(c):
            return ST[:, 3 * NT + c:3 * NT + c + 1]

        def NM(c):
            return ST[:, 4 * NT + c:4 * NT + c + 1]

        for c in range(NT):
            @self.block("vector", 1,
                        waits=[("V", stats_done), ("T", self.val("T"))],
                        tag=f"l1fA{c}")
            def fA(v, st, c=c):
                st.last(v.tensor_scalar(XN1[:, :], XS[:, c * D:(c + 1) * D],
                                        ISTD(c), NM(c), ALU.mult, ALU.add))

            @self.block("tensor", 1, waits=[("V", fA.total)], tag=f"l1fB{c}")
            def fB(t, st, c=c):
                for j in range(NF):
                    ins = t.transpose(PD[:, j * 128:(j + 1) * 128],
                                      XN1[:, j * 128:(j + 1) * 128], IDENT[:, :])
                st.last(ins)

            @self.block("vector", 1, waits=[("T", fB.total)], tag=f"l1fBv{c}")
            def fBv(v, st, c=c):
                pdv = PD[:, 0:1024].rearrange("p (f q) -> p f q", q=128)
                xv = XNT[:, :].rearrange("p (f q) -> p f q", q=2048)[
                    :, :, c * 128:(c + 1) * 128]
                st.last(v.tensor_copy(xv, pdv))

        self.sem("LWF0")
        self.dma_load_sems.add("LWF0")
        self.semc["LWF0"] += 16
        lwf_q = self.semc["LWF0"]

        @self.block("gpsimd", 0,
                    waits=[("V", self.val("V")), ("S", self.val("S")),
                           ("T", self.val("T"))] +
                    [(s, self.semc[s]) for s in sorted(self.dma_load_sems)
                     if s != "LWF0" and self.semc[s] > 0],
                    tag="l1f_wqload")
        def _(g, st):
            g.dma_start(out=WPART[:, :].rearrange("p (f o) -> p f o", o=1024),
                        in_=wqkv[:, 0:1024].rearrange("(f p) o -> p f o", p=128)
                        ).then_inc(self.semh["LWF0"], 16)

        for c in range(NT):
            @self.block("tensor", 1,
                        waits=[("LWF0", lwf_q), ("V", self.val("V")),
                               ("S", self.val("S"))], tag=f"l1fq{c}")
            def sq1(t, st, c=c):
                for og in range(2):
                    for fc in range(NF):
                        ins = t.matmul(
                            PB[:, og * 512:(og + 1) * 512],
                            XNT[:, fc * 2048 + c * 128:fc * 2048 + (c + 1) * 128],
                            WPART[:, fc * 1024 + og * 512:fc * 1024 + (og + 1) * 512],
                            start=(fc == 0), stop=(fc == NF - 1))
                st.last(ins)

            @self.block("vector", 7, waits=[("T", sq1.total)], tag=f"l1fqv{c}")
            def sq2(v, st, c=c):
                st(v.tensor_copy(QTM[:, :], PB[:, 0:1024]))
                st(v.tensor_tensor(QSQ[:, :], QTM[:, :], QTM[:, :], ALU.mult))
                st(v.reduce_sum(SCL[:, :].rearrange("p (h o) -> p h o", o=1),
                                QSQ[:, :].rearrange("p (h e) -> p h e", e=64),
                                axis=AX.X))
                st(v.tensor_tensor(MAXN2[:, :], MAXN2[:, :], SCL[:, :], ALU.max))
                st(v.tensor_tensor(QSQ[:, :], QTM[:, :], GHATB[:, :], ALU.mult))
                st(v.reduce_sum(DTC[:, :].rearrange("p (h o) -> p h o", o=1),
                                QSQ[:, :].rearrange("p (h e) -> p h e", e=64),
                                axis=AX.X))
                st.last(v.reciprocal(SCL[:, :], SCL[:, :]))

            @self.block("scalar", 1, waits=[("V", sq2.total)], tag=f"l1fqs{c}")
            def sq3(s, st):
                st.last(s.activation(SCL[:, :], SCL[:, :], AF.Sqrt))

            @self.block("vector", 2, waits=[("S", sq3.total)], tag=f"l1fqx{c}")
            def sq4(v, st, c=c):
                st(v.tensor_tensor(DTC[:, :], DTC[:, :], SCL[:, :], ALU.mult))
                st.last(v.reduce_sum(SIMS[:, c:c + 1], DTC[:, :], axis=AX.X))

        @self.block("vector", 5, waits=[("V", self.val("V"))], tag="l1f_mm")
        def smm(v, st):
            st(v.tensor_copy(SIMS[0:REG, 0:1], SIMS[0:REG, 1:2]))
            st(v.reduce_max(SCL[:, 0:1], SIMS[:, :], axis=AX.X))
            st(v.tensor_reduce(SCL[:, 1:2], SIMS[:, :], axis=AX.X, op=ALU.min))
            st(v.tensor_scalar(SCL[:, 1:2], SCL[:, 1:2], -1.0, None, ALU.mult))
            st.last(v.tensor_copy(SCB[:, 0:2], SCL[:, 0:2]))

        @self.block("tensor", 1, waits=[("V", smm.total)], tag="l1f_mmt")
        def smt(t, st):
            st.last(t.transpose(PD[0:2, 0:128], SCB[:, 0:2], IDENT[:, :]))

        @self.block("vector", 1, waits=[("T", smt.total)], tag="l1f_mmv")
        def smv(v, st):
            st.last(v.reduce_max(CCB[:, :], PD[0:2, 0:128], axis=AX.X))

        self.sem("LCC")
        self.sem("GCC")
        self.semc["LCC"] += 32
        self.semc["GCC"] += 1
        lcc = self.semc["LCC"]
        gcc = self.semc["GCC"]

        @self.block("gpsimd", 0, waits=[("V", smv.total)], tag="l1f_cc")
        def _(g, st):
            g.dma_start(out=cc_in.ap(), in_=CCB[:, :]).then_inc(self.semh["LCC"], 16)
            g.wait_ge(self.semh["LCC"], lcc - 16)
            g.collective_compute(
                "AllReduce", ALU.max,
                replica_groups=[list(range(8))],
                ins=[cc_in.ap().opt()],
                outs=[cc_out.ap().opt()],
            ).then_inc(self.semh["GCC"], 1)
            g.wait_ge(self.semh["GCC"], gcc)
            g.dma_start(out=CCS[:, 0:2], in_=cc_out.ap()).then_inc(self.semh["LCC"], 16)

        @self.block("vector", 2, waits=[("LCC", lcc)], tag="l1f_thr")
        def thr(v, st):
            st(v.tensor_scalar(CCS[:, 2:3], CCS[:, 0:1], 0.9, None, ALU.mult))
            st.last(v.scalar_tensor_tensor(CCS[:, 3:4], CCS[:, 1:2], -0.1,
                                           CCS[:, 2:3], ALU.mult, ALU.add))

        @self.block("tensor", 1, waits=[("V", thr.total)], tag="l1f_thrb")
        def thb(t, st):
            st.last(t.matmul(PB[:, 0:1], ONESFR[:, :], CCS[0:1, 3:4],
                             start=True, stop=True))

        @self.block("vector", 7, waits=[("T", thb.total)], tag="l1f_msk")
        def msk(v, st):
            st(v.tensor_copy(THRB[:, :], PB[:, 0:1]))
            st(v.tensor_scalar(PMK[:, :], SIMS[:, :], THRB[:, :], None, ALU.is_gt))
            st(v.memset(PMK[0:REG, 0:1], 1.0))
            st(v.tensor_copy(WQCL[:, :], PMK[:, :]))
            st(v.memset(WQCL[0:REG, 0:1], 0.5))
            st(v.memset(RGW[:, :], 1.0))
            st.last(v.memset(RGW[0:REG, 0:1], 2.0))

        for oc in range(16):
            isq = oc < 8
            if not isq:
                semn = f"LWK{oc % 2}"
                self.sem(semn)
                self.dma_load_sems.add(semn)
                self.semc[semn] += 16
                lwk = self.semc[semn]

                @self.block("gpsimd", 0,
                            waits=[("V", self.val("V") if oc >= 10 else 0),
                                   ("S", self.val("S") if oc < 10 else 0)],
                            tag=f"l1fk{oc}g")
                def _(g, st, oc=oc, semn=semn):
                    ko = 1024 + (oc - 8) * 128
                    g.dma_start(out=WQS[:, (oc % 2) * 1024:(oc % 2) * 1024 + 1024]
                                .rearrange("p (f o) -> p f o", o=128),
                                in_=wqkv[:, ko:ko + 128]
                                .rearrange("(f p) o -> p f o", p=128)
                                ).then_inc(self.semh[semn], 16)

            @self.block("tensor", 1,
                        waits=([(f"LWK{oc % 2}", self.semc[f"LWK{oc % 2}"])]
                               if not isq else []) +
                        [("V", self.val("V")), ("S", self.val("S"))],
                        tag=f"l1fk{oc}t")
            def qk1(t, st, oc=oc, isq=isq):
                for fc in range(NF):
                    for q in range(4):
                        if isq:
                            lh = WPART[:, fc * 1024 + oc * 128:fc * 1024 + (oc + 1) * 128]
                        else:
                            lh = WQS[:, (oc % 2) * 1024 + fc * 128:
                                     (oc % 2) * 1024 + (fc + 1) * 128]
                        ins = t.matmul(
                            PA[:, q * 512:(q + 1) * 512], lh,
                            XNT[:, fc * 2048 + q * 512:fc * 2048 + (q + 1) * 512],
                            start=(fc == 0), stop=(fc == NF - 1))
                st.last(ins)

            @self.block("vector", 2 if not isq else 1, waits=[("T", qk1.total)],
                        tag=f"l1fk{oc}v")
            def qk2(v, st, oc=oc, isq=isq):
                if isq:
                    st.last(v.tensor_scalar(QKE[:, :], PA[:, :], 0.125, None,
                                            ALU.mult))
                else:
                    st(v.tensor_copy(QKE[:, :], PA[:, :]))
                    st.last(v.reduce_max(KMXC[:, oc - 8:oc - 7], QKE[:, :],
                                         axis=AX.X, apply_absolute_value=True))

            self.sem("SQF")
            self.semc["SQF"] += 16
            sqf = self.semc["SQF"]

            @self.block("sync", 0, waits=[("V", qk2.total)], tag=f"l1fk{oc}s")
            def _(e, st, oc=oc, isq=isq):
                dst = qf_d if isq else kf_d
                ro = (oc % 8) * 128
                e.dma_start(out=dst.ap()[ro:ro + 128, :], in_=QKE[:, :]
                            ).then_inc(self.semh["SQF"], 16)

        @self.block("vector", 2, waits=[("V", self.val("V"))], tag="l1f_b1")
        def sb1(v, st):
            st(v.reduce_max(KMXC[:, 0:1], KMXC[:, :], axis=AX.X))
            st.last(v.tensor_copy(SCB[:, 0:1], KMXC[:, 0:1]))

        @self.block("tensor", 1, waits=[("V", sb1.total)], tag="l1f_b2")
        def sb2(t, st):
            st.last(t.transpose(PD[0:1, 0:128], SCB[:, 0:1], IDENT[:, :]))

        @self.block("vector", 2, waits=[("T", sb2.total)], tag="l1f_b3")
        def sb3(v, st):
            st(v.reduce_max(CCS[0:1, 2:3], PD[0:1, 0:128], axis=AX.X))
            st.last(v.tensor_copy(SCB[:, 0:16], MAXN2[:, :]))

        @self.block("tensor", 1, waits=[("V", sb3.total)], tag="l1f_b4")
        def sb4(t, st):
            st.last(t.transpose(PD[0:16, 0:128], SCB[:, 0:16], IDENT[:, :]))

        @self.block("vector", 1, waits=[("T", sb4.total)], tag="l1f_b5")
        def sb5(v, st):
            st.last(v.reduce_max(SRW[:, :], PD[0:16, 0:128], axis=AX.X))

        @self.block("scalar", 1, waits=[("V", sb5.total)], tag="l1f_b6")
        def sb6(s, st):
            st.last(s.activation(SRW[:, :], SRW[:, :], AF.Sqrt))

        @self.block("tensor", 1, waits=[("S", sb6.total)], tag="l1f_b7")
        def sb7(t, st):
            st.last(t.matmul(PB[0:16, 4:5], ONESFR[0:1, 0:16], CCS[0:1, 2:3],
                             start=True, stop=True))

        @self.block("vector", 2, waits=[("T", sb7.total)], tag="l1f_b8")
        def sb8(v, st):
            st(v.scalar_tensor_tensor(SRW[:, :], SRW[:, :], -1.0, PB[0:16, 4:5],
                                      ALU.mult, ALU.mult))
            st.last(v.tensor_copy(SRB[:, :], SRW[:, :]))

        @self.block("tensor", 1, waits=[("V", sb8.total)], tag="l1f_b9")
        def sb9(t, st):
            st.last(t.transpose(PD[0:1, 128:144], SRB[:, :],
                                IDENT[0:16, 0:16]))

        @self.block("vector", 1, waits=[("T", sb9.total)], tag="l1f_b10")
        def sb10(v, st):
            st.last(v.tensor_copy(SRR[:, :], PD[0:1, 128:144]))

        @self.block("tensor", 1, waits=[("V", sb10.total)], tag="l1f_b11")
        def sb11(t, st):
            st.last(t.matmul(PB[:, 8:24], ONESFR[:, :], SRR[:, :],
                             start=True, stop=True))

        @self.block("vector", 1, waits=[("T", sb11.total)], tag="l1f_b12")
        def sb12(v, st):
            st.last(v.tensor_copy(BCOLS[:, :], PB[:, 8:24]))

        self.semc["LWF0"] += 16
        lwf_v = self.semc["LWF0"]

        @self.block("gpsimd", 0,
                    waits=[("T", self.val("T")), ("SQF", self.val("SQF"))],
                    tag="l1f_wvload")
        def _(g, st):
            g.dma_start(out=WPART[:, :].rearrange("p (f o) -> p f o", o=1024),
                        in_=wqkv[:, 2048:3072].rearrange("(f p) o -> p f o", p=128)
                        ).then_inc(self.semh["LWF0"], 16)

        for c in range(NT):
            for og in range(2):
                @self.block("tensor", 1,
                            waits=[("LWF0", lwf_v), ("V", self.val("V"))],
                            tag=f"l1fv{c}o{og}")
                def vv1(t, st, c=c, og=og):
                    for fc in range(NF):
                        ins = t.matmul(
                            PB[:, 0:512],
                            XNT[:, fc * 2048 + c * 128:fc * 2048 + (c + 1) * 128],
                            WPART[:, fc * 1024 + og * 512:fc * 1024 + (og + 1) * 512],
                            start=(fc == 0), stop=(fc == NF - 1))
                    st.last(ins)

                @self.block("vector", 2, waits=[("T", vv1.total),
                                                ("V", msk.total)],
                            tag=f"l1fv{c}o{og}v")
                def vv2(v, st, c=c, og=og):
                    pbv = PB[:, 0:512].rearrange("p (h e) -> p h e", e=64)
                    vamv = VAM[:, c * 1040 + og * 520:c * 1040 + (og + 1) * 520] \
                        .rearrange("p (h e) -> p h e", e=65)[:, :, 0:64]
                    vpmv = VPM[:, c * 1040 + og * 520:c * 1040 + (og + 1) * 520] \
                        .rearrange("p (h e) -> p h e", e=65)[:, :, 0:64]
                    st(v.tensor_scalar(vamv, pbv, RGW[:, c:c + 1], None, ALU.mult))
                    st.last(v.tensor_scalar(vpmv, pbv, PMK[:, c:c + 1], None,
                                            ALU.mult))

            @self.block("vector", 2, waits=[("V", self.val("V"))],
                        tag=f"l1fv{c}m")
            def vv3(v, st, c=c):
                vamc = VAM[:, c * 1040:(c + 1) * 1040].rearrange(
                    "p (h e) -> p h e", e=65)[:, :, 64:65]
                vpmc = VPM[:, c * 1040:(c + 1) * 1040].rearrange(
                    "p (h e) -> p h e", e=65)[:, :, 64:65]
                st(v.tensor_copy(vamc, RGW[:, c:c + 1].rearrange(
                    "p o -> p o ()" if False else "p (a o) -> p a o", o=1)
                    .broadcast_to([128, 16, 1])))
                st.last(v.tensor_copy(vpmc, PMK[:, c:c + 1].rearrange(
                    "p (a o) -> p a o", o=1).broadcast_to([128, 16, 1])))

        @self.block("vector", 1, waits=[("V", self.val("V"))], tag="l1f_wqb")
        def wqb(v, st):
            st.last(v.tensor_copy(SCB[:, 0:16], WQCL[:, :]))

        self.sem("LWQR")
        self.semc["LWQR"] += 16
        lwqr1 = self.semc["LWQR"]

        @self.block("sync", 0, waits=[("V", wqb.total)], tag="l1f_wqs")
        def _(e, st):
            e.dma_start(out=wq_d.ap(), in_=SCB[:, 0:16]
                        ).then_inc(self.semh["LWQR"], 16)

        self.barrier("l1f_s1", extra=[("SQF", self.val("SQF")),
                                      ("LWQR", self.val("LWQR"))])
        sub1.close()
        self.cur_es = outer1

        sub2 = contextlib.ExitStack()
        self.cur_es = sub2
        PTB = self.sbuf("PTB", [128, 16 * 1024], BF16)
        KFH = self.sbuf("KFH", [64, 2048], BF16)
        QFH = self.sbuf("QFH", [64, 2048], BF16)
        OPOSS = self.sbuf("OPOSS", [64, 1024], BF16)
        ONEGS = self.sbuf("ONEGS", [64, 1024], BF16)
        OCH = OPOSS
        R1 = self.sbuf("R1", [1, 1024], F32)
        R2 = self.sbuf("R2", [1, 1024], F32)
        WQROW = self.sbuf("WQROW", [1, 2048], BF16)
        WQ1ROW = self.sbuf("WQ1ROW", [1, 2048], BF16)

        self.semc["LWQR"] += 16
        lwqr2 = self.semc["LWQR"]

        @self.block("gpsimd", 0, waits=[("LWQR", lwqr1)], tag="l1f_wql")
        def _(g, st):
            g.dma_start(out=WQROW[:, :].rearrange("a (c p) -> a c p", p=128),
                        in_=wq_d.ap().rearrange("p c -> () c p"),
                        allow_slow_non_contiguous=True
                        ).then_inc(self.semh["LWQR"], 16)

        @self.block("vector", 1, waits=[("LWQR", lwqr2)], tag="l1f_wqr")
        def wqr(v, st):
            st.last(v.tensor_scalar(WQ1ROW[:, :], WQROW[:, :], -1.0, 1.0,
                                    ALU.mult, ALU.add))

        self.sem("SOC")
        for h in range(H):
            self.sem("LKH")
            self.semc["LKH"] += 32
            lkh = self.semc["LKH"]

            @self.block("gpsimd", 0,
                        waits=[("T", self.val("T")), ("S", self.val("S"))],
                        tag=f"l1fh{h}g")
            def _(g, st, h=h):
                g.dma_start(out=KFH[:, :], in_=kf_d.ap()[64 * h:64 * h + 64, :]
                            ).then_inc(self.semh["LKH"], 16)
                g.dma_start(out=QFH[:, :], in_=qf_d.ap()[64 * h:64 * h + 64, :]
                            ).then_inc(self.semh["LKH"], 16)

            for half in range(2):
                for kt in range(NT):
                    psPP = PB[:, 0:1024] if kt % 2 == 0 else PA[:, 0:1024]

                    @self.block("tensor", 1,
                                waits=[("LKH", lkh), ("S", self.val("S")),
                                       ("V", self.val("V") if kt == 0 else 0)],
                                tag=f"l1fh{h}f{half}s{kt}")
                    def hs1(t, st, h=h, half=half, kt=kt, psPP=psPP):
                        for q in range(2):
                            ins = t.matmul(
                                psPP[:, q * 512:(q + 1) * 512],
                                KFH[:, kt * 128:(kt + 1) * 128],
                                QFH[:, half * 1024 + q * 512:
                                    half * 1024 + (q + 1) * 512],
                                start=True, stop=True)
                        st.last(ins)

                    @self.block("scalar", 1, waits=[("T", hs1.total)],
                                tag=f"l1fh{h}f{half}e{kt}")
                    def hs2(s, st, h=h, kt=kt, psPP=psPP):
                        st.last(s.activation(PTB[:, kt * 1024:(kt + 1) * 1024],
                                             psPP, AF.Exp,
                                             bias=BCOLS[:, h:h + 1]))

                @self.block("tensor", 1, waits=[("S", self.val("S")),
                                                ("V", self.val("V"))],
                            tag=f"l1fh{h}f{half}pv")
                def hpv(t, st, h=h):
                    for kt in range(NT):
                        for q in range(2):
                            t.matmul(PA[0:65, q * 512:(q + 1) * 512],
                                     VPM[:, kt * 1040 + h * 65:
                                         kt * 1040 + (h + 1) * 65],
                                     PTB[:, kt * 1024 + q * 512:
                                         kt * 1024 + (q + 1) * 512],
                                     start=(kt == 0), stop=(kt == NT - 1),
                                     skip_group_check=True)
                            ins = t.matmul(PA[0:65, 1024 + q * 512:
                                              1024 + (q + 1) * 512],
                                           VAM[:, kt * 1040 + h * 65:
                                               kt * 1040 + (h + 1) * 65],
                                           PTB[:, kt * 1024 + q * 512:
                                               kt * 1024 + (q + 1) * 512],
                                           start=(kt == 0), stop=(kt == NT - 1),
                                           skip_group_check=True)
                    st.last(ins)

                @self.block("vector", 8, waits=[("T", hpv.total)],
                            tag=f"l1fh{h}f{half}cb")
                def hcb(v, st, h=h, half=half):
                    st(v.tensor_copy(OPOSS[:, :], PA[0:64, 0:1024]))
                    st(v.tensor_tensor(ONEGS[:, :], PA[0:64, 1024:2048],
                                       OPOSS[:, :], ALU.subtract))
                    st(v.reciprocal(R1[:, :], PA[64:65, 0:1024]))
                    st(v.tensor_copy(R2[:, :], PA[64:65, 0:1024]))
                    st(v.tensor_tensor(R2[:, :], PA[64:65, 1024:2048], R2[:, :],
                                       ALU.subtract))
                    st(v.reciprocal(R2[:, :], R2[:, :]))
                    st(v.tensor_tensor(R1[:, :], R1[:, :],
                                       WQROW[:, half * 1024:(half + 1) * 1024],
                                       ALU.mult))
                    st.last(v.tensor_tensor(R2[:, :], R2[:, :],
                                            WQ1ROW[:, half * 1024:(half + 1) * 1024],
                                            ALU.mult))

                @self.block("tensor", 2, waits=[("V", hcb.total)],
                            tag=f"l1fh{h}f{half}bc")
                def hbc(t, st):
                    t.matmul(PB[0:64, 0:512], ONESFR[0:1, 0:64], R1[:, 0:512],
                             start=True, stop=True)
                    st(t.matmul(PB[0:64, 512:1024], ONESFR[0:1, 0:64],
                                R1[:, 512:1024], start=True, stop=True))
                    t.matmul(PA[0:64, 0:512], ONESFR[0:1, 0:64], R2[:, 0:512],
                             start=True, stop=True)
                    st.last(t.matmul(PA[0:64, 512:1024], ONESFR[0:1, 0:64],
                                     R2[:, 512:1024], start=True, stop=True))

                @self.block("vector", 3, waits=[("T", hbc.total)],
                            tag=f"l1fh{h}f{half}cc")
                def hcc(v, st):
                    st(v.tensor_tensor(OPOSS[:, :], OPOSS[:, :], PB[0:64, 0:1024],
                                       ALU.mult))
                    st(v.tensor_tensor(ONEGS[:, :], ONEGS[:, :], PA[0:64, 0:1024],
                                       ALU.mult))
                    st.last(v.tensor_tensor(OCH[:, :], OPOSS[:, :], ONEGS[:, :],
                                            ALU.add))

                self.semc["SOC"] += 16
                soc = self.semc["SOC"]

                @self.block("sync", 0, waits=[("V", hcc.total)],
                            tag=f"l1fh{h}f{half}so")
                def _(e, st, h=h, half=half):
                    e.dma_start(out=oc_d.ap()[64 * h:64 * h + 64,
                                              half * 1024:(half + 1) * 1024],
                                in_=OCH[:, :]).then_inc(self.semh["SOC"], 16)

        self.barrier("l1f_s2", extra=[("SOC", self.val("SOC"))])
        sub2.close()
        sub3 = contextlib.ExitStack()
        self.cur_es = sub3
        WPROJ = self.sbuf("WPROJ", [128, 8 * 1024], BF16)
        OCT = self.sbuf("OCT", [128, 1024], BF16)
        BPROW = self.sbuf("BPROW", [1, 1024], BF16)

        self.sem("LG5")
        self.semc["LG5"] += 32
        lg5 = self.semc["LG5"]

        @self.block("gpsimd", 0, waits=[("SOC", self.val("SOC"))],
                    tag="l1f_wpload")
        def _(g, st):
            g.dma_start(out=WPROJ[:, :].rearrange("p (f o) -> p f o", o=1024),
                        in_=self.dram["l1f_wproj"].ap()
                        .rearrange("(f p) o -> p f o", p=128)
                        ).then_inc(self.semh["LG5"], 16)
            g.dma_start(out=BPROW[:, :], in_=self.dram["l1f_bproj"].ap()
                        ).then_inc(self.semh["LG5"], 16)

        for c in range(NT):
            self.sem("LOC")
            self.semc["LOC"] += 16
            loc = self.semc["LOC"]

            @self.block("gpsimd", 0,
                        waits=[("T", self.val("T")), ("SOC", self.val("SOC"))],
                        tag=f"l1fo{c}g")
            def _(g, st, c=c):
                g.dma_start(out=OCT[:, :].rearrange("p (f q) -> p f q", q=128),
                            in_=oc_d.ap()[:, c * 128:(c + 1) * 128]
                            .rearrange("(f p) q -> p f q", p=128)
                            ).then_inc(self.semh["LOC"], 16)

            @self.block("tensor", 1, waits=[("LOC", loc), ("LG5", lg5),
                                            ("V", self.val("V"))],
                        tag=f"l1fo{c}t")
            def op1(t, st, c=c):
                for fc in range(NF):
                    for q in range(2):
                        t.matmul(PA[:, q * 512:(q + 1) * 512],
                                 OCT[:, fc * 128:(fc + 1) * 128],
                                 WPROJ[:, fc * 1024 + q * 512:
                                       fc * 1024 + (q + 1) * 512],
                                 start=(fc == 0), stop=False)
                for q in range(2):
                    ins = t.matmul(PA[:, q * 512:(q + 1) * 512], ONESROW[:, :],
                                   BPROW[:, q * 512:(q + 1) * 512],
                                   start=False, stop=(q == 1))
                st.last(ins)

            self.postln_residual(c, PA, D, op1.total, f"l1fI{c}")

        self.barrier("l1f")
        sub3.close()
        self.cur_es = outer1
        self.pop_scope()

    def phase_store(self):
        outap = self.dram["out"].ap().rearrange("(t p) d -> p t d", p=128)
        self.sem("STO")
        self.semc["STO"] = 16

        @self.block("sync", 0, waits=[("V", self.val("V"))] +
                    ([("SDC", self.val("SDC"))] if "SDC" in self.semh else []),
                    tag="store")
        def _(e, st):
            e.dma_start(out=outap,
                        in_=self.sb["XS"][:, :].rearrange("p (t d) -> p t d", d=D)
                        ).then_inc(self.semh["STO"], 16)
            e.wait_ge(self.semh["STO"], 16)


def build(upto="l0a"):
    return MK(upto).build()


def input_map(inp, core, upto="l0a"):
    m = {
        "x": np.ascontiguousarray(inp["L0_attn_x"][core]),
        "l0a_wqkv": inp["L0_attn_Wqkv_w"],
        "l0a_bqkv": inp["L0_attn_Wqkv_b"].reshape(1, -1),
        "l0a_wo": inp["L0_attn_out_w"],
        "l0a_bo": inp["L0_attn_out_b"].reshape(1, -1),
        "l0a_gamma": inp["L0_attn_gamma_raw"].reshape(1, -1),
    }
    nb = ["l0a", "l0f", "l1a", "l1f"].index(upto) + 1
    if nb >= 2:
        m.update({"l0f_w1": inp["L0_ffn_w1"], "l0f_b1": inp["L0_ffn_b1"].reshape(1, -1),
                  "l0f_w2": inp["L0_ffn_w2"], "l0f_b2": inp["L0_ffn_b2"].reshape(1, -1)})
    if nb >= 3:
        m.update({"l1a_wg": inp["L1_attn_Wg"], "l1a_bg": inp["L1_attn_bg"].reshape(1, -1),
                  "l1a_wv": inp["L1_attn_Wv"], "l1a_bv": inp["L1_attn_bv"].reshape(1, -1),
                  "l1a_wd": inp["L1_attn_Wd"], "l1a_bd": inp["L1_attn_bd"].reshape(1, -1)})
    if nb >= 4:
        m.update({"l1f_wqkv": inp["L1_ffn_W_qkv"],
                  "l1f_g": inp["L1_ffn_g_info"][0, 0].reshape(1, -1),
                  "l1f_wproj": inp["L1_ffn_W_proj"],
                  "l1f_bproj": inp["L1_ffn_b_proj"].reshape(1, -1)})
    return m



def kernel(**inputs):
    from concourse.bass_utils import run_bass_kernel_spmd
    nc = build("l1f")
    in_maps = [input_map(inputs, core, "l1f") for core in range(8)]
    res = run_bass_kernel_spmd(nc, in_maps, core_ids=list(range(8)))
    out = np.stack([np.asarray(r["out"], np.float32) for r in res.results], axis=0)
    return out


# version 02
# speedup vs baseline: 1.0239x; 1.0239x over previous
import contextlib
import numpy as np
import concourse.bass as bass
import concourse.mybir as mybir

F32 = mybir.dt.float32
BF16 = mybir.dt.bfloat16
AF = mybir.ActivationFunctionType
ALU = mybir.AluOpType
AX = mybir.AxisListType

S, D, H, HD = 2048, 1024, 16, 64
NT, NF, NP = 16, 8, 8
C = 128
EPS = 1e-5


import os


def _needs_drain(ins):
    try:
        for o in ins.ins.outs:
            if "float32" not in str(o.dtype):
                return True
            n = 1
            for pair in list(o.ap)[1:]:
                n *= list(pair)[1]
            if n < 512:
                return True
        return False
    except Exception:
        return True


class Stepper:
    def __init__(self, eng, sem, base, semname=""):
        self.e, self.sem, self.v = eng, sem, base
        self.sw = semname == "V"

    def __call__(self, ins):
        ins.then_inc(self.sem, 1)
        self.v += 1
        if self.sw and _needs_drain(ins):
            self.e.wait_ge(self.sem, self.v)
        return ins

    last = __call__


class MK:
    ENGINES = ("sync", "gpsimd", "vector", "scalar", "tensor")
    SEMMAP = {"vector": "V", "scalar": "S", "tensor": "T", "gpsimd": "G",
              "sync": "Y"}

    def __init__(self, upto="l1f"):
        self.upto = upto
        self.nc = bass.Bass("TRN2", target_bir_lowering=False)
        self.ops = {e: [] for e in self.ENGINES}
        self.semc = {}
        self.semh = {}
        self.es = contextlib.ExitStack()
        self.sb = {}
        self.ps = {}
        self.dram = {}

    def sem(self, name):
        if name not in self.semh:
            self.semh[name] = self.es.enter_context(self.nc.semaphore(name))
            self.semc[name] = 0
        return self.semh[name]

    def val(self, name):
        self.sem(name)
        return self.semc[name]

    def block(self, eng, nsteps, waits=(), tag=""):
        semname = self.SEMMAP[eng]
        self.sem(semname)
        base = self.semc[semname]
        self.semc[semname] += nsteps
        tot = self.semc[semname]
        wl = [(w[0], w[1]) for w in waits if w[0] != semname]

        class Res:
            total = tot

        def deco(fn):
            def run(e, fn=fn, wl=wl, base=base, tot=tot, semname=semname):
                for s, v in wl:
                    if v > 0:
                        e.wait_ge(self.semh[s], v)
                st = Stepper(e, self.semh[semname], base, semname)
                fn(e, st)
                assert st.v == tot, f"{tag}: steps {st.v-base} != {tot-base}"
            self.ops[eng].append(run)
            return Res

        deco.total = tot
        return deco

    def sbuf(self, name, shape, dtype, scope=None):
        es = scope if scope is not None else getattr(self, "cur_es", None) or self.es
        h = es.enter_context(self.nc.sbuf_tensor(name, list(shape), dtype))
        self.sb[name] = h
        return h

    def push_scope(self):
        self.cur_es = contextlib.ExitStack()
        return self.cur_es

    def pop_scope(self):
        self.cur_es.close()
        self.cur_es = None

    def barrier(self, tag, extra=()):
        tv = {s: self.val(s) for s in ("T", "V", "S", "G")}
        ex = list(extra)
        for eng in ("tensor", "vector", "scalar", "gpsimd"):
            @self.block(eng, 0, waits=list(tv.items()) + ex,
                        tag=f"barw{tag}{eng}")
            def _(e, st):
                pass

    def psum(self, name, shape, dtype):
        h = self.es.enter_context(self.nc.psum_tensor(name, list(shape), dtype))
        self.ps[name] = h
        return h

    def din(self, name, shape, dtype=F32):
        self.dram[name] = self.nc.dram_tensor(name, list(shape), dtype,
                                              kind="ExternalInput")

    def dout(self, name, shape, dtype=F32):
        self.dram[name] = self.nc.dram_tensor(name, list(shape), dtype,
                                              kind="ExternalOutput")

    def declare_io(self):
        d = self.din
        d("x", [S, D])
        d("l0a_wqkv", [3 * D, D]); d("l0a_bqkv", [1, 3 * D])
        d("l0a_wo", [D, D]); d("l0a_bo", [1, D]); d("l0a_gamma", [1, H])
        order = ["l0a", "l0f", "l1a", "l1f"]
        self.nblocks = order.index(self.upto) + 1
        if self.nblocks >= 2:
            d("l0f_w1", [4 * D, D]); d("l0f_b1", [1, 4 * D])
            d("l0f_w2", [D, 4 * D]); d("l0f_b2", [1, D])
        if self.nblocks >= 3:
            d("l1a_wg", [D, D]); d("l1a_bg", [1, D])
            d("l1a_wv", [D, D]); d("l1a_bv", [1, D])
            d("l1a_wd", [D, D]); d("l1a_bd", [1, D])
        if self.nblocks >= 4:
            d("l1f_wqkv", [D, 3 * D]); d("l1f_g", [1, D])
            d("l1f_wproj", [D, D]); d("l1f_bproj", [1, D])
            self.dram["cc_in"] = self.nc.dram_tensor("cc_in", [1, 2], F32)
            self.dram["cc_out"] = self.nc.dram_tensor("cc_out", [1, 2], F32)
        self.dout("out", [S, D])

    def build(self):
        self.declare_io()
        with self.es:
            self.sbuf("XS", [128, NT * D], F32)
            self.sbuf("IDENT", [128, 128], BF16)
            self.sbuf("ONESROW", [1, 128], BF16)
            self.sbuf("SQJ", [128, D], BF16)
            self.sb["SQJ_global"] = self.sb["SQJ"]
            self.sbuf("STAT", [128, 6 * NT], F32)
            self.sbuf("COLS", [128, 32], F32)
            self.sbuf("TMPF", [128, D], F32)
            self.sbuf("ONESFR", [1, 128], F32)
            self.sbuf("ONESFC", [128, 1], F32)
            self.sbuf("CFH", [128, 1], F32)
            self.sbuf("CFK", [128, 2], F32)
            self.psum("PA", [128, 2048], F32)
            self.psum("PB", [128, 1024], F32)
            self.psum("PD", [128, 2048], BF16)

            self.phase_init()
            self.phase_l0a()
            if self.nblocks >= 2:
                self.phase_l0f()
            if self.nblocks >= 3:
                self.phase_l1a()
            if self.nblocks >= 4:
                self.phase_l1f()
            self.phase_store()

            block = self.es.enter_context(self.nc.Block())
            for eng in self.ENGINES:
                fns = self.ops[eng]
                getattr(block, eng)(lambda e, fns=fns: [f(e) for f in fns])
        return self.nc

    def prep_weight_ocmajor(self, dram_name, nrows, dst, pref, stgname=None):
        noc = nrows // 128
        stgname = stgname or f"stg_{dram_name}"
        stg = self.sb[stgname] if stgname in self.sb and \
            self.sb[stgname].shape[1] == 2 * D else \
            self.sbuf(stgname, [128, 2 * D], BF16)
        w = self.dram[dram_name].ap()
        for oc in range(noc):
            buf = oc % 2
            semn = f"{pref}{buf}"
            self.sem(semn)
            if not hasattr(self, "dma_load_sems"):
                self.dma_load_sems = set()
            prev_waits = ([(s, self.semc[s]) for s in self.dma_load_sems
                           if s != semn and self.semc[s] > 0]
                          if oc < 2 else [])
            self.dma_load_sems.add(semn)
            self.semc[semn] += 16
            lw = self.semc[semn]

            @self.block("gpsimd", 0,
                        waits=[("V", self.val("V") if oc >= 2 else 0),
                               ("S", self.val("S") if oc < 2 else 0),
                               ("T", self.val("T") if oc < 2 else 0)] + prev_waits,
                        tag=f"prep{dram_name}{oc}g")
            def _(g, st, oc=oc, buf=buf, semn=semn):
                g.dma_start(out=stg[:, buf * D:(buf + 1) * D],
                            in_=w[oc * 128:(oc + 1) * 128, :]
                            ).then_inc(self.semh[semn], 16)

            @self.block("tensor", 1,
                        waits=[(semn, lw), ("V", self.val("V")),
                               ("G", 5 if oc == 0 else 0)],
                        tag=f"prep{dram_name}{oc}t")
            def _(t, st, oc=oc, buf=buf):
                for fc in range(NF):
                    ins = t.transpose(self.ps["PD"][:, fc * 128:(fc + 1) * 128],
                                      stg[:, buf * D + fc * 128:buf * D + (fc + 1) * 128],
                                      self.sb["IDENT"][:, :])
                st.last(ins)

            @self.block("vector", 1, waits=[("T", self.val("T"))],
                        tag=f"prep{dram_name}{oc}v")
            def _(v, st, oc=oc):
                st.last(v.tensor_copy(dst[:, oc * 1024:(oc + 1) * 1024],
                                      self.ps["PD"][:, 0:1024]))

    def bulk_ln_stats(self, src_name, wait_semL, wait_valL, tag):
        ST = self.sb["STAT"]
        XS = self.sb[src_name]
        SQJ = self.sb["SQJ"]

        @self.block("vector", 1, waits=[(wait_semL, wait_valL)], tag=tag + "v1")
        def bv1(v, st):
            st.last(v.reduce_sum(ST[:, 0:NT],
                                 XS[:, :].rearrange("p (t d) -> p t d", d=D),
                                 axis=AX.X))

        @self.block("scalar", NT, waits=[(wait_semL, wait_valL)], tag=tag + "s1")
        def bs1(s, st):
            for i in range(NT):
                st(s.activation(SQJ[:, 0:D], XS[:, i * D:(i + 1) * D],
                                AF.Square, accum_out=ST[:, NT + i:NT + i + 1]))

        @self.block("vector", 4, waits=[("V", bv1.total), ("S", bs1.total)],
                    tag=tag + "v2")
        def bv2(v, st):
            mean, sq = ST[:, 0:NT], ST[:, NT:2 * NT]
            var, rcp = ST[:, 2 * NT:3 * NT], ST[:, 3 * NT:4 * NT]
            st(v.tensor_tensor(var, mean, mean, ALU.mult))
            st(v.scalar_tensor_tensor(var, var, -1.0 / D, sq, ALU.mult, ALU.add))
            st(v.tensor_scalar(var, var, 1.0 / D, EPS, ALU.mult, ALU.add))
            st.last(v.reciprocal(rcp, var))

        @self.block("scalar", 1, waits=[("V", bv2.total)], tag=tag + "s2")
        def bs2(s, st):
            st.last(s.activation(ST[:, 3 * NT:4 * NT], ST[:, 3 * NT:4 * NT],
                                 AF.Sqrt))

        @self.block("vector", 1, waits=[("S", bs2.total)], tag=tag + "v3")
        def bv3(v, st):
            st.last(v.scalar_tensor_tensor(ST[:, 4 * NT:5 * NT], ST[:, 0:NT],
                                           -1.0 / D, ST[:, 3 * NT:4 * NT],
                                           ALU.mult, ALU.mult))
        return bv3.total

    def postln_residual(self, c, psrc, width, wT, tag, scale_col=None, wV=0):
        CO = self.sb["COLS"]
        su, sq, var = CO[:, 8:9], CO[:, 9:10], CO[:, 10:11]
        istd, nm = CO[:, 11:12], CO[:, 12:13]
        XS = self.sb["XS"]
        TMP = self.sb["TMPF"]
        SQJ = self.sb["SQJ"]

        @self.block("vector", 2, waits=[("T", wT), ("V", wV)], tag=tag + "v1")
        def pv1(v, st):
            if scale_col is None:
                st(v.tensor_copy(TMP[:, 0:width], psrc[:, 0:width]))
            else:
                st(v.tensor_scalar(TMP[:, 0:width], psrc[:, 0:width],
                                   scale_col, None, ALU.mult))
            st.last(v.reduce_sum(su, TMP[:, 0:width], axis=AX.X))

        @self.block("scalar", 1, waits=[("V", pv1.total)], tag=tag + "s1")
        def ps1(s, st):
            st(s.activation(SQJ[:, 0:width], TMP[:, 0:width],
                            AF.Square, accum_out=sq))

        @self.block("vector", 4, waits=[("S", ps1.total)], tag=tag + "v2")
        def pv2(v, st):
            st(v.tensor_tensor(var, su, su, ALU.mult))
            st(v.scalar_tensor_tensor(var, var, -1.0 / width, sq, ALU.mult, ALU.add))
            st(v.tensor_scalar(var, var, 1.0 / width, EPS, ALU.mult, ALU.add))
            st.last(v.reciprocal(istd, var))

        @self.block("scalar", 1, waits=[("V", pv2.total)], tag=tag + "s2")
        def ps2(s, st):
            st.last(s.activation(istd, istd, AF.Sqrt))

        @self.block("vector", 3, waits=[("S", ps2.total)], tag=tag + "v3")
        def pv3(v, st, c=c):
            st(v.scalar_tensor_tensor(nm, su, -1.0 / width, istd, ALU.mult,
                                      ALU.mult))
            st(v.tensor_scalar(TMP[:, 0:width], TMP[:, 0:width], istd, nm,
                               ALU.mult, ALU.add))
            st.last(v.tensor_tensor(XS[:, c * D:(c + 1) * D],
                                    XS[:, c * D:(c + 1) * D], TMP[:, 0:width],
                                    ALU.add))
        return pv3.total

    def phase_init(self):
        xin = self.dram["x"].ap().rearrange("(t p) d -> p t d", p=128)
        self.sem("LX")
        self.semc["LX"] = 16

        @self.block("sync", 0, tag="loadx")
        def _(e, st, xin=xin):
            e.dma_start(out=self.sb["XS"][:, :].rearrange("p (t d) -> p t d", d=D),
                        in_=xin).then_inc(self.semh["LX"], 16)

        @self.block("vector", 4, tag="init_cfh")
        def icf(v, st):
            CFK, CFH = self.sb["CFK"], self.sb["CFH"]
            st(v.memset(CFK[:, 0:1], 0.3))
            st(v.tensor_scalar(CFK[:, 1:2], CFK[:, 0:1], 1.5 * 2.0 ** 23,
                               -(1.5 * 2.0 ** 23), ALU.add, ALU.add))
            st(v.tensor_tensor(CFH[:, :], CFK[:, 0:1], CFK[:, 1:2], ALU.is_equal))
            st.last(v.tensor_scalar(CFH[:, :], CFH[:, :], 0.5, None, ALU.mult))

        @self.block("gpsimd", 5, tag="init")
        def _(g, st):
            st(g.memset(self.sb["IDENT"][:, :], 1.0))
            st(g.memset(self.sb["ONESROW"][:, :], 1.0))
            st(g.memset(self.sb["ONESFR"][:, :], 1.0))
            st(g.memset(self.sb["ONESFC"][:, :], 1.0))
            st.last(g.affine_select(self.sb["IDENT"][:, :], self.sb["IDENT"][:, :],
                                    pattern=[[-1, 128]], base=0,
                                    channel_multiplier=1,
                                    compare_op=ALU.is_equal, fill=0.0))

    def phase_l0a(self):
        import os
        CUT = int(os.environ.get("MK_CUT", "9"))
        if CUT == 0:
            return
        scope = self.push_scope()
        WQKVT = self.sbuf("WQKVT", [128, 24 * 1024], BF16)
        WOT = self.sbuf("WOT", [128, 8 * 1024], BF16)
        QB = self.sbuf("QB", [128, 24], F32)
        KB8 = self.sbuf("KB8", [128, 8], F32)
        VBROW = self.sbuf("VBROW", [1, 1024], BF16)
        OBROW = self.sbuf("OBROW", [1, 1024], BF16)
        GRB = self.sbuf("GRB", [128, 16], F32)
        LNG = self.sbuf("LNG", [128, 16], F32)
        GCB = self.sbuf("GCB", [128, 16], F32)
        IJT = self.sbuf("IJT", [128, 128], F32)
        IP1 = self.sbuf("IP1", [128, 128], F32)
        JT64 = self.sbuf("JT64", [128, 64], F32)
        DECAY = self.sbuf("DECAY", [128, 2048], BF16)
        GAMT = self.sbuf("GAMT", [128, 1024], BF16)
        GKT = self.sbuf("GKT", [128, 1024], BF16)
        GIDT = self.sbuf("GIDT", [128, 512], BF16)
        XN = self.sbuf("XN", [128, D], BF16)
        XNT = self.sbuf("XNT", [128, 1024], BF16)
        QBUF = self.sbuf("QBUF", [128, 1024], BF16)
        KBUF = self.sbuf("KBUF", [128, 1024], BF16)
        QT = self.sbuf("QT", [128, 1024], BF16)
        VBUF = self.sbuf("VBUF", [128, 1024], BF16)
        SMT = self.sbuf("SMT", [128, 2048], BF16)
        KTB = self.sbuf("KTB", [128, 1024], BF16)
        RBUF = self.sbuf("RBUF", [128, 512], BF16)
        YBUF = self.sbuf("YBUF", [128, 1024], BF16)
        PA, PB, PD = (self.ps[k] for k in ("PA", "PB", "PD"))
        IDENT, ONESROW = self.sb["IDENT"], self.sb["ONESROW"]
        XS, ST = self.sb["XS"], self.sb["STAT"]

        self.sem("LG0")
        self.semc["LG0"] += 64
        lg0 = self.semc["LG0"]

        @self.block("gpsimd", 0, tag="l0a_smallloads")
        def _(g, st):
            bq = self.dram["l0a_bqkv"].ap()
            g.dma_start(out=QB[:, :],
                        in_=bq.rearrange("o (j p) -> p (o j)", p=128),
                        allow_slow_non_contiguous=True
                        ).then_inc(self.semh["LG0"], 16)
            g.dma_start(out=VBROW[:, :], in_=bq[:, 2048:3072]
                        ).then_inc(self.semh["LG0"], 16)
            g.dma_start(out=OBROW[:, :], in_=self.dram["l0a_bo"].ap()
                        ).then_inc(self.semh["LG0"], 16)
            g.dma_start(out=GRB[:, :],
                        in_=self.dram["l0a_gamma"].ap().partition_broadcast(128),
                        allow_slow_non_contiguous=True
                        ).then_inc(self.semh["LG0"], 16)

        @self.block("scalar", 3, waits=[("LG0", lg0)], tag="l0a_gamma")
        def gs(s, st):
            st(s.activation(GRB[:, :], GRB[:, :], AF.Sigmoid))
            st(s.activation(LNG[:, :], GRB[:, :], AF.Ln))
            st.last(s.activation(GCB[:, :], LNG[:, :], AF.Exp, scale=float(C)))
        gb = gs

        @self.block("gpsimd", 4, tag="l0a_iotas")
        def gi(g, st):
            st(g.iota(IJT[:, :], pattern=[[1, 128]], base=0, channel_multiplier=-1,
                      allow_small_or_imprecise_dtypes=True))
            st(g.affine_select(IJT[:, :], IJT[:, :], pattern=[[1, 128]], base=0,
                               channel_multiplier=-1, compare_op=ALU.is_ge,
                               fill=1e30))
            st(g.iota(IP1[:, :], pattern=[[1, 128]], base=1, channel_multiplier=0,
                      allow_small_or_imprecise_dtypes=True))
            st.last(g.iota(JT64[:, :], pattern=[[0, 64]], base=C - 1,
                           channel_multiplier=-1,
                           allow_small_or_imprecise_dtypes=True))

        @self.block("scalar", 1, waits=[("G", gi.total)], tag="l0a_decays")
        def ds(s, st):
            for h in range(H):
                bI = (h % 2) * 8 + (h // 2)
                s.activation(DECAY[:, bI * 128:(bI + 1) * 128], IJT[:, :], AF.Exp,
                             scale=LNG[:, h:h + 1])
            for p in range(NP):
                s.activation(GAMT[0:64, p * 128:(p + 1) * 128], IP1[0:64, :],
                             AF.Exp, scale=LNG[0:64, 2 * p:2 * p + 1])
                s.activation(GAMT[64:128, p * 128:(p + 1) * 128], IP1[64:128, :],
                             AF.Exp, scale=LNG[64:128, 2 * p + 1:2 * p + 2])
            for h in range(H):
                kc = 64 * (h // 2) + 512 * (h % 2)
                ins = s.activation(GKT[:, kc:kc + 64], JT64[:, 0:64],
                                   AF.Exp, scale=LNG[:, h:h + 1])
            st.last(ins)

        @self.block("vector", 2, waits=[("LG0", lg0), ("G", 5), ("S", ds.total)], tag="l0a_misc")
        def vm(v, st):
            st(v.tensor_scalar(KB8[:, :], QB[:, 8:16], 0.125, None, ALU.mult))
            for p in range(NP):
                v.tensor_scalar(GIDT[0:64, p * 64:(p + 1) * 64],
                                IDENT[0:64, 0:64],
                                GCB[0:64, 2 * p:2 * p + 1], None, ALU.mult)
                ins = v.tensor_scalar(GIDT[64:128, p * 64:(p + 1) * 64],
                                      IDENT[0:64, 0:64],
                                      GCB[64:128, 2 * p + 1:2 * p + 2], None,
                                      ALU.mult)
            st.last(ins)

        @self.block("vector", 1, tag="l0a_rinit")
        def rz(v, st):
            st.last(v.memset(RBUF[:, :], 0.0))

        if CUT >= 2:
            self.prep_weight_ocmajor("l0a_wqkv", 3 * D, WQKVT, "LWA")
            self.prep_weight_ocmajor("l0a_wo", D, WOT, "LWB")

        if CUT < 3:
            return
        stats_done = self.bulk_ln_stats("XS", "LX", 16, "l0a_st")

        def ISTD(c):
            return ST[:, 3 * NT + c:3 * NT + c + 1]

        def NM(c):
            return ST[:, 4 * NT + c:4 * NT + c + 1]

        if CUT < 4:
            return
        for c in range(NT):
            @self.block("vector", 1,
                        waits=[("V", stats_done), ("T", self.val("T"))],
                        tag=f"l0aA{c}")
            def sA(v, st, c=c):
                st.last(v.tensor_scalar(XN[:, :], XS[:, c * D:(c + 1) * D],
                                        ISTD(c), NM(c), ALU.mult, ALU.add))

            @self.block("tensor", 1, waits=[("V", sA.total)], tag=f"l0aB{c}")
            def sB(t, st):
                for j in range(NF):
                    ins = t.transpose(PD[:, j * 128:(j + 1) * 128],
                                      XN[:, j * 128:(j + 1) * 128], IDENT[:, :])
                st.last(ins)

            @self.block("vector", 1, waits=[("T", sB.total)], tag=f"l0aBv{c}")
            def sBv(v, st):
                st.last(v.tensor_copy(XNT[:, :], PD[:, 0:1024]))

            if CUT < 5:
                continue
            @self.block("tensor", 1, waits=[("V", sBv.total)], tag=f"l0aC{c}")
            def sC(t, st):
                for oc in range(16):
                    for fc in range(NF):
                        ins = t.matmul(
                            PA[:, oc * 128:(oc + 1) * 128],
                            WQKVT[:, (oc * 8 + fc) * 128:(oc * 8 + fc + 1) * 128],
                            XNT[:, fc * 128:(fc + 1) * 128],
                            start=(oc % 4 == 0 and fc == 0),
                            stop=(oc % 4 == 3 and fc == NF - 1))
                st.last(ins)

            @self.block("vector", 1, waits=[("T", sC.total)], tag=f"l0aCv{c}")
            def sCv(v, st):
                for p in range(NP):
                    v.tensor_scalar(QBUF[:, p * 128:(p + 1) * 128],
                                    PA[:, p * 128:(p + 1) * 128],
                                    QB[:, p:p + 1], None, ALU.add)
                for p in range(NP):
                    ins = v.tensor_scalar(KBUF[:, p * 128:(p + 1) * 128],
                                          PA[:, (8 + p) * 128:(9 + p) * 128],
                                          0.125, KB8[:, p:p + 1], ALU.mult,
                                          ALU.add)
                st.last(ins)

            if CUT < 6:
                continue
            @self.block("tensor", 1, waits=[("V", sCv.total)], tag=f"l0aD{c}")
            def sD(t, st):
                for voc in range(8):
                    for fc in range(NF):
                        t.matmul(PB[:, voc * 128:(voc + 1) * 128],
                                 XNT[:, fc * 128:(fc + 1) * 128],
                                 WQKVT[:, ((16 + voc) * 8 + fc) * 128:
                                       ((16 + voc) * 8 + fc + 1) * 128],
                                 start=(voc % 4 == 0 and fc == 0), stop=False)
                for voc in range(8):
                    ins = t.matmul(PB[:, voc * 128:(voc + 1) * 128],
                                   ONESROW[:, :],
                                   VBROW[:, voc * 128:(voc + 1) * 128],
                                   start=False, stop=(voc % 4 == 3))
                st.last(ins)

            @self.block("vector", 1, waits=[("T", sD.total)], tag=f"l0aDv{c}")
            def sDv(v, st):
                st.last(v.tensor_copy(VBUF[:, :], PB[:, :]))

            if CUT < 7:
                continue
            @self.block("tensor", 1, waits=[("V", sCv.total)], tag=f"l0aE{c}")
            def sE(t, st):
                for p in range(NP):
                    t.matmul(PA[:, p * 128:(p + 1) * 128],
                             KBUF[0:64, p * 128:(p + 1) * 128],
                             QBUF[0:64, p * 128:(p + 1) * 128],
                             start=(p % 4 == 0), stop=(p % 4 == 3),
                             tile_position=(0, 0))
                    ins = t.matmul(PA[:, 1024 + p * 128:1024 + (p + 1) * 128],
                                   KBUF[64:128, p * 128:(p + 1) * 128],
                                   QBUF[64:128, p * 128:(p + 1) * 128],
                                   start=(p % 4 == 0), stop=(p % 4 == 3),
                                   tile_position=(64, 0))
                st.last(ins)

            @self.block("vector", 2, waits=[("T", sE.total)], tag=f"l0aEv{c}")
            def sEv(v, st):
                st(v.tensor_tensor(SMT[:, :], PA[:, :], DECAY[:, :], ALU.mult))
                st.last(v.tensor_tensor(QT[:, :], QBUF[:, :], GAMT[:, :],
                                        ALU.mult))

            @self.block("tensor", 1, waits=[("V", sBv.total)], tag=f"l0aF{c}")
            def sF(t, st):
                for h in range(H):
                    rh = (h % 2) * 64
                    p = h // 2
                    kc = 64 * p + 1024 * (h % 2)
                    ins = t.transpose(PD[:, kc:kc + 64],
                                      KBUF[rh:rh + 64, p * 128:(p + 1) * 128],
                                      IDENT[rh:rh + 64, rh:rh + 64])
                st.last(ins)

            @self.block("vector", 1, waits=[("T", sF.total)], tag=f"l0aFv{c}")
            def sFv(v, st):
                pdv = PD[:, :].rearrange("p (b q) -> p b q", q=1024)[:, :, 0:512]
                ktv = KTB[:, :].rearrange("p (b q) -> p b q", q=512)
                gkv = GKT[:, :].rearrange("p (b q) -> p b q", q=512)
                st.last(v.tensor_tensor(ktv, pdv, gkv, ALU.mult))

            if CUT < 8:
                continue
            @self.block("tensor", 2,
                        waits=[("V", max(sEv.total, sFv.total, sDv.total))],
                        tag=f"l0aG{c}")
            def sG(t, st):
                for p in range(NP):
                    t.matmul(PA[0:64, p * 128:(p + 1) * 128],
                             VBUF[:, (2 * p) * 64:(2 * p + 1) * 64],
                             SMT[:, p * 128:(p + 1) * 128],
                             start=(p % 4 == 0), stop=False,
                             tile_position=(0, 0), skip_group_check=True)
                    t.matmul(PA[64:128, 1024 + p * 128:1024 + (p + 1) * 128],
                             VBUF[:, (2 * p + 1) * 64:(2 * p + 2) * 64],
                             SMT[:, 1024 + p * 128:1024 + (p + 1) * 128],
                             start=(p % 4 == 0), stop=False,
                             tile_position=(0, 64), skip_group_check=True)
                    t.matmul(PA[0:64, p * 128:(p + 1) * 128],
                             RBUF[0:64, p * 64:(p + 1) * 64],
                             QT[0:64, p * 128:(p + 1) * 128],
                             start=False, stop=(p % 4 == 3),
                             tile_position=(0, 0), skip_group_check=True)
                    ins = t.matmul(PA[64:128, 1024 + p * 128:1024 + (p + 1) * 128],
                                   RBUF[64:128, p * 64:(p + 1) * 64],
                                   QT[64:128, p * 128:(p + 1) * 128],
                                   start=False, stop=(p % 4 == 3),
                                   tile_position=(64, 64), skip_group_check=True)
                st(ins)
                for p in range(NP):
                    t.matmul(PB[0:64, p * 64:(p + 1) * 64],
                             GIDT[0:64, p * 64:(p + 1) * 64],
                             RBUF[0:64, p * 64:(p + 1) * 64],
                             start=(p == 0), stop=False, tile_position=(0, 0),
                             skip_group_check=True)
                    t.matmul(PB[64:128, 512 + p * 64:512 + (p + 1) * 64],
                             GIDT[64:128, p * 64:(p + 1) * 64],
                             RBUF[64:128, p * 64:(p + 1) * 64],
                             start=(p == 0), stop=False, tile_position=(64, 64),
                             skip_group_check=True)
                    t.matmul(PB[0:64, p * 64:(p + 1) * 64],
                             KTB[:, p * 64:(p + 1) * 64],
                             VBUF[:, (2 * p) * 64:(2 * p + 1) * 64],
                             start=False, stop=(p == 7), tile_position=(0, 0),
                             skip_group_check=True)
                    ins = t.matmul(PB[64:128, 512 + p * 64:512 + (p + 1) * 64],
                                   KTB[:, 512 + p * 64:512 + (p + 1) * 64],
                                   VBUF[:, (2 * p + 1) * 64:(2 * p + 2) * 64],
                                   start=False, stop=(p == 7),
                                   tile_position=(0, 64), skip_group_check=True)
                st.last(ins)

            @self.block("vector", 4, waits=[("T", sG.total)], tag=f"l0aGv{c}")
            def sGv(v, st):
                st(v.tensor_copy(YBUF[0:64, :], PA[0:64, 0:1024]))
                st(v.tensor_copy(YBUF[64:128, :], PA[64:128, 1024:2048]))
                st(v.tensor_copy(RBUF[0:64, :], PB[0:64, 0:512]))
                st.last(v.tensor_copy(RBUF[64:128, :], PB[64:128, 512:1024]))

            if CUT < 9:
                continue
            @self.block("tensor", 1, waits=[("V", sGv.total)], tag=f"l0aH{c}")
            def sH(t, st):
                for ooc in range(8):
                    for fc in range(NF):
                        t.matmul(PA[:, ooc * 128:(ooc + 1) * 128],
                                 YBUF[:, fc * 128:(fc + 1) * 128],
                                 WOT[:, (ooc * 8 + fc) * 128:
                                     (ooc * 8 + fc + 1) * 128],
                                 start=(ooc % 4 == 0 and fc == 0), stop=False)
                for ooc in range(8):
                    ins = t.matmul(PA[:, ooc * 128:(ooc + 1) * 128],
                                   ONESROW[:, :],
                                   OBROW[:, ooc * 128:(ooc + 1) * 128],
                                   start=False, stop=(ooc % 4 == 3))
                st.last(ins)

            self.postln_residual(c, PA, D, sH.total, f"l0aI{c}")

        self.barrier("l0a")
        self.pop_scope()

    def prep_ternary(self, dram_name, nrows, ncols, dst, pref, mwb, mwi, tag):
        noc = nrows // 128
        nfc = ncols // 128
        dbuf = 2 if ncols <= 1024 else 1
        stg = self.sbuf(f"stgA_{dram_name}", [128, dbuf * ncols], F32)
        stgq = self.sbuf(f"stgQ_{dram_name}", [128, ncols], BF16)
        stgi = self.sbuf(f"stgI_{dram_name}", [128, ncols], mybir.dt.int8)
        SQJ = self.sb["SQJ"]
        CO = self.sb["COLS"]
        pab, acc = CO[:, 20:21], CO[:, 21:22]
        w = self.dram[dram_name].ap()
        PD, PB = self.ps["PD"], self.ps["PB"]

        @self.block("vector", 1, tag=tag + "z")
        def vz(v, st):
            st.last(v.memset(acc, 0.0))

        lastA = None
        for oc in range(noc):
            buf = oc % dbuf
            semn = f"{pref}{buf}"
            self.sem(semn)
            if not hasattr(self, "dma_load_sems"):
                self.dma_load_sems = set()
            prev_waits = ([(s, self.semc[s]) for s in self.dma_load_sems
                           if s != semn and self.semc[s] > 0]
                          if oc < dbuf else [])
            self.dma_load_sems.add(semn)
            self.semc[semn] += 16
            lw = self.semc[semn]

            @self.block("gpsimd", 0,
                        waits=[("S", self.val("S")),
                               ("V", self.val("V") if oc < dbuf else 0),
                               ("T", self.val("T") if oc < dbuf else 0)] + prev_waits,
                        tag=f"{tag}A{oc}g")
            def _(g, st, oc=oc, buf=buf, semn=semn):
                g.dma_start(out=stg[:, buf * ncols:(buf + 1) * ncols],
                            in_=w[oc * 128:(oc + 1) * 128, :]
                            ).then_inc(self.semh[semn], 16)

            @self.block("scalar", 1,
                        waits=[(semn, lw), ("S", self.val("S")),
                               ("V", self.val("V"))],
                        tag=f"{tag}A{oc}s")
            def sA(s, st, oc=oc, buf=buf):
                st.last(s.activation(SQJ[:, 0:ncols],
                                     stg[:, buf * ncols:(buf + 1) * ncols],
                                     AF.Abs, accum_out=pab))

            @self.block("vector", 1,
                        waits=[("S", sA.total), ("V", self.val("V"))],
                        tag=f"{tag}A{oc}v")
            def vA(v, st):
                st.last(v.tensor_add(acc, acc, pab))
            lastA = vA

        @self.block("vector", 1, waits=[("V", lastA.total)], tag=tag + "c")
        def vc(v, st):
            st.last(v.tensor_copy(CO[:, 22:23], acc))

        @self.block("tensor", 1, waits=[("V", vc.total)], tag=tag + "mm")
        def tmm(t, st):
            st.last(t.matmul(PB[0:1, 0:1], CO[:, 22:23], self.sb["ONESFC"][:, :],
                             start=True, stop=True))

        @self.block("vector", 1, waits=[("T", tmm.total)], tag=tag + "mv")
        def vmv(v, st):
            st.last(v.tensor_scalar(CO[0:1, 23:24], PB[0:1, 0:1],
                                    1.0 / (nrows * ncols), 1e-5, ALU.mult,
                                    ALU.max))

        @self.block("tensor", 1, waits=[("V", vmv.total)], tag=tag + "mb")
        def tmb(t, st):
            st.last(t.matmul(PB[:, 4:5], self.sb["ONESFR"][:, :],
                             CO[0:1, 23:24], start=True, stop=True))

        @self.block("vector", 2, waits=[("T", tmb.total)], tag=tag + "mw")
        def vmw(v, st):
            st(v.tensor_copy(mwb[:, :], PB[:, 4:5]))
            st.last(v.reciprocal(mwi[:, :], mwb[:, :]))

        nrounds = (nfc + 15) // 16
        for oc in range(noc):
            buf = oc % dbuf
            semn = f"{pref}{buf}"
            self.sem(semn)
            self.semc[semn] += 16
            lw = self.semc[semn]

            @self.block("gpsimd", 0,
                        waits=[("V", max(vmw.total, self.val("V") if oc >= dbuf else 0)),
                               ("S", self.val("S"))],
                        tag=f"{tag}B{oc}g")
            def _(g, st, oc=oc, buf=buf, semn=semn):
                g.dma_start(out=stg[:, buf * ncols:(buf + 1) * ncols],
                            in_=w[oc * 128:(oc + 1) * 128, :]
                            ).then_inc(self.semh[semn], 16)

            @self.block("scalar", 1,
                        waits=[(semn, lw), ("S", self.val("S")),
                               ("V", self.val("V"))],
                        tag=f"{tag}B{oc}sg")
            def sB(s, st, oc=oc, buf=buf):
                st.last(s.activation(SQJ[:, 0:ncols],
                                     stg[:, buf * ncols:(buf + 1) * ncols],
                                     AF.Sign))

            @self.block("vector", 5,
                        waits=[(semn, lw), ("V", max(vmw.total, self.val("V"))),
                               ("T", self.val("T")), ("S", sB.total)],
                        tag=f"{tag}B{oc}v")
            def vB(v, st, oc=oc, buf=buf):
                sl = stg[:, buf * ncols:(buf + 1) * ncols]
                sgn = SQJ[:, 0:ncols]
                st(v.tensor_scalar(sl, sl, mwi[:, :], 1.0, ALU.mult, ALU.min))
                st(v.tensor_scalar(sl, sl, -1.0, None, ALU.max))
                st(v.tensor_scalar(sgn, sgn, self.sb["CFH"][:, :], None,
                                   ALU.mult))
                st(v.scalar_tensor_tensor(stgi[:, :], sl, 0.0, sgn, ALU.add,
                                          ALU.add))
                st.last(v.tensor_copy(stgq[:, :], stgi[:, :]))

            for r in range(nrounds):
                f0 = 16 * r
                f1 = min(nfc, f0 + 16)

                @self.block("tensor", 1,
                            waits=[("V", vB.total if r == 0 else self.val("V"))],
                            tag=f"{tag}B{oc}t{r}")
                def tB(t, st, oc=oc, f0=f0, f1=f1):
                    for fc in range(f0, f1):
                        ins = t.transpose(PD[:, (fc - f0) * 128:(fc - f0 + 1) * 128],
                                          stgq[:, fc * 128:(fc + 1) * 128],
                                          self.sb["IDENT"][:, :])
                    st.last(ins)

                @self.block("vector", 1, waits=[("T", tB.total)],
                            tag=f"{tag}B{oc}e{r}")
                def vE(v, st, oc=oc, f0=f0, f1=f1):
                    st.last(v.tensor_copy(
                        dst[:, (oc * nfc + f0) * 128:(oc * nfc + f1) * 128],
                        PD[:, 0:(f1 - f0) * 128]))

    def phase_l0f(self):
        NC2 = 4 * D
        XS, CO, ST = self.sb["XS"], self.sb["COLS"], self.sb["STAT"]
        IDENT, ONESROW = self.sb["IDENT"], self.sb["ONESROW"]
        PA, PB, PD = (self.ps[k] for k in ("PA", "PB", "PD"))
        TMP = self.sb["TMPF"]
        qi2d = self.nc.dram_tensor("qi2_scratch", [S, NC2], mybir.dt.int8)
        import os
        DBG = int(os.environ.get("MK_DBG", "0"))
        if DBG:
            self.dout("dbg_h", [128, NC2])
            self.dout("dbg_t1", [128, 128])
            self.dout("dbg_mw", [1, 4])
            self.dout("dbg_qi", [128, D])
            self.dout("dbg_q2", [128, NC2])
        SCL2 = self.sbuf("SCL2", [128, NT], F32)
        MW2B = self.sbuf("MW2B", [128, 1], F32)
        MW2I = self.sbuf("MW2I", [128, 1], F32)
        _oldsqj = self.sb["SQJ"]
        self.sbuf("SQJA", [128, 4 * D], BF16)
        self.sb["SQJ"] = self.sb["SQJA"]
        SQJP = self.sb["SQJA"]

        self.push_scope()
        TERN1T = self.sbuf("TERN1T", [128, 32 * 8 * 128], BF16)
        MW1B = self.sbuf("MW1B", [128, 1], F32)
        MW1I = self.sbuf("MW1I", [128, 1], F32)
        B1ROW = self.sbuf("B1ROW", [1, NC2], BF16)
        QI = self.sbuf("QI", [128, D], BF16)
        QIT = self.sbuf("QIT", [128, D], BF16)
        SGN = self.sbuf("SGN", [128, NC2], BF16)
        HBUF = self.sbuf("HBUF", [128, NC2], BF16)
        T4K = self.sbuf("T4K", [128, NC2], F32)
        QI2C = self.sbuf("QI2C", [128, NC2], mybir.dt.int8)
        RSCB = self.sbuf("RSCB", [128, 1], BF16)
        RSROW = self.sbuf("RSROW", [1, 128], BF16)

        self.sem("LG1")
        self.semc["LG1"] += 16
        lg1 = self.semc["LG1"]

        @self.block("gpsimd", 0, tag="l0f_b1load")
        def _(g, st):
            g.dma_start(out=B1ROW[:, :], in_=self.dram["l0f_b1"].ap()
                        ).then_inc(self.semh["LG1"], 16)

        self.prep_ternary("l0f_w1", 4 * D, D, TERN1T, "LW1", MW1B, MW1I, "w1p")
        QI32 = self.sb["stgI_l0f_w1"]


        c_idx = {"su": 0, "sq": 1, "mean": 2, "sqd": 3, "m2": 4, "var": 5,
                 "veps": 6, "rcp1": 7, "istd": 13, "rv": 14, "rcp2": 15,
                 "r": 16, "d1": 17, "d2": 18, "mm": 19, "mx": 24, "mn": 25,
                 "t3": 26, "scale": 27, "alpha": 28, "beta": 29, "scol": 30,
                 "rscol": 31}

        def CC(n):
            i = c_idx[n]
            return CO[:, i:i + 1]

        for c in range(NT):
            @self.block("vector", 3, waits=[("V", self.val("V")),
                                            ("S", self.val("S"))],
                        tag=f"l0f1s{c}")
            def fv1(v, st, c=c):
                xt = XS[:, c * D:(c + 1) * D]
                st(v.reduce_sum(CC("su"), xt, axis=AX.X))
                st(v.reduce_max(CC("mx"), xt, axis=AX.X))
                st.last(v.tensor_reduce(CC("mn"), xt, axis=AX.X, op=ALU.min))

            @self.block("scalar", 1, waits=[("V", fv1.total)], tag=f"l0f1q{c}")
            def fs1(s, st, c=c):
                st.last(s.activation(SQJP[:, 0:D],
                                     XS[:, c * D:(c + 1) * D], AF.Square,
                                     accum_out=CC("sq")))

            @self.block("vector", 8, waits=[("S", fs1.total)], tag=f"l0f1c{c}")
            def fv2(v, st):
                st(v.tensor_scalar(CC("mean"), CC("su"), 1.0 / D, None, ALU.mult))
                st(v.tensor_scalar(CC("sqd"), CC("sq"), 1.0 / D, None, ALU.mult))
                st(v.tensor_tensor(CC("m2"), CC("mean"), CC("mean"), ALU.mult))
                st(v.tensor_tensor(CC("var"), CC("sqd"), CC("m2"), ALU.subtract))
                st(v.tensor_scalar(CC("veps"), CC("var"), EPS, None, ALU.add))
                st(v.reciprocal(CC("rcp1"), CC("veps")))
                st(v.tensor_tensor(CC("rv"), CC("var"), CC("rcp1"), ALU.mult))
                st.last(v.tensor_scalar(CC("rv"), CC("rv"), EPS, None, ALU.add))

            @self.block("scalar", 2, waits=[("V", fv2.total)], tag=f"l0f1r{c}")
            def fs2(s, st):
                st(s.activation(CC("istd"), CC("rcp1"), AF.Sqrt))
                st.last(s.activation(CC("rcp2"), CC("rv"), AF.Sqrt))

            @self.block("vector", 10, waits=[("S", fs2.total)], tag=f"l0f1d{c}")
            def fv3(v, st):
                st(v.reciprocal(CC("r"), CC("rcp2")))
                st(v.tensor_tensor(CC("d1"), CC("mx"), CC("mean"), ALU.subtract))
                st(v.tensor_tensor(CC("d2"), CC("mean"), CC("mn"), ALU.subtract))
                st(v.tensor_tensor(CC("mm"), CC("d1"), CC("d2"), ALU.max))
                st(v.tensor_tensor(CC("t3"), CC("mm"), CC("istd"), ALU.mult))
                st(v.tensor_tensor(CC("t3"), CC("t3"), CC("r"), ALU.mult))
                st(v.tensor_scalar(CC("t3"), CC("t3"), 1e-5, None, ALU.max))
                st(v.reciprocal(CC("scale"), CC("t3")))
                st(v.tensor_scalar(CC("scale"), CC("scale"), 127.0, None,
                                   ALU.mult))
                st.last(v.tensor_tensor(CC("alpha"), CC("istd"), CC("r"),
                                        ALU.mult))

            @self.block("vector", 1, waits=[("V", fv3.total)], tag=f"l0f1nm{c}")
            def fvnm(v, st):
                st.last(v.tensor_scalar(CC("d1"), CC("mean"), -1.0, None,
                                        ALU.mult))

            @self.block("scalar", 1, waits=[("V", fvnm.total)], tag=f"l0f1sg{c}")
            def fssg(s, st, c=c):
                st.last(s.activation(SGN[:, 0:D], XS[:, c * D:(c + 1) * D],
                                     AF.Sign, bias=CC("d1")))

            @self.block("vector", 7,
                        waits=[("S", fssg.total),
                               ("SDB", 64 if DBG and c == 1 else 0)],
                        tag=f"l0f1e{c}")
            def fv4(v, st, c=c):
                st(v.tensor_tensor(CC("alpha"), CC("alpha"), CC("scale"),
                                   ALU.mult))
                st(v.scalar_tensor_tensor(CC("beta"), CC("mean"), -1.0,
                                          CC("alpha"), ALU.mult, ALU.mult))
                st(v.tensor_scalar(TMP[:, :], XS[:, c * D:(c + 1) * D],
                                   CC("alpha"), CC("beta"), ALU.mult, ALU.add))
                st(v.tensor_scalar(SGN[:, 0:D], SGN[:, 0:D],
                                   self.sb["CFH"][:, :], None, ALU.mult))
                st(v.scalar_tensor_tensor(QI32[:, :], TMP[:, :], 0.0,
                                          SGN[:, 0:D], ALU.add, ALU.add))
                st(v.tensor_copy(QI[:, :], QI32[:, :]))
                st.last(v.scalar_tensor_tensor(CC("rscol"), CC("scale"), 1.0,
                                               MW1I[:, :], ALU.mult, ALU.mult))

            @self.block("tensor", 1, waits=[("V", fv4.total)], tag=f"l0f1t{c}")
            def ft1(t, st):
                for j in range(NF):
                    ins = t.transpose(PD[:, j * 128:(j + 1) * 128],
                                      QI[:, j * 128:(j + 1) * 128], IDENT[:, :])
                st.last(ins)

            @self.block("vector", 2, waits=[("T", ft1.total)], tag=f"l0f1u{c}")
            def fv5(v, st):
                st(v.tensor_copy(QIT[:, :], PD[:, 0:1024]))
                st.last(v.tensor_copy(RSCB[:, :], CC("rscol")))

            @self.block("tensor", 1, waits=[("V", fv5.total)], tag=f"l0f1w{c}")
            def ft2(t, st):
                st.last(t.matmul(PB[0:1, 8:136], RSCB[:, :], IDENT[:, :],
                                 start=True, stop=True))

            @self.block("vector", 1, waits=[("T", ft2.total)], tag=f"l0f1x{c}")
            def fv6(v, st):
                st.last(v.tensor_copy(RSROW[:, :], PB[0:1, 8:136]))

            @self.block("vector", 2, waits=[("V", fv6.total)], tag=f"l0f1y{c}")
            def fv7(v, st):
                st(v.reciprocal(CC("t3"), CC("scale")))
                st.last(v.tensor_tensor(CC("scol"), MW1B[:, :], CC("t3"), ALU.mult))

            for half in range(2):
                @self.block("tensor", 1,
                            waits=[("V", fv7.total), ("LG1", lg1),
                                   ("S", self.val("S") if half == 1 else 0)],
                            tag=f"l0f1m{c}h{half}")
                def ft3(t, st, half=half):
                    for o16 in range(16):
                        oc = half * 16 + o16
                        for fc in range(NF):
                            t.matmul(PA[:, o16 * 128:(o16 + 1) * 128],
                                     QIT[:, fc * 128:(fc + 1) * 128],
                                     TERN1T[:, (oc * 8 + fc) * 128:
                                            (oc * 8 + fc + 1) * 128],
                                     start=(o16 % 4 == 0 and fc == 0), stop=False)
                    for o16 in range(16):
                        oc = half * 16 + o16
                        ins = t.matmul(PA[:, o16 * 128:(o16 + 1) * 128],
                                       RSROW[:, :],
                                       B1ROW[:, oc * 128:(oc + 1) * 128],
                                       start=False, stop=(o16 % 4 == 3))
                    st.last(ins)

                @self.block("scalar", 4,
                            waits=[("T", ft3.total),
                                   ("SDB", 64 if DBG and c == 1 else 0)],
                            tag=f"l0f1g{c}h{half}")
                def fs3(s, st, half=half):
                    for q in range(4):
                        st(s.activation(
                            HBUF[:, half * 2048 + q * 512:half * 2048 + (q + 1) * 512],
                            PA[:, q * 512:(q + 1) * 512], AF.Gelu,
                            scale=CC("scol")))

            @self.block("scalar", 1, waits=[("S", self.val("S"))],
                        tag=f"l0f1h{c}")
            def fs4(s, st):
                st.last(s.activation(SQJP[:, 0:NC2], HBUF[:, :],
                                     AF.Square, accum_out=CC("sq")))

            @self.block("vector", 3, waits=[("S", fs4.total)], tag=f"l0f1i{c}")
            def fv8(v, st):
                st(v.reduce_max(CC("mx"), HBUF[:, :], axis=AX.X,
                                apply_absolute_value=True))
                st(v.tensor_scalar(CC("rv"), CC("sq"), 1.0 / NC2, EPS,
                                   ALU.mult, ALU.add))
                st.last(v.reciprocal(CC("rcp2"), CC("rv")))

            @self.block("scalar", 1, waits=[("V", fv8.total)], tag=f"l0f1j{c}")
            def fs5(s, st):
                st.last(s.activation(CC("rcp2"), CC("rcp2"), AF.Sqrt))

            @self.block("scalar", 1, waits=[("S", fs5.total)], tag=f"l0f1sg2{c}")
            def fssg2(s, st):
                st.last(s.activation(SGN[:, 0:NC2], HBUF[:, :], AF.Sign))

            @self.block("vector", 7, waits=[("S", fssg2.total)], tag=f"l0f1k{c}")
            def fv9(v, st, c=c):
                st(v.tensor_tensor(CC("t3"), CC("mx"), CC("rcp2"), ALU.mult))
                st(v.tensor_scalar(CC("t3"), CC("t3"), 1e-5, None, ALU.max))
                st(v.reciprocal(CC("scale"), CC("t3")))
                st(v.tensor_scalar(CC("scale"), CC("scale"), 127.0, None,
                                   ALU.mult))
                st(v.tensor_copy(SCL2[:, c:c + 1], CC("scale")))
                st(v.tensor_tensor(CC("alpha"), CC("rcp2"), CC("scale"),
                                   ALU.mult))
                st.last(v.tensor_scalar(T4K[:, :], HBUF[:, :], CC("alpha"),
                                        None, ALU.mult))

            @self.block("vector", 2,
                        waits=[("V", fv9.total), ("SQ2", self.val("SQ2")),
                               ("SDB", 64 if DBG and c == 1 else 0)],
                        tag=f"l0f1l{c}")
            def fv10(v, st):
                st(v.tensor_scalar(SGN[:, 0:NC2], SGN[:, 0:NC2],
                                   self.sb["CFH"][:, :], None, ALU.mult))
                st.last(v.scalar_tensor_tensor(QI2C[:, :], T4K[:, :], 0.0,
                                               SGN[:, 0:NC2], ALU.add, ALU.add))

            self.sem("SQ2")
            self.semc["SQ2"] += 16
            sq2t = self.semc["SQ2"]

            @self.block("sync", 0, waits=[("V", fv10.total)], tag=f"l0f1z{c}")
            def _(e, st, c=c):
                e.dma_start(out=qi2d.ap()[c * 128:(c + 1) * 128, :],
                            in_=QI2C[:, :]).then_inc(self.semh["SQ2"], 16)

            if DBG and c == 0:
                self.sem("SDB")
                self.sem("SDC")
                self.semc["SDB"] += 64
                self.semc["SDC"] += 16

                @self.block("gpsimd", 0, waits=[("V", fv10.total)], tag="dbg0")
                def _(g, st):
                    g.dma_start(out=self.dram["dbg_h"].ap(), in_=HBUF[:, :]
                                ).then_inc(self.semh["SDB"], 16)
                    g.dma_start(out=self.dram["dbg_t1"].ap(),
                                in_=TERN1T[:, 0:128]).then_inc(self.semh["SDB"], 16)
                    g.dma_start(out=self.dram["dbg_qi"].ap(), in_=QI[:, :]
                                ).then_inc(self.semh["SDB"], 16)
                    g.dma_start(out=self.dram["dbg_q2"].ap(),
                                in_=QI2C[:, :]).then_inc(self.semh["SDB"], 16)

                @self.block("vector", 1, waits=[], tag="dbgmw")
                def dmw(v, st):
                    st.last(v.tensor_copy(self.sb["COLS"][0:1, 26:27],
                                          MW1B[0:1, 0:1]))

                @self.block("sync", 0, waits=[("V", dmw.total)], tag="dbg1")
                def _(e, st):
                    e.dma_start(out=self.dram["dbg_mw"].ap()[:, 0:1],
                                in_=MW1B[0:1, 0:1]).then_inc(self.semh["SDC"], 16)

                @self.block("gpsimd", 0, waits=[("SDB", 64)], tag="dbg2")
                def _(g, st):
                    pass

        self.barrier("l0f1", extra=[("SQ2", self.val("SQ2"))] +
                     ([("SDB", self.val("SDB")), ("SDC", self.val("SDC"))]
                      if DBG else []))
        self.pop_scope()

        self.push_scope()
        TERN2T = self.sbuf("TERN2T", [128, 8 * 32 * 128], BF16)
        self.sbuf("SQJB", [128, 4 * D], BF16)
        self.sb["SQJ"] = self.sb["SQJB"]
        B2ROW = self.sbuf("B2ROW", [1, D], BF16)
        QB2 = self.sbuf("QB2", [128, NC2], BF16)
        QIT2 = self.sbuf("QIT2", [128, NC2], BF16)
        QI2L = self.sbuf("QI2L", [128, NC2], mybir.dt.int8)
        RSCB2 = self.sbuf("RSCB2", [128, 1], BF16)
        RSROW2 = self.sbuf("RSROW2", [1, 128], BF16)

        self.sem("LG2")
        self.semc["LG2"] += 16
        lg2 = self.semc["LG2"]

        @self.block("gpsimd", 0, tag="l0f_b2load")
        def _(g, st):
            g.dma_start(out=B2ROW[:, :], in_=self.dram["l0f_b2"].ap()
                        ).then_inc(self.semh["LG2"], 16)

        self.prep_ternary("l0f_w2", D, NC2, TERN2T, "LW2", MW2B, MW2I, "w2p")

        for c in range(NT):
            self.sem("LQ2")
            self.semc["LQ2"] += 16
            lq2 = self.semc["LQ2"]

            @self.block("gpsimd", 0, waits=[("V", self.val("V")),
                                            ("SQ2", 16 * (c + 1))],
                        tag=f"l0f2l{c}")
            def _(g, st, c=c):
                g.dma_start(out=QI2L[:, :],
                            in_=qi2d.ap()[c * 128:(c + 1) * 128, :]
                            ).then_inc(self.semh["LQ2"], 16)

            @self.block("vector", 4, waits=[("LQ2", lq2), ("SQ2", self.val("SQ2"))],
                        tag=f"l0f2c{c}")
            def gv1(v, st, c=c):
                st(v.tensor_copy(QB2[:, :], QI2L[:, :]))
                st(v.reciprocal(CC("t3"), SCL2[:, c:c + 1]))
                st(v.tensor_tensor(CC("scol"), MW2B[:, :], CC("t3"), ALU.mult))
                st.last(v.scalar_tensor_tensor(CC("rscol"), SCL2[:, c:c + 1],
                                               1.0, MW2I[:, :], ALU.mult, ALU.mult))

            for r in range(2):
                @self.block("tensor", 1,
                            waits=[("V", gv1.total if r == 0 else self.val("V"))],
                            tag=f"l0f2t{c}r{r}")
                def gt1(t, st, r=r):
                    for j in range(16):
                        ins = t.transpose(PD[:, j * 128:(j + 1) * 128],
                                          QB2[:, (16 * r + j) * 128:
                                              (16 * r + j + 1) * 128],
                                          IDENT[:, :])
                    st.last(ins)

                @self.block("vector", 1, waits=[("T", gt1.total)],
                            tag=f"l0f2u{c}r{r}")
                def gv2(v, st, r=r):
                    st.last(v.tensor_copy(QIT2[:, r * 2048:(r + 1) * 2048],
                                          PD[:, 0:2048]))

            @self.block("vector", 1, waits=[("V", gv2.total)], tag=f"l0f2v{c}")
            def gv3(v, st):
                st.last(v.tensor_copy(RSCB2[:, :], CC("rscol")))

            @self.block("tensor", 1, waits=[("V", gv3.total)], tag=f"l0f2w{c}")
            def gt2(t, st):
                st.last(t.matmul(PB[0:1, 8:136], RSCB2[:, :], IDENT[:, :],
                                 start=True, stop=True))

            @self.block("vector", 1, waits=[("T", gt2.total)], tag=f"l0f2x{c}")
            def gv4(v, st):
                st.last(v.tensor_copy(RSROW2[:, :], PB[0:1, 8:136]))

            @self.block("tensor", 1, waits=[("V", gv4.total), ("LG2", lg2)],
                        tag=f"l0f2m{c}")
            def gt3(t, st):
                for o2c in range(8):
                    for fc2 in range(32):
                        t.matmul(PA[:, o2c * 128:(o2c + 1) * 128],
                                 QIT2[:, fc2 * 128:(fc2 + 1) * 128],
                                 TERN2T[:, (o2c * 32 + fc2) * 128:
                                        (o2c * 32 + fc2 + 1) * 128],
                                 start=(o2c % 4 == 0 and fc2 == 0), stop=False)
                for o2c in range(8):
                    ins = t.matmul(PA[:, o2c * 128:(o2c + 1) * 128],
                                   RSROW2[:, :],
                                   B2ROW[:, o2c * 128:(o2c + 1) * 128],
                                   start=False, stop=(o2c % 4 == 3))
                st.last(ins)

            self.postln_residual(c, PA, D, gt3.total, f"l0f2I{c}",
                                 scale_col=CC("scol"))

        self.barrier("l0f2")
        self.pop_scope()

    def phase_l1a(self):
        self.sb["SQJ"] = self.sb.get("SQJ_global", self.sb["SQJ"])
        XS, CO, ST = self.sb["XS"], self.sb["COLS"], self.sb["STAT"]
        IDENT, ONESROW = self.sb["IDENT"], self.sb["ONESROW"]
        PA, PB, PD = (self.ps[k] for k in ("PA", "PB", "PD"))
        TMP = self.sb["TMPF"]

        self.push_scope()
        ABUF = self.sbuf("ABUF", [128, NF * 2048], BF16)
        XSC = self.sbuf("XSC", [128, NF * 2048], BF16)
        GB = self.sbuf("GB", [128, 24], F32)
        SGT = self.sbuf("SGT", [128, 2048], BF16)

        self.sem("LG3")
        self.semc["LG3"] += 48
        lg3 = self.semc["LG3"]

        @self.block("gpsimd", 0, tag="l1a_bload")
        def _(g, st):
            for i, nm in enumerate(("l1a_bg", "l1a_bv", "l1a_bd")):
                g.dma_start(out=GB[:, 8 * i:8 * (i + 1)],
                            in_=self.dram[nm].ap().rearrange("o (j p) -> p (o j)", p=128),
                            allow_slow_non_contiguous=True
                            ).then_inc(self.semh["LG3"], 16)

        sub1 = contextlib.ExitStack()
        old_es = self.cur_es
        self.cur_es = sub1
        XNT = self.sbuf("XNT1a", [128, NF * 2048], BF16)
        XN1 = self.sbuf("XN1a", [128, D], BF16)
        WOC = self.sbuf("WOC1a", [128, 8 * 1024], BF16)

        stats_done = self.bulk_ln_stats("XS", "V", self.val("V"), "l1a_st")

        def ISTD(c):
            return ST[:, 3 * NT + c:3 * NT + c + 1]

        def NM(c):
            return ST[:, 4 * NT + c:4 * NT + c + 1]

        for c in range(NT):
            @self.block("vector", 1,
                        waits=[("V", stats_done), ("T", self.val("T"))],
                        tag=f"l1aA{c}")
            def aA(v, st, c=c):
                st.last(v.tensor_scalar(XN1[:, :], XS[:, c * D:(c + 1) * D],
                                        ISTD(c), NM(c), ALU.mult, ALU.add))

            @self.block("tensor", 1, waits=[("V", aA.total)], tag=f"l1aB{c}")
            def aB(t, st, c=c):
                for j in range(NF):
                    ins = t.transpose(PD[:, j * 128:(j + 1) * 128],
                                      XN1[:, j * 128:(j + 1) * 128], IDENT[:, :])
                st.last(ins)

            @self.block("vector", 1, waits=[("T", aB.total)], tag=f"l1aBv{c}")
            def aBv(v, st, c=c):
                pdv = PD[:, 0:1024].rearrange("p (f q) -> p f q", q=128)
                xv = XNT[:, :].rearrange("p (f q) -> p f q", q=2048)[
                    :, :, c * 128:(c + 1) * 128]
                st.last(v.tensor_copy(xv, pdv))

        for mi, (wnm, bcol0, kind) in enumerate(
                (("l1a_wv", 8, "tanh"), ("l1a_wg", 0, "sigmul"),
                 ("l1a_wd", 16, "sigaff"))):
            self.prep_weight_ocmajor(wnm, D, WOC, f"LWg{mi}", stgname="stg_l1a")
            for oc in range(NF):
                @self.block("tensor", 1,
                            waits=[("V", self.val("V")), ("S", self.val("S")),
                                   ("LG3", lg3)],
                            tag=f"l1aP{mi}o{oc}")
                def aP(t, st, oc=oc):
                    for fc in range(NF):
                        for q in range(4):
                            ins = t.matmul(
                                PA[:, q * 512:(q + 1) * 512],
                                WOC[:, (oc * 8 + fc) * 128:(oc * 8 + fc + 1) * 128],
                                XNT[:, fc * 2048 + q * 512:fc * 2048 + (q + 1) * 512],
                                start=(fc == 0), stop=(fc == NF - 1))
                    st.last(ins)

                @self.block("scalar", 1, waits=[("T", aP.total)],
                            tag=f"l1aE{mi}o{oc}")
                def aE(s, st, oc=oc, mi=mi, bcol0=bcol0, kind=kind):
                    fn = AF.Tanh if kind == "tanh" else AF.Sigmoid
                    dstb = XSC if kind == "tanh" else (SGT if kind == "sigmul"
                                                       else ABUF)
                    dst = (dstb[:, oc * 2048:(oc + 1) * 2048] if kind != "sigmul"
                           else SGT[:, :])
                    st.last(s.activation(dst, PA[:, :], fn,
                                         bias=GB[:, bcol0 + oc:bcol0 + oc + 1]))

                if kind == "sigmul":
                    @self.block("vector", 1, waits=[("S", aE.total)],
                                tag=f"l1aM{mi}o{oc}")
                    def aM(v, st, oc=oc):
                        sl = XSC[:, oc * 2048:(oc + 1) * 2048]
                        st.last(v.tensor_tensor(sl, sl, SGT[:, :], ALU.mult))
                elif kind == "sigaff":
                    @self.block("vector", 1, waits=[("S", aE.total)],
                                tag=f"l1aM{mi}o{oc}")
                    def aM(v, st, oc=oc):
                        sl = ABUF[:, oc * 2048:(oc + 1) * 2048]
                        st.last(v.tensor_scalar(sl, sl, 0.998, 0.001, ALU.mult,
                                                ALU.add))

        self.barrier("l1a_sub1")
        sub1.close()
        self.cur_es = old_es

        sub2 = contextlib.ExitStack()
        self.cur_es = sub2
        HTM = self.sbuf("HTM", [128, NT * D], BF16)
        HFC = self.sbuf("HFC", [128, 2048], BF16)

        for fc in range(NF):
            @self.block("vector", 1, waits=[("T", self.val("T"))],
                        tag=f"l1aS{fc}")
            def aS(v, st, fc=fc):
                st.last(v.tensor_tensor_scan(HFC[:, :],
                                             ABUF[:, fc * 2048:(fc + 1) * 2048],
                                             XSC[:, fc * 2048:(fc + 1) * 2048],
                                             0.0, ALU.mult, ALU.add))

            @self.block("tensor", 1, waits=[("V", aS.total)], tag=f"l1aT{fc}")
            def aT(t, st, fc=fc):
                for tt_ in range(NT):
                    ins = t.transpose(PD[:, tt_ * 128:(tt_ + 1) * 128],
                                      HFC[:, tt_ * 128:(tt_ + 1) * 128],
                                      IDENT[:, :])
                st.last(ins)

            @self.block("vector", 1, waits=[("T", aT.total)], tag=f"l1aTv{fc}")
            def aTv(v, st, fc=fc):
                pdv = PD[:, 0:2048].rearrange("p (t q) -> p t q", q=128)
                hv = HTM[:, :].rearrange("p (t q) -> p t q", q=1024)[
                    :, :, fc * 128:(fc + 1) * 128]
                st.last(v.tensor_copy(hv, pdv))

        for c in range(NT):
            self.postln_residual(c, HTM[:, c * D:(c + 1) * D], D,
                                 self.val("T"), f"l1aI{c}", wV=self.val("V"))

        self.barrier("l1a")
        sub2.close()
        self.cur_es = old_es
        self.pop_scope()

    def phase_l1f(self):
        REG = 5
        XS, CO, ST = self.sb["XS"], self.sb["COLS"], self.sb["STAT"]
        IDENT, ONESROW = self.sb["IDENT"], self.sb["ONESROW"]
        ONESFR = self.sb["ONESFR"]
        PA, PB, PD = (self.ps[k] for k in ("PA", "PB", "PD"))
        TMP = self.sb["TMPF"]
        SQJ = self.sb["SQJ_global"]
        self.sb["SQJ"] = SQJ
        nc = self.nc
        qf_d = nc.dram_tensor("qf_scratch", [D, 2048], BF16)
        kf_d = nc.dram_tensor("kf_scratch", [D, 2048], BF16)
        oc_d = nc.dram_tensor("ocf_scratch", [D, 2048], BF16)
        wq_d = nc.dram_tensor("wq_scratch", [128, 16], BF16)
        cc_in = self.dram["cc_in"]
        cc_out = self.dram["cc_out"]
        wqkv = self.dram["l1f_wqkv"].ap()

        self.push_scope()
        VPM = self.sbuf("VPM", [128, 16 * 1040], BF16)
        VAM = self.sbuf("VAM", [128, 16 * 1040], BF16)
        SIMS = self.sbuf("SIMS", [128, NT], F32)
        PMK = self.sbuf("PMK", [128, NT], F32)
        WQCL = self.sbuf("WQCL", [128, NT], F32)
        RGW = self.sbuf("RGW", [128, NT], F32)
        MAXN2 = self.sbuf("MAXN2", [128, 16], F32)
        GHATB = self.sbuf("GHATB", [128, D], BF16)
        BCOLS = self.sbuf("BCOLS", [128, 16], F32)
        THRB = self.sbuf("THRB", [128, 1], F32)
        CCS = self.sbuf("CCS", [1, 4], F32)
        SCL = self.sbuf("SCL", [128, 16], F32)
        DTC = self.sbuf("DTC", [128, 16], F32)
        KMXC = self.sbuf("KMXC", [128, 8], F32)
        SCB = self.sbuf("SCB", [128, 16], BF16)
        CCB = self.sbuf("CCB", [2, 1], F32)
        SRW = self.sbuf("SRW", [16, 1], F32)
        SRB = self.sbuf("SRB", [16, 1], BF16)
        SRR = self.sbuf("SRR", [1, 16], F32)

        self.sem("LG4")
        self.semc["LG4"] += 16
        lg4 = self.semc["LG4"]

        @self.block("gpsimd", 0, waits=[("V", self.val("V"))], tag="l1f_gload")
        def _(g, st):
            g.dma_start(out=GHATB[:, :],
                        in_=self.dram["l1f_g"].ap().partition_broadcast(128),
                        allow_slow_non_contiguous=True
                        ).then_inc(self.semh["LG4"], 16)

        @self.block("vector", 3, waits=[("LG4", lg4), ("S", self.val("S")),
                                        ("T", self.val("T"))], tag="l1f_gn1")
        def gn1(v, st):
            st(v.tensor_tensor(TMP[:, :], GHATB[:, :], GHATB[:, :], ALU.mult))
            st(v.reduce_sum(SCL[:, :].rearrange("p (h o) -> p h o", o=1),
                            TMP[:, :].rearrange("p (h e) -> p h e", e=64),
                            axis=AX.X))
            st.last(v.reciprocal(SCL[:, :], SCL[:, :]))

        @self.block("scalar", 1, waits=[("V", gn1.total)], tag="l1f_gn2")
        def gn2(s, st):
            st.last(s.activation(SCL[:, :], SCL[:, :], AF.Sqrt))

        @self.block("vector", 2, waits=[("S", gn2.total)], tag="l1f_gn3")
        def gn3(v, st):
            gv = GHATB[:, :].rearrange("p (h e) -> p h e", e=64)
            sv = SCL[:, :].rearrange("p (h o) -> p h o", o=1)
            st(v.memset(MAXN2[:, :], 0.0))
            st.last(v.tensor_tensor(gv, gv, sv.broadcast_to([128, 16, 64]),
                                    ALU.mult))

        sub1 = contextlib.ExitStack()
        outer1 = self.cur_es
        self.cur_es = sub1
        XNT = self.sbuf("XNT1f", [128, NF * 2048], BF16)
        XN1 = self.sbuf("XN1f", [128, D], BF16)
        WPART = self.sbuf("WPART", [128, 8 * 1024], BF16)
        WQS = self.sbuf("WQS", [128, 2 * 1024], BF16)
        QKE = self.sbuf("QKE", [128, 2048], BF16)
        QTM = self.sbuf("QTM", [128, D], BF16)
        QSQ = TMP

        stats_done = self.bulk_ln_stats("XS", "V", self.val("V"), "l1f_st")

        def ISTD(c):
            return ST[:, 3 * NT + c:3 * NT + c + 1]

        def NM(c):
            return ST[:, 4 * NT + c:4 * NT + c + 1]

        for c in range(NT):
            @self.block("vector", 1,
                        waits=[("V", stats_done), ("T", self.val("T"))],
                        tag=f"l1fA{c}")
            def fA(v, st, c=c):
                st.last(v.tensor_scalar(XN1[:, :], XS[:, c * D:(c + 1) * D],
                                        ISTD(c), NM(c), ALU.mult, ALU.add))

            @self.block("tensor", 1, waits=[("V", fA.total)], tag=f"l1fB{c}")
            def fB(t, st, c=c):
                for j in range(NF):
                    ins = t.transpose(PD[:, j * 128:(j + 1) * 128],
                                      XN1[:, j * 128:(j + 1) * 128], IDENT[:, :])
                st.last(ins)

            @self.block("vector", 1, waits=[("T", fB.total)], tag=f"l1fBv{c}")
            def fBv(v, st, c=c):
                pdv = PD[:, 0:1024].rearrange("p (f q) -> p f q", q=128)
                xv = XNT[:, :].rearrange("p (f q) -> p f q", q=2048)[
                    :, :, c * 128:(c + 1) * 128]
                st.last(v.tensor_copy(xv, pdv))

        self.sem("LWF0")
        self.dma_load_sems.add("LWF0")
        self.semc["LWF0"] += 16
        lwf_q = self.semc["LWF0"]

        @self.block("gpsimd", 0,
                    waits=[("V", self.val("V")), ("S", self.val("S")),
                           ("T", self.val("T"))] +
                    [(s, self.semc[s]) for s in sorted(self.dma_load_sems)
                     if s != "LWF0" and self.semc[s] > 0],
                    tag="l1f_wqload")
        def _(g, st):
            g.dma_start(out=WPART[:, :].rearrange("p (f o) -> p f o", o=1024),
                        in_=wqkv[:, 0:1024].rearrange("(f p) o -> p f o", p=128)
                        ).then_inc(self.semh["LWF0"], 16)

        for c in range(NT):
            @self.block("tensor", 1,
                        waits=[("LWF0", lwf_q), ("V", self.val("V")),
                               ("S", self.val("S"))], tag=f"l1fq{c}")
            def sq1(t, st, c=c):
                for og in range(2):
                    for fc in range(NF):
                        ins = t.matmul(
                            PB[:, og * 512:(og + 1) * 512],
                            XNT[:, fc * 2048 + c * 128:fc * 2048 + (c + 1) * 128],
                            WPART[:, fc * 1024 + og * 512:fc * 1024 + (og + 1) * 512],
                            start=(fc == 0), stop=(fc == NF - 1))
                st.last(ins)

            @self.block("vector", 7, waits=[("T", sq1.total)], tag=f"l1fqv{c}")
            def sq2(v, st, c=c):
                st(v.tensor_copy(QTM[:, :], PB[:, 0:1024]))
                st(v.tensor_tensor(QSQ[:, :], QTM[:, :], QTM[:, :], ALU.mult))
                st(v.reduce_sum(SCL[:, :].rearrange("p (h o) -> p h o", o=1),
                                QSQ[:, :].rearrange("p (h e) -> p h e", e=64),
                                axis=AX.X))
                st(v.tensor_tensor(MAXN2[:, :], MAXN2[:, :], SCL[:, :], ALU.max))
                st(v.tensor_tensor(QSQ[:, :], QTM[:, :], GHATB[:, :], ALU.mult))
                st(v.reduce_sum(DTC[:, :].rearrange("p (h o) -> p h o", o=1),
                                QSQ[:, :].rearrange("p (h e) -> p h e", e=64),
                                axis=AX.X))
                st.last(v.reciprocal(SCL[:, :], SCL[:, :]))

            @self.block("scalar", 1, waits=[("V", sq2.total)], tag=f"l1fqs{c}")
            def sq3(s, st):
                st.last(s.activation(SCL[:, :], SCL[:, :], AF.Sqrt))

            @self.block("vector", 2, waits=[("S", sq3.total)], tag=f"l1fqx{c}")
            def sq4(v, st, c=c):
                st(v.tensor_tensor(DTC[:, :], DTC[:, :], SCL[:, :], ALU.mult))
                st.last(v.reduce_sum(SIMS[:, c:c + 1], DTC[:, :], axis=AX.X))

        @self.block("vector", 5, waits=[("V", self.val("V"))], tag="l1f_mm")
        def smm(v, st):
            st(v.tensor_copy(SIMS[0:REG, 0:1], SIMS[0:REG, 1:2]))
            st(v.reduce_max(SCL[:, 0:1], SIMS[:, :], axis=AX.X))
            st(v.tensor_reduce(SCL[:, 1:2], SIMS[:, :], axis=AX.X, op=ALU.min))
            st(v.tensor_scalar(SCL[:, 1:2], SCL[:, 1:2], -1.0, None, ALU.mult))
            st.last(v.tensor_copy(SCB[:, 0:2], SCL[:, 0:2]))

        @self.block("tensor", 1, waits=[("V", smm.total)], tag="l1f_mmt")
        def smt(t, st):
            st.last(t.transpose(PD[0:2, 0:128], SCB[:, 0:2], IDENT[:, :]))

        @self.block("vector", 1, waits=[("T", smt.total)], tag="l1f_mmv")
        def smv(v, st):
            st.last(v.reduce_max(CCB[:, :], PD[0:2, 0:128], axis=AX.X))

        self.sem("LCC")
        self.sem("GCC")
        self.semc["LCC"] += 32
        self.semc["GCC"] += 1
        lcc = self.semc["LCC"]
        gcc = self.semc["GCC"]

        @self.block("gpsimd", 0, waits=[("V", smv.total)], tag="l1f_cc")
        def _(g, st):
            g.dma_start(out=cc_in.ap(), in_=CCB[:, :]).then_inc(self.semh["LCC"], 16)
            g.wait_ge(self.semh["LCC"], lcc - 16)
            g.collective_compute(
                "AllReduce", ALU.max,
                replica_groups=[list(range(8))],
                ins=[cc_in.ap().opt()],
                outs=[cc_out.ap().opt()],
            ).then_inc(self.semh["GCC"], 1)
            g.wait_ge(self.semh["GCC"], gcc)
            g.dma_start(out=CCS[:, 0:2], in_=cc_out.ap()).then_inc(self.semh["LCC"], 16)

        @self.block("vector", 2, waits=[("LCC", lcc)], tag="l1f_thr")
        def thr(v, st):
            st(v.tensor_scalar(CCS[:, 2:3], CCS[:, 0:1], 0.9, None, ALU.mult))
            st.last(v.scalar_tensor_tensor(CCS[:, 3:4], CCS[:, 1:2], -0.1,
                                           CCS[:, 2:3], ALU.mult, ALU.add))

        @self.block("tensor", 1, waits=[("V", thr.total)], tag="l1f_thrb")
        def thb(t, st):
            st.last(t.matmul(PB[:, 0:1], ONESFR[:, :], CCS[0:1, 3:4],
                             start=True, stop=True))

        @self.block("vector", 7, waits=[("T", thb.total)], tag="l1f_msk")
        def msk(v, st):
            st(v.tensor_copy(THRB[:, :], PB[:, 0:1]))
            st(v.tensor_scalar(PMK[:, :], SIMS[:, :], THRB[:, :], None, ALU.is_gt))
            st(v.memset(PMK[0:REG, 0:1], 1.0))
            st(v.tensor_copy(WQCL[:, :], PMK[:, :]))
            st(v.memset(WQCL[0:REG, 0:1], 0.5))
            st(v.memset(RGW[:, :], 1.0))
            st.last(v.memset(RGW[0:REG, 0:1], 2.0))

        for oc in range(16):
            isq = oc < 8
            if not isq:
                semn = f"LWK{oc % 2}"
                self.sem(semn)
                self.dma_load_sems.add(semn)
                self.semc[semn] += 16
                lwk = self.semc[semn]

                @self.block("gpsimd", 0,
                            waits=[("V", self.val("V") if oc >= 10 else 0),
                                   ("S", self.val("S") if oc < 10 else 0)],
                            tag=f"l1fk{oc}g")
                def _(g, st, oc=oc, semn=semn):
                    ko = 1024 + (oc - 8) * 128
                    g.dma_start(out=WQS[:, (oc % 2) * 1024:(oc % 2) * 1024 + 1024]
                                .rearrange("p (f o) -> p f o", o=128),
                                in_=wqkv[:, ko:ko + 128]
                                .rearrange("(f p) o -> p f o", p=128)
                                ).then_inc(self.semh[semn], 16)

            @self.block("tensor", 1,
                        waits=([(f"LWK{oc % 2}", self.semc[f"LWK{oc % 2}"])]
                               if not isq else []) +
                        [("V", self.val("V")), ("S", self.val("S"))],
                        tag=f"l1fk{oc}t")
            def qk1(t, st, oc=oc, isq=isq):
                for fc in range(NF):
                    for q in range(4):
                        if isq:
                            lh = WPART[:, fc * 1024 + oc * 128:fc * 1024 + (oc + 1) * 128]
                        else:
                            lh = WQS[:, (oc % 2) * 1024 + fc * 128:
                                     (oc % 2) * 1024 + (fc + 1) * 128]
                        ins = t.matmul(
                            PA[:, q * 512:(q + 1) * 512], lh,
                            XNT[:, fc * 2048 + q * 512:fc * 2048 + (q + 1) * 512],
                            start=(fc == 0), stop=(fc == NF - 1))
                st.last(ins)

            @self.block("vector", 2 if not isq else 1,
                        waits=[("T", qk1.total), ("SQF", self.val("SQF"))],
                        tag=f"l1fk{oc}v")
            def qk2(v, st, oc=oc, isq=isq):
                if isq:
                    st.last(v.tensor_scalar(QKE[:, :], PA[:, :], 0.125, None,
                                            ALU.mult))
                else:
                    st(v.tensor_copy(QKE[:, :], PA[:, :]))
                    st.last(v.reduce_max(KMXC[:, oc - 8:oc - 7], QKE[:, :],
                                         axis=AX.X, apply_absolute_value=True))

            self.sem("SQF")
            self.semc["SQF"] += 16
            sqf = self.semc["SQF"]

            @self.block("sync", 0, waits=[("V", qk2.total)], tag=f"l1fk{oc}s")
            def _(e, st, oc=oc, isq=isq):
                dst = qf_d if isq else kf_d
                ro = (oc % 8) * 128
                e.dma_start(out=dst.ap()[ro:ro + 128, :], in_=QKE[:, :]
                            ).then_inc(self.semh["SQF"], 16)

        @self.block("vector", 2, waits=[("V", self.val("V"))], tag="l1f_b1")
        def sb1(v, st):
            st(v.reduce_max(KMXC[:, 0:1], KMXC[:, :], axis=AX.X))
            st.last(v.tensor_copy(SCB[:, 0:1], KMXC[:, 0:1]))

        @self.block("tensor", 1, waits=[("V", sb1.total)], tag="l1f_b2")
        def sb2(t, st):
            st.last(t.transpose(PD[0:1, 0:128], SCB[:, 0:1], IDENT[:, :]))

        @self.block("vector", 2, waits=[("T", sb2.total)], tag="l1f_b3")
        def sb3(v, st):
            st(v.reduce_max(CCS[0:1, 2:3], PD[0:1, 0:128], axis=AX.X))
            st.last(v.tensor_copy(SCB[:, 0:16], MAXN2[:, :]))

        @self.block("tensor", 1, waits=[("V", sb3.total)], tag="l1f_b4")
        def sb4(t, st):
            st.last(t.transpose(PD[0:16, 0:128], SCB[:, 0:16], IDENT[:, :]))

        @self.block("vector", 1, waits=[("T", sb4.total)], tag="l1f_b5")
        def sb5(v, st):
            st.last(v.reduce_max(SRW[:, :], PD[0:16, 0:128], axis=AX.X))

        @self.block("scalar", 1, waits=[("V", sb5.total)], tag="l1f_b6")
        def sb6(s, st):
            st.last(s.activation(SRW[:, :], SRW[:, :], AF.Sqrt))

        @self.block("tensor", 1, waits=[("S", sb6.total)], tag="l1f_b7")
        def sb7(t, st):
            st.last(t.matmul(PB[0:16, 4:5], ONESFR[0:1, 0:16], CCS[0:1, 2:3],
                             start=True, stop=True))

        @self.block("vector", 2, waits=[("T", sb7.total)], tag="l1f_b8")
        def sb8(v, st):
            st(v.scalar_tensor_tensor(SRW[:, :], SRW[:, :], -1.0, PB[0:16, 4:5],
                                      ALU.mult, ALU.mult))
            st.last(v.tensor_copy(SRB[:, :], SRW[:, :]))

        @self.block("tensor", 1, waits=[("V", sb8.total)], tag="l1f_b9")
        def sb9(t, st):
            st.last(t.transpose(PD[0:1, 128:144], SRB[:, :],
                                IDENT[0:16, 0:16]))

        @self.block("vector", 1, waits=[("T", sb9.total)], tag="l1f_b10")
        def sb10(v, st):
            st.last(v.tensor_copy(SRR[:, :], PD[0:1, 128:144]))

        @self.block("tensor", 1, waits=[("V", sb10.total)], tag="l1f_b11")
        def sb11(t, st):
            st.last(t.matmul(PB[:, 8:24], ONESFR[:, :], SRR[:, :],
                             start=True, stop=True))

        @self.block("vector", 1, waits=[("T", sb11.total)], tag="l1f_b12")
        def sb12(v, st):
            st.last(v.tensor_copy(BCOLS[:, :], PB[:, 8:24]))

        self.semc["LWF0"] += 16
        lwf_v = self.semc["LWF0"]

        @self.block("gpsimd", 0,
                    waits=[("T", self.val("T")), ("SQF", self.val("SQF"))],
                    tag="l1f_wvload")
        def _(g, st):
            g.dma_start(out=WPART[:, :].rearrange("p (f o) -> p f o", o=1024),
                        in_=wqkv[:, 2048:3072].rearrange("(f p) o -> p f o", p=128)
                        ).then_inc(self.semh["LWF0"], 16)

        for c in range(NT):
            for og in range(2):
                @self.block("tensor", 1,
                            waits=[("LWF0", lwf_v), ("V", self.val("V"))],
                            tag=f"l1fv{c}o{og}")
                def vv1(t, st, c=c, og=og):
                    for fc in range(NF):
                        ins = t.matmul(
                            PB[:, 0:512],
                            XNT[:, fc * 2048 + c * 128:fc * 2048 + (c + 1) * 128],
                            WPART[:, fc * 1024 + og * 512:fc * 1024 + (og + 1) * 512],
                            start=(fc == 0), stop=(fc == NF - 1))
                    st.last(ins)

                @self.block("vector", 2, waits=[("T", vv1.total),
                                                ("V", msk.total)],
                            tag=f"l1fv{c}o{og}v")
                def vv2(v, st, c=c, og=og):
                    pbv = PB[:, 0:512].rearrange("p (h e) -> p h e", e=64)
                    vamv = VAM[:, c * 1040 + og * 520:c * 1040 + (og + 1) * 520] \
                        .rearrange("p (h e) -> p h e", e=65)[:, :, 0:64]
                    vpmv = VPM[:, c * 1040 + og * 520:c * 1040 + (og + 1) * 520] \
                        .rearrange("p (h e) -> p h e", e=65)[:, :, 0:64]
                    st(v.tensor_scalar(vamv, pbv, RGW[:, c:c + 1], None, ALU.mult))
                    st.last(v.tensor_scalar(vpmv, pbv, PMK[:, c:c + 1], None,
                                            ALU.mult))

            @self.block("vector", 2, waits=[("V", self.val("V"))],
                        tag=f"l1fv{c}m")
            def vv3(v, st, c=c):
                vamc = VAM[:, c * 1040:(c + 1) * 1040].rearrange(
                    "p (h e) -> p h e", e=65)[:, :, 64:65]
                vpmc = VPM[:, c * 1040:(c + 1) * 1040].rearrange(
                    "p (h e) -> p h e", e=65)[:, :, 64:65]
                st(v.tensor_copy(vamc, RGW[:, c:c + 1].rearrange(
                    "p o -> p o ()" if False else "p (a o) -> p a o", o=1)
                    .broadcast_to([128, 16, 1])))
                st.last(v.tensor_copy(vpmc, PMK[:, c:c + 1].rearrange(
                    "p (a o) -> p a o", o=1).broadcast_to([128, 16, 1])))

        @self.block("vector", 1, waits=[("V", self.val("V"))], tag="l1f_wqb")
        def wqb(v, st):
            st.last(v.tensor_copy(SCB[:, 0:16], WQCL[:, :]))

        self.sem("LWQR")
        self.semc["LWQR"] += 16
        lwqr1 = self.semc["LWQR"]

        @self.block("sync", 0, waits=[("V", wqb.total)], tag="l1f_wqs")
        def _(e, st):
            e.dma_start(out=wq_d.ap(), in_=SCB[:, 0:16]
                        ).then_inc(self.semh["LWQR"], 16)

        self.barrier("l1f_s1", extra=[("SQF", self.val("SQF")),
                                      ("LWQR", self.val("LWQR"))])
        sub1.close()
        self.cur_es = outer1

        sub2 = contextlib.ExitStack()
        self.cur_es = sub2
        PTB = self.sbuf("PTB", [128, 16 * 1024], BF16)
        KFH = self.sbuf("KFH", [64, 2048], BF16)
        QFH = self.sbuf("QFH", [64, 2048], BF16)
        OPOSS = self.sbuf("OPOSS", [64, 1024], BF16)
        ONEGS = self.sbuf("ONEGS", [64, 1024], BF16)
        OCH = OPOSS
        R1 = self.sbuf("R1", [1, 1024], F32)
        R2 = self.sbuf("R2", [1, 1024], F32)
        WQROW = self.sbuf("WQROW", [1, 2048], BF16)
        WQ1ROW = self.sbuf("WQ1ROW", [1, 2048], BF16)

        self.semc["LWQR"] += 16
        lwqr2 = self.semc["LWQR"]

        @self.block("gpsimd", 0, waits=[("LWQR", lwqr1)], tag="l1f_wql")
        def _(g, st):
            g.dma_start(out=WQROW[:, :].rearrange("a (c p) -> a c p", p=128),
                        in_=wq_d.ap().rearrange("p c -> () c p"),
                        allow_slow_non_contiguous=True
                        ).then_inc(self.semh["LWQR"], 16)

        @self.block("vector", 1, waits=[("LWQR", lwqr2)], tag="l1f_wqr")
        def wqr(v, st):
            st.last(v.tensor_scalar(WQ1ROW[:, :], WQROW[:, :], -1.0, 1.0,
                                    ALU.mult, ALU.add))

        self.sem("SOC")
        for h in range(H):
            self.sem("LKH")
            self.semc["LKH"] += 32
            lkh = self.semc["LKH"]

            @self.block("gpsimd", 0,
                        waits=[("T", self.val("T")), ("S", self.val("S"))],
                        tag=f"l1fh{h}g")
            def _(g, st, h=h):
                g.dma_start(out=KFH[:, :], in_=kf_d.ap()[64 * h:64 * h + 64, :]
                            ).then_inc(self.semh["LKH"], 16)
                g.dma_start(out=QFH[:, :], in_=qf_d.ap()[64 * h:64 * h + 64, :]
                            ).then_inc(self.semh["LKH"], 16)

            for half in range(2):
                for kt in range(NT):
                    psPP = PB[:, 0:1024] if kt % 2 == 0 else PA[:, 0:1024]

                    @self.block("tensor", 1,
                                waits=[("LKH", lkh), ("S", self.val("S")),
                                       ("V", self.val("V") if kt == 0 else 0)],
                                tag=f"l1fh{h}f{half}s{kt}")
                    def hs1(t, st, h=h, half=half, kt=kt, psPP=psPP):
                        for q in range(2):
                            ins = t.matmul(
                                psPP[:, q * 512:(q + 1) * 512],
                                KFH[:, kt * 128:(kt + 1) * 128],
                                QFH[:, half * 1024 + q * 512:
                                    half * 1024 + (q + 1) * 512],
                                start=True, stop=True)
                        st.last(ins)

                    @self.block("scalar", 1, waits=[("T", hs1.total)],
                                tag=f"l1fh{h}f{half}e{kt}")
                    def hs2(s, st, h=h, kt=kt, psPP=psPP):
                        st.last(s.activation(PTB[:, kt * 1024:(kt + 1) * 1024],
                                             psPP, AF.Exp,
                                             bias=BCOLS[:, h:h + 1]))

                @self.block("tensor", 1, waits=[("S", self.val("S")),
                                                ("V", self.val("V"))],
                            tag=f"l1fh{h}f{half}pv")
                def hpv(t, st, h=h):
                    for kt in range(NT):
                        for q in range(2):
                            t.matmul(PA[0:65, q * 512:(q + 1) * 512],
                                     VPM[:, kt * 1040 + h * 65:
                                         kt * 1040 + (h + 1) * 65],
                                     PTB[:, kt * 1024 + q * 512:
                                         kt * 1024 + (q + 1) * 512],
                                     start=(kt == 0), stop=(kt == NT - 1),
                                     skip_group_check=True)
                            ins = t.matmul(PA[0:65, 1024 + q * 512:
                                              1024 + (q + 1) * 512],
                                           VAM[:, kt * 1040 + h * 65:
                                               kt * 1040 + (h + 1) * 65],
                                           PTB[:, kt * 1024 + q * 512:
                                               kt * 1024 + (q + 1) * 512],
                                           start=(kt == 0), stop=(kt == NT - 1),
                                           skip_group_check=True)
                    st.last(ins)

                @self.block("vector", 8, waits=[("T", hpv.total),
                                                ("SOC", self.val("SOC"))],
                            tag=f"l1fh{h}f{half}cb")
                def hcb(v, st, h=h, half=half):
                    st(v.tensor_copy(OPOSS[:, :], PA[0:64, 0:1024]))
                    st(v.tensor_tensor(ONEGS[:, :], PA[0:64, 1024:2048],
                                       OPOSS[:, :], ALU.subtract))
                    st(v.reciprocal(R1[:, :], PA[64:65, 0:1024]))
                    st(v.tensor_copy(R2[:, :], PA[64:65, 0:1024]))
                    st(v.tensor_tensor(R2[:, :], PA[64:65, 1024:2048], R2[:, :],
                                       ALU.subtract))
                    st(v.reciprocal(R2[:, :], R2[:, :]))
                    st(v.tensor_tensor(R1[:, :], R1[:, :],
                                       WQROW[:, half * 1024:(half + 1) * 1024],
                                       ALU.mult))
                    st.last(v.tensor_tensor(R2[:, :], R2[:, :],
                                            WQ1ROW[:, half * 1024:(half + 1) * 1024],
                                            ALU.mult))

                @self.block("tensor", 2, waits=[("V", hcb.total)],
                            tag=f"l1fh{h}f{half}bc")
                def hbc(t, st):
                    t.matmul(PB[0:64, 0:512], ONESFR[0:1, 0:64], R1[:, 0:512],
                             start=True, stop=True)
                    st(t.matmul(PB[0:64, 512:1024], ONESFR[0:1, 0:64],
                                R1[:, 512:1024], start=True, stop=True))
                    t.matmul(PA[0:64, 0:512], ONESFR[0:1, 0:64], R2[:, 0:512],
                             start=True, stop=True)
                    st.last(t.matmul(PA[0:64, 512:1024], ONESFR[0:1, 0:64],
                                     R2[:, 512:1024], start=True, stop=True))

                @self.block("vector", 3, waits=[("T", hbc.total)],
                            tag=f"l1fh{h}f{half}cc")
                def hcc(v, st):
                    st(v.tensor_tensor(OPOSS[:, :], OPOSS[:, :], PB[0:64, 0:1024],
                                       ALU.mult))
                    st(v.tensor_tensor(ONEGS[:, :], ONEGS[:, :], PA[0:64, 0:1024],
                                       ALU.mult))
                    st.last(v.tensor_tensor(OCH[:, :], OPOSS[:, :], ONEGS[:, :],
                                            ALU.add))

                self.semc["SOC"] += 16
                soc = self.semc["SOC"]

                @self.block("sync", 0, waits=[("V", hcc.total)],
                            tag=f"l1fh{h}f{half}so")
                def _(e, st, h=h, half=half):
                    e.dma_start(out=oc_d.ap()[64 * h:64 * h + 64,
                                              half * 1024:(half + 1) * 1024],
                                in_=OCH[:, :]).then_inc(self.semh["SOC"], 16)

        self.barrier("l1f_s2", extra=[("SOC", self.val("SOC"))])
        sub2.close()
        sub3 = contextlib.ExitStack()
        self.cur_es = sub3
        WPROJ = self.sbuf("WPROJ", [128, 8 * 1024], BF16)
        OCT = self.sbuf("OCT", [128, 1024], BF16)
        BPROW = self.sbuf("BPROW", [1, 1024], BF16)

        self.sem("LG5")
        self.semc["LG5"] += 32
        lg5 = self.semc["LG5"]

        @self.block("gpsimd", 0, waits=[("SOC", self.val("SOC"))],
                    tag="l1f_wpload")
        def _(g, st):
            g.dma_start(out=WPROJ[:, :].rearrange("p (f o) -> p f o", o=1024),
                        in_=self.dram["l1f_wproj"].ap()
                        .rearrange("(f p) o -> p f o", p=128)
                        ).then_inc(self.semh["LG5"], 16)
            g.dma_start(out=BPROW[:, :], in_=self.dram["l1f_bproj"].ap()
                        ).then_inc(self.semh["LG5"], 16)

        for c in range(NT):
            self.sem("LOC")
            self.semc["LOC"] += 16
            loc = self.semc["LOC"]

            @self.block("gpsimd", 0,
                        waits=[("T", self.val("T")), ("SOC", self.val("SOC"))],
                        tag=f"l1fo{c}g")
            def _(g, st, c=c):
                g.dma_start(out=OCT[:, :].rearrange("p (f q) -> p f q", q=128),
                            in_=oc_d.ap()[:, c * 128:(c + 1) * 128]
                            .rearrange("(f p) q -> p f q", p=128)
                            ).then_inc(self.semh["LOC"], 16)

            @self.block("tensor", 1, waits=[("LOC", loc), ("LG5", lg5),
                                            ("V", self.val("V"))],
                        tag=f"l1fo{c}t")
            def op1(t, st, c=c):
                for fc in range(NF):
                    for q in range(2):
                        t.matmul(PA[:, q * 512:(q + 1) * 512],
                                 OCT[:, fc * 128:(fc + 1) * 128],
                                 WPROJ[:, fc * 1024 + q * 512:
                                       fc * 1024 + (q + 1) * 512],
                                 start=(fc == 0), stop=False)
                for q in range(2):
                    ins = t.matmul(PA[:, q * 512:(q + 1) * 512], ONESROW[:, :],
                                   BPROW[:, q * 512:(q + 1) * 512],
                                   start=False, stop=(q == 1))
                st.last(ins)

            self.postln_residual(c, PA, D, op1.total, f"l1fI{c}")

        self.barrier("l1f")
        sub3.close()
        self.cur_es = outer1
        self.pop_scope()

    def phase_store(self):
        outap = self.dram["out"].ap().rearrange("(t p) d -> p t d", p=128)
        self.sem("STO")
        self.semc["STO"] = 16

        @self.block("sync", 0, waits=[("V", self.val("V"))] +
                    ([("SDC", self.val("SDC"))] if "SDC" in self.semh else []),
                    tag="store")
        def _(e, st):
            e.dma_start(out=outap,
                        in_=self.sb["XS"][:, :].rearrange("p (t d) -> p t d", d=D)
                        ).then_inc(self.semh["STO"], 16)
            e.wait_ge(self.semh["STO"], 16)


def build(upto="l0a"):
    return MK(upto).build()


def input_map(inp, core, upto="l0a"):
    m = {
        "x": np.ascontiguousarray(inp["L0_attn_x"][core]),
        "l0a_wqkv": inp["L0_attn_Wqkv_w"],
        "l0a_bqkv": inp["L0_attn_Wqkv_b"].reshape(1, -1),
        "l0a_wo": inp["L0_attn_out_w"],
        "l0a_bo": inp["L0_attn_out_b"].reshape(1, -1),
        "l0a_gamma": inp["L0_attn_gamma_raw"].reshape(1, -1),
    }
    nb = ["l0a", "l0f", "l1a", "l1f"].index(upto) + 1
    if nb >= 2:
        m.update({"l0f_w1": inp["L0_ffn_w1"], "l0f_b1": inp["L0_ffn_b1"].reshape(1, -1),
                  "l0f_w2": inp["L0_ffn_w2"], "l0f_b2": inp["L0_ffn_b2"].reshape(1, -1)})
    if nb >= 3:
        m.update({"l1a_wg": inp["L1_attn_Wg"], "l1a_bg": inp["L1_attn_bg"].reshape(1, -1),
                  "l1a_wv": inp["L1_attn_Wv"], "l1a_bv": inp["L1_attn_bv"].reshape(1, -1),
                  "l1a_wd": inp["L1_attn_Wd"], "l1a_bd": inp["L1_attn_bd"].reshape(1, -1)})
    if nb >= 4:
        m.update({"l1f_wqkv": inp["L1_ffn_W_qkv"],
                  "l1f_g": inp["L1_ffn_g_info"][0, 0].reshape(1, -1),
                  "l1f_wproj": inp["L1_ffn_W_proj"],
                  "l1f_bproj": inp["L1_ffn_b_proj"].reshape(1, -1)})
    return m



def kernel(**inputs):
    from concourse.bass_utils import run_bass_kernel_spmd
    nc = build("l1f")
    in_maps = [input_map(inputs, core, "l1f") for core in range(8)]
    res = run_bass_kernel_spmd(nc, in_maps, core_ids=list(range(8)))
    out = np.stack([np.asarray(r["out"], np.float32) for r in res.results], axis=0)
    return out


# version 03
# speedup vs baseline: 1.1220x; 1.0958x over previous
import contextlib
import numpy as np
import concourse.bass as bass
import concourse.mybir as mybir

F32 = mybir.dt.float32
BF16 = mybir.dt.bfloat16
AF = mybir.ActivationFunctionType
ALU = mybir.AluOpType
AX = mybir.AxisListType

S, D, H, HD = 2048, 1024, 16, 64
NT, NF, NP = 16, 8, 8
C = 128
EPS = 1e-5


import os


def _needs_drain(ins):
    try:
        for o in ins.ins.outs:
            if "float32" not in str(o.dtype):
                return True
            n = 1
            for pair in list(o.ap)[1:]:
                n *= list(pair)[1]
            if n < 512:
                return True
        return False
    except Exception:
        return True


class Stepper:
    def __init__(self, eng, sem, base, semname=""):
        self.e, self.sem, self.v = eng, sem, base
        self.sw = semname == "V"

    def __call__(self, ins):
        ins.then_inc(self.sem, 1)
        self.v += 1
        if self.sw and _needs_drain(ins):
            self.e.wait_ge(self.sem, self.v)
        return ins

    last = __call__


class MK:
    ENGINES = ("sync", "gpsimd", "vector", "scalar", "tensor")
    SEMMAP = {"vector": "V", "scalar": "S", "tensor": "T", "gpsimd": "G",
              "sync": "Y"}

    def __init__(self, upto="l1f"):
        self.upto = upto
        self.nc = bass.Bass("TRN2", target_bir_lowering=False)
        self.ops = {e: [] for e in self.ENGINES}
        self.semc = {}
        self.semh = {}
        self.es = contextlib.ExitStack()
        self.sb = {}
        self.ps = {}
        self.dram = {}

    def sem(self, name):
        if name not in self.semh:
            self.semh[name] = self.es.enter_context(self.nc.semaphore(name))
            self.semc[name] = 0
        return self.semh[name]

    def val(self, name):
        self.sem(name)
        return self.semc[name]

    def block(self, eng, nsteps, waits=(), tag=""):
        semname = self.SEMMAP[eng]
        self.sem(semname)
        base = self.semc[semname]
        self.semc[semname] += nsteps
        tot = self.semc[semname]
        wl = [(w[0], w[1]) for w in waits if w[0] != semname]

        class Res:
            total = tot

        def deco(fn):
            def run(e, fn=fn, wl=wl, base=base, tot=tot, semname=semname):
                for s, v in wl:
                    if v > 0:
                        e.wait_ge(self.semh[s], v)
                st = Stepper(e, self.semh[semname], base, semname)
                fn(e, st)
                assert st.v == tot, f"{tag}: steps {st.v-base} != {tot-base}"
            self.ops[eng].append(run)
            return Res

        deco.total = tot
        return deco

    def sbuf(self, name, shape, dtype, scope=None):
        es = scope if scope is not None else getattr(self, "cur_es", None) or self.es
        h = es.enter_context(self.nc.sbuf_tensor(name, list(shape), dtype))
        self.sb[name] = h
        return h

    def push_scope(self):
        self.cur_es = contextlib.ExitStack()
        return self.cur_es

    def pop_scope(self):
        self.cur_es.close()
        self.cur_es = None

    def barrier(self, tag, extra=()):
        tv = {s: self.val(s) for s in ("T", "V", "S", "G")}
        ex = list(extra)
        for eng in ("tensor", "vector", "scalar", "gpsimd"):
            @self.block(eng, 0, waits=list(tv.items()) + ex,
                        tag=f"barw{tag}{eng}")
            def _(e, st):
                pass

    def psum(self, name, shape, dtype):
        h = self.es.enter_context(self.nc.psum_tensor(name, list(shape), dtype))
        self.ps[name] = h
        return h

    def din(self, name, shape, dtype=F32):
        self.dram[name] = self.nc.dram_tensor(name, list(shape), dtype,
                                              kind="ExternalInput")

    def dout(self, name, shape, dtype=F32):
        self.dram[name] = self.nc.dram_tensor(name, list(shape), dtype,
                                              kind="ExternalOutput")

    def declare_io(self):
        d = self.din
        d("x", [S, D])
        d("l0a_wqkv", [3 * D, D]); d("l0a_bqkv", [1, 3 * D])
        d("l0a_wo", [D, D]); d("l0a_bo", [1, D]); d("l0a_gamma", [1, H])
        order = ["l0a", "l0f", "l1a", "l1f"]
        self.nblocks = order.index(self.upto) + 1
        if self.nblocks >= 2:
            d("l0f_w1", [4 * D, D]); d("l0f_b1", [1, 4 * D])
            d("l0f_w2", [D, 4 * D]); d("l0f_b2", [1, D])
        if self.nblocks >= 3:
            d("l1a_wg", [D, D]); d("l1a_bg", [1, D])
            d("l1a_wv", [D, D]); d("l1a_bv", [1, D])
            d("l1a_wd", [D, D]); d("l1a_bd", [1, D])
        if self.nblocks >= 4:
            d("l1f_wqkv", [D, 3 * D]); d("l1f_g", [1, D])
            d("l1f_wproj", [D, D]); d("l1f_bproj", [1, D])
            self.dram["cc_in"] = self.nc.dram_tensor("cc_in", [1, 2], F32)
            self.dram["cc_out"] = self.nc.dram_tensor("cc_out", [1, 2], F32)
        self.dout("out", [S, D])

    def build(self):
        self.declare_io()
        with self.es:
            self.sbuf("XS", [128, NT * D], F32)
            self.sbuf("IDENT", [128, 128], BF16)
            self.sbuf("ONESROW", [1, 128], BF16)
            self.sbuf("SQJ", [128, D], BF16)
            self.sb["SQJ_global"] = self.sb["SQJ"]
            self.sbuf("STAT", [128, 6 * NT], F32)
            self.sbuf("COLS", [128, 32], F32)
            self.sbuf("TMPF", [128, D], F32)
            self.sbuf("ONESFR", [1, 128], F32)
            self.sbuf("ONESFC", [128, 1], F32)
            self.sbuf("CFH", [128, 1], F32)
            self.sbuf("CFK", [128, 2], F32)
            self.psum("PA", [128, 2048], F32)
            self.psum("PB", [128, 1024], F32)
            self.psum("PD", [128, 2048], BF16)

            self.phase_init()
            self.phase_l0a()
            if self.nblocks >= 2:
                self.phase_l0f()
            if self.nblocks >= 3:
                self.phase_l1a()
            if self.nblocks >= 4:
                self.phase_l1f()
            self.phase_store()

            block = self.es.enter_context(self.nc.Block())
            for eng in self.ENGINES:
                fns = self.ops[eng]
                getattr(block, eng)(lambda e, fns=fns: [f(e) for f in fns])
        return self.nc

    def prep_weight_ocmajor(self, dram_name, nrows, dst, pref, stgname=None):
        noc = nrows // 128
        stgname = stgname or f"stg_{dram_name}"
        stg = self.sb[stgname] if stgname in self.sb and \
            self.sb[stgname].shape[1] == 2 * D else \
            self.sbuf(stgname, [128, 2 * D], BF16)
        w = self.dram[dram_name].ap()
        for oc in range(noc):
            buf = oc % 2
            semn = f"{pref}{buf}"
            self.sem(semn)
            if not hasattr(self, "dma_load_sems"):
                self.dma_load_sems = set()
            prev_waits = ([(s, self.semc[s]) for s in self.dma_load_sems
                           if s != semn and self.semc[s] > 0]
                          if oc < 2 else [])
            self.dma_load_sems.add(semn)
            self.semc[semn] += 16
            lw = self.semc[semn]

            @self.block("gpsimd", 0,
                        waits=[("V", self.val("V") if oc >= 2 else 0),
                               ("S", self.val("S") if oc < 2 else 0),
                               ("T", self.val("T") if oc < 2 else 0)] + prev_waits,
                        tag=f"prep{dram_name}{oc}g")
            def _(g, st, oc=oc, buf=buf, semn=semn):
                g.dma_start(out=stg[:, buf * D:(buf + 1) * D],
                            in_=w[oc * 128:(oc + 1) * 128, :]
                            ).then_inc(self.semh[semn], 16)

            @self.block("tensor", 1,
                        waits=[(semn, lw), ("V", self.val("V")),
                               ("G", 5 if oc == 0 else 0)],
                        tag=f"prep{dram_name}{oc}t")
            def _(t, st, oc=oc, buf=buf):
                for fc in range(NF):
                    ins = t.transpose(self.ps["PD"][:, fc * 128:(fc + 1) * 128],
                                      stg[:, buf * D + fc * 128:buf * D + (fc + 1) * 128],
                                      self.sb["IDENT"][:, :])
                st.last(ins)

            @self.block("vector", 1, waits=[("T", self.val("T"))],
                        tag=f"prep{dram_name}{oc}v")
            def _(v, st, oc=oc):
                st.last(v.tensor_copy(dst[:, oc * 1024:(oc + 1) * 1024],
                                      self.ps["PD"][:, 0:1024]))

    def bulk_ln_stats(self, src_name, wait_semL, wait_valL, tag):
        ST = self.sb["STAT"]
        XS = self.sb[src_name]
        SQJ = self.sb["SQJ"]

        @self.block("vector", 1, waits=[(wait_semL, wait_valL)], tag=tag + "v1")
        def bv1(v, st):
            st.last(v.reduce_sum(ST[:, 0:NT],
                                 XS[:, :].rearrange("p (t d) -> p t d", d=D),
                                 axis=AX.X))

        @self.block("scalar", NT, waits=[(wait_semL, wait_valL)], tag=tag + "s1")
        def bs1(s, st):
            for i in range(NT):
                st(s.activation(SQJ[:, 0:D], XS[:, i * D:(i + 1) * D],
                                AF.Square, accum_out=ST[:, NT + i:NT + i + 1]))

        @self.block("vector", 4, waits=[("V", bv1.total), ("S", bs1.total)],
                    tag=tag + "v2")
        def bv2(v, st):
            mean, sq = ST[:, 0:NT], ST[:, NT:2 * NT]
            var, rcp = ST[:, 2 * NT:3 * NT], ST[:, 3 * NT:4 * NT]
            st(v.tensor_tensor(var, mean, mean, ALU.mult))
            st(v.scalar_tensor_tensor(var, var, -1.0 / D, sq, ALU.mult, ALU.add))
            st(v.tensor_scalar(var, var, 1.0 / D, EPS, ALU.mult, ALU.add))
            st.last(v.reciprocal(rcp, var))

        @self.block("scalar", 1, waits=[("V", bv2.total)], tag=tag + "s2")
        def bs2(s, st):
            st.last(s.activation(ST[:, 3 * NT:4 * NT], ST[:, 3 * NT:4 * NT],
                                 AF.Sqrt))

        @self.block("vector", 1, waits=[("S", bs2.total)], tag=tag + "v3")
        def bv3(v, st):
            st.last(v.scalar_tensor_tensor(ST[:, 4 * NT:5 * NT], ST[:, 0:NT],
                                           -1.0 / D, ST[:, 3 * NT:4 * NT],
                                           ALU.mult, ALU.mult))
        return bv3.total

    def postln_residual(self, c, psrc, width, wT, tag, scale_col=None, wV=0):
        CO = self.sb["COLS"]
        su, sq, var = CO[:, 8:9], CO[:, 9:10], CO[:, 10:11]
        istd, nm = CO[:, 11:12], CO[:, 12:13]
        XS = self.sb["XS"]
        TMP = self.sb["TMPF"]
        SQJ = self.sb["SQJ"]

        @self.block("vector", 2, waits=[("T", wT), ("V", wV)], tag=tag + "v1")
        def pv1(v, st):
            if scale_col is None:
                st(v.tensor_copy(TMP[:, 0:width], psrc[:, 0:width]))
            else:
                st(v.tensor_scalar(TMP[:, 0:width], psrc[:, 0:width],
                                   scale_col, None, ALU.mult))
            st.last(v.reduce_sum(su, TMP[:, 0:width], axis=AX.X))

        @self.block("scalar", 1, waits=[("V", pv1.total)], tag=tag + "s1")
        def ps1(s, st):
            st(s.activation(SQJ[:, 0:width], TMP[:, 0:width],
                            AF.Square, accum_out=sq))

        @self.block("vector", 4, waits=[("S", ps1.total)], tag=tag + "v2")
        def pv2(v, st):
            st(v.tensor_tensor(var, su, su, ALU.mult))
            st(v.scalar_tensor_tensor(var, var, -1.0 / width, sq, ALU.mult, ALU.add))
            st(v.tensor_scalar(var, var, 1.0 / width, EPS, ALU.mult, ALU.add))
            st.last(v.reciprocal(istd, var))

        @self.block("scalar", 1, waits=[("V", pv2.total)], tag=tag + "s2")
        def ps2(s, st):
            st.last(s.activation(istd, istd, AF.Sqrt))

        @self.block("vector", 3, waits=[("S", ps2.total)], tag=tag + "v3")
        def pv3(v, st, c=c):
            st(v.scalar_tensor_tensor(nm, su, -1.0 / width, istd, ALU.mult,
                                      ALU.mult))
            st(v.tensor_scalar(TMP[:, 0:width], TMP[:, 0:width], istd, nm,
                               ALU.mult, ALU.add))
            st.last(v.tensor_tensor(XS[:, c * D:(c + 1) * D],
                                    XS[:, c * D:(c + 1) * D], TMP[:, 0:width],
                                    ALU.add))
        return pv3.total

    def phase_init(self):
        xin = self.dram["x"].ap().rearrange("(t p) d -> p t d", p=128)
        self.sem("LX")
        self.semc["LX"] = 16

        @self.block("sync", 0, tag="loadx")
        def _(e, st, xin=xin):
            e.dma_start(out=self.sb["XS"][:, :].rearrange("p (t d) -> p t d", d=D),
                        in_=xin).then_inc(self.semh["LX"], 16)

        @self.block("vector", 4, tag="init_cfh")
        def icf(v, st):
            CFK, CFH = self.sb["CFK"], self.sb["CFH"]
            st(v.memset(CFK[:, 0:1], 0.3))
            st(v.tensor_scalar(CFK[:, 1:2], CFK[:, 0:1], 1.5 * 2.0 ** 23,
                               -(1.5 * 2.0 ** 23), ALU.add, ALU.add))
            st(v.tensor_tensor(CFH[:, :], CFK[:, 0:1], CFK[:, 1:2], ALU.is_equal))
            st.last(v.tensor_scalar(CFH[:, :], CFH[:, :], 0.5, None, ALU.mult))

        @self.block("gpsimd", 5, tag="init")
        def _(g, st):
            st(g.memset(self.sb["IDENT"][:, :], 1.0))
            st(g.memset(self.sb["ONESROW"][:, :], 1.0))
            st(g.memset(self.sb["ONESFR"][:, :], 1.0))
            st(g.memset(self.sb["ONESFC"][:, :], 1.0))
            st.last(g.affine_select(self.sb["IDENT"][:, :], self.sb["IDENT"][:, :],
                                    pattern=[[-1, 128]], base=0,
                                    channel_multiplier=1,
                                    compare_op=ALU.is_equal, fill=0.0))

    def phase_l0a(self):
        import os
        CUT = int(os.environ.get("MK_CUT", "9"))
        if CUT == 0:
            return
        scope = self.push_scope()
        WQKVT = self.sbuf("WQKVT", [128, 24 * 1024], BF16)
        WOT = self.sbuf("WOT", [128, 8 * 1024], BF16)
        QB = self.sbuf("QB", [128, 24], F32)
        KB8 = self.sbuf("KB8", [128, 8], F32)
        VBROW = self.sbuf("VBROW", [1, 1024], BF16)
        OBROW = self.sbuf("OBROW", [1, 1024], BF16)
        GRB = self.sbuf("GRB", [128, 16], F32)
        LNG = self.sbuf("LNG", [128, 16], F32)
        GCB = self.sbuf("GCB", [128, 16], F32)
        IJT = self.sbuf("IJT", [128, 128], F32)
        IP1 = self.sbuf("IP1", [128, 128], F32)
        JT64 = self.sbuf("JT64", [128, 64], F32)
        DECAY = self.sbuf("DECAY", [128, 2048], BF16)
        GAMT = self.sbuf("GAMT", [128, 1024], BF16)
        GKT = self.sbuf("GKT", [128, 1024], BF16)
        GIDT = self.sbuf("GIDT", [128, 512], BF16)
        XN = self.sbuf("XN", [128, D], BF16)
        XNT = self.sbuf("XNT", [128, 1024], BF16)
        QBUF = self.sbuf("QBUF", [128, 1024], BF16)
        KBUF = self.sbuf("KBUF", [128, 1024], BF16)
        QT = self.sbuf("QT", [128, 1024], BF16)
        VBUF = self.sbuf("VBUF", [128, 1024], BF16)
        SMT = self.sbuf("SMT", [128, 2048], BF16)
        KTB = self.sbuf("KTB", [128, 1024], BF16)
        RBUF = self.sbuf("RBUF", [128, 512], BF16)
        YBUF = self.sbuf("YBUF", [128, 1024], BF16)
        PA, PB, PD = (self.ps[k] for k in ("PA", "PB", "PD"))
        IDENT, ONESROW = self.sb["IDENT"], self.sb["ONESROW"]
        XS, ST = self.sb["XS"], self.sb["STAT"]

        self.sem("LG0")
        self.semc["LG0"] += 64
        lg0 = self.semc["LG0"]

        @self.block("gpsimd", 0, tag="l0a_smallloads")
        def _(g, st):
            bq = self.dram["l0a_bqkv"].ap()
            g.dma_start(out=QB[:, :],
                        in_=bq.rearrange("o (j p) -> p (o j)", p=128),
                        allow_slow_non_contiguous=True
                        ).then_inc(self.semh["LG0"], 16)
            g.dma_start(out=VBROW[:, :], in_=bq[:, 2048:3072]
                        ).then_inc(self.semh["LG0"], 16)
            g.dma_start(out=OBROW[:, :], in_=self.dram["l0a_bo"].ap()
                        ).then_inc(self.semh["LG0"], 16)
            g.dma_start(out=GRB[:, :],
                        in_=self.dram["l0a_gamma"].ap().partition_broadcast(128),
                        allow_slow_non_contiguous=True
                        ).then_inc(self.semh["LG0"], 16)

        @self.block("scalar", 3, waits=[("LG0", lg0)], tag="l0a_gamma")
        def gs(s, st):
            st(s.activation(GRB[:, :], GRB[:, :], AF.Sigmoid))
            st(s.activation(LNG[:, :], GRB[:, :], AF.Ln))
            st.last(s.activation(GCB[:, :], LNG[:, :], AF.Exp, scale=float(C)))
        gb = gs

        @self.block("gpsimd", 4, tag="l0a_iotas")
        def gi(g, st):
            st(g.iota(IJT[:, :], pattern=[[1, 128]], base=0, channel_multiplier=-1,
                      allow_small_or_imprecise_dtypes=True))
            st(g.affine_select(IJT[:, :], IJT[:, :], pattern=[[1, 128]], base=0,
                               channel_multiplier=-1, compare_op=ALU.is_ge,
                               fill=1e30))
            st(g.iota(IP1[:, :], pattern=[[1, 128]], base=1, channel_multiplier=0,
                      allow_small_or_imprecise_dtypes=True))
            st.last(g.iota(JT64[:, :], pattern=[[0, 64]], base=C - 1,
                           channel_multiplier=-1,
                           allow_small_or_imprecise_dtypes=True))

        @self.block("scalar", 1, waits=[("G", gi.total)], tag="l0a_decays")
        def ds(s, st):
            for h in range(H):
                bI = (h % 2) * 8 + (h // 2)
                s.activation(DECAY[:, bI * 128:(bI + 1) * 128], IJT[:, :], AF.Exp,
                             scale=LNG[:, h:h + 1])
            for p in range(NP):
                s.activation(GAMT[0:64, p * 128:(p + 1) * 128], IP1[0:64, :],
                             AF.Exp, scale=LNG[0:64, 2 * p:2 * p + 1])
                s.activation(GAMT[64:128, p * 128:(p + 1) * 128], IP1[64:128, :],
                             AF.Exp, scale=LNG[64:128, 2 * p + 1:2 * p + 2])
            for h in range(H):
                kc = 64 * (h // 2) + 512 * (h % 2)
                ins = s.activation(GKT[:, kc:kc + 64], JT64[:, 0:64],
                                   AF.Exp, scale=LNG[:, h:h + 1])
            st.last(ins)

        @self.block("vector", 2, waits=[("LG0", lg0), ("G", 5), ("S", ds.total)], tag="l0a_misc")
        def vm(v, st):
            st(v.tensor_scalar(KB8[:, :], QB[:, 8:16], 0.125, None, ALU.mult))
            for p in range(NP):
                v.tensor_scalar(GIDT[0:64, p * 64:(p + 1) * 64],
                                IDENT[0:64, 0:64],
                                GCB[0:64, 2 * p:2 * p + 1], None, ALU.mult)
                ins = v.tensor_scalar(GIDT[64:128, p * 64:(p + 1) * 64],
                                      IDENT[0:64, 0:64],
                                      GCB[64:128, 2 * p + 1:2 * p + 2], None,
                                      ALU.mult)
            st.last(ins)

        @self.block("vector", 1, tag="l0a_rinit")
        def rz(v, st):
            st.last(v.memset(RBUF[:, :], 0.0))

        if CUT >= 2:
            self.prep_weight_ocmajor("l0a_wqkv", 3 * D, WQKVT, "LWA")
            self.prep_weight_ocmajor("l0a_wo", D, WOT, "LWB")

        if CUT < 3:
            return
        stats_done = self.bulk_ln_stats("XS", "LX", 16, "l0a_st")

        def ISTD(c):
            return ST[:, 3 * NT + c:3 * NT + c + 1]

        def NM(c):
            return ST[:, 4 * NT + c:4 * NT + c + 1]

        if CUT < 4:
            return
        for c in range(NT):
            @self.block("vector", 1,
                        waits=[("V", stats_done), ("T", self.val("T"))],
                        tag=f"l0aA{c}")
            def sA(v, st, c=c):
                st.last(v.tensor_scalar(XN[:, :], XS[:, c * D:(c + 1) * D],
                                        ISTD(c), NM(c), ALU.mult, ALU.add))

            @self.block("tensor", 1, waits=[("V", sA.total)], tag=f"l0aB{c}")
            def sB(t, st):
                for j in range(NF):
                    ins = t.transpose(PD[:, j * 128:(j + 1) * 128],
                                      XN[:, j * 128:(j + 1) * 128], IDENT[:, :])
                st.last(ins)

            @self.block("vector", 1, waits=[("T", sB.total)], tag=f"l0aBv{c}")
            def sBv(v, st):
                st.last(v.tensor_copy(XNT[:, :], PD[:, 0:1024]))

            if CUT < 5:
                continue
            @self.block("tensor", 1, waits=[("V", sBv.total)], tag=f"l0aC{c}")
            def sC(t, st):
                for oc in range(16):
                    for fc in range(NF):
                        ins = t.matmul(
                            PA[:, oc * 128:(oc + 1) * 128],
                            WQKVT[:, (oc * 8 + fc) * 128:(oc * 8 + fc + 1) * 128],
                            XNT[:, fc * 128:(fc + 1) * 128],
                            start=(oc % 4 == 0 and fc == 0),
                            stop=(oc % 4 == 3 and fc == NF - 1))
                st.last(ins)

            @self.block("vector", 1, waits=[("T", sC.total)], tag=f"l0aCv{c}")
            def sCv(v, st):
                for p in range(NP):
                    v.tensor_scalar(QBUF[:, p * 128:(p + 1) * 128],
                                    PA[:, p * 128:(p + 1) * 128],
                                    QB[:, p:p + 1], None, ALU.add)
                for p in range(NP):
                    ins = v.tensor_scalar(KBUF[:, p * 128:(p + 1) * 128],
                                          PA[:, (8 + p) * 128:(9 + p) * 128],
                                          0.125, KB8[:, p:p + 1], ALU.mult,
                                          ALU.add)
                st.last(ins)

            if CUT < 6:
                continue
            @self.block("tensor", 1, waits=[("V", sCv.total)], tag=f"l0aD{c}")
            def sD(t, st):
                for voc in range(8):
                    for fc in range(NF):
                        t.matmul(PB[:, voc * 128:(voc + 1) * 128],
                                 XNT[:, fc * 128:(fc + 1) * 128],
                                 WQKVT[:, ((16 + voc) * 8 + fc) * 128:
                                       ((16 + voc) * 8 + fc + 1) * 128],
                                 start=(voc % 4 == 0 and fc == 0), stop=False)
                for voc in range(8):
                    ins = t.matmul(PB[:, voc * 128:(voc + 1) * 128],
                                   ONESROW[:, :],
                                   VBROW[:, voc * 128:(voc + 1) * 128],
                                   start=False, stop=(voc % 4 == 3))
                st.last(ins)

            @self.block("vector", 1, waits=[("T", sD.total)], tag=f"l0aDv{c}")
            def sDv(v, st):
                st.last(v.tensor_copy(VBUF[:, :], PB[:, :]))

            if CUT < 7:
                continue
            @self.block("tensor", 1, waits=[("V", sCv.total)], tag=f"l0aE{c}")
            def sE(t, st):
                for p in range(NP):
                    t.matmul(PA[:, p * 128:(p + 1) * 128],
                             KBUF[0:64, p * 128:(p + 1) * 128],
                             QBUF[0:64, p * 128:(p + 1) * 128],
                             start=(p % 4 == 0), stop=(p % 4 == 3),
                             tile_position=(0, 0))
                    ins = t.matmul(PA[:, 1024 + p * 128:1024 + (p + 1) * 128],
                                   KBUF[64:128, p * 128:(p + 1) * 128],
                                   QBUF[64:128, p * 128:(p + 1) * 128],
                                   start=(p % 4 == 0), stop=(p % 4 == 3),
                                   tile_position=(64, 0))
                st.last(ins)

            @self.block("vector", 2, waits=[("T", sE.total)], tag=f"l0aEv{c}")
            def sEv(v, st):
                st(v.tensor_tensor(SMT[:, :], PA[:, :], DECAY[:, :], ALU.mult))
                st.last(v.tensor_tensor(QT[:, :], QBUF[:, :], GAMT[:, :],
                                        ALU.mult))

            @self.block("tensor", 1, waits=[("V", sBv.total)], tag=f"l0aF{c}")
            def sF(t, st):
                for h in range(H):
                    rh = (h % 2) * 64
                    p = h // 2
                    kc = 64 * p + 1024 * (h % 2)
                    ins = t.transpose(PD[:, kc:kc + 64],
                                      KBUF[rh:rh + 64, p * 128:(p + 1) * 128],
                                      IDENT[rh:rh + 64, rh:rh + 64])
                st.last(ins)

            @self.block("vector", 1, waits=[("T", sF.total)], tag=f"l0aFv{c}")
            def sFv(v, st):
                pdv = PD[:, :].rearrange("p (b q) -> p b q", q=1024)[:, :, 0:512]
                ktv = KTB[:, :].rearrange("p (b q) -> p b q", q=512)
                gkv = GKT[:, :].rearrange("p (b q) -> p b q", q=512)
                st.last(v.tensor_tensor(ktv, pdv, gkv, ALU.mult))

            if CUT < 8:
                continue
            @self.block("tensor", 2,
                        waits=[("V", max(sEv.total, sFv.total, sDv.total))],
                        tag=f"l0aG{c}")
            def sG(t, st):
                for p in range(NP):
                    t.matmul(PA[0:64, p * 128:(p + 1) * 128],
                             VBUF[:, (2 * p) * 64:(2 * p + 1) * 64],
                             SMT[:, p * 128:(p + 1) * 128],
                             start=(p % 4 == 0), stop=False,
                             tile_position=(0, 0), skip_group_check=True)
                    t.matmul(PA[64:128, 1024 + p * 128:1024 + (p + 1) * 128],
                             VBUF[:, (2 * p + 1) * 64:(2 * p + 2) * 64],
                             SMT[:, 1024 + p * 128:1024 + (p + 1) * 128],
                             start=(p % 4 == 0), stop=False,
                             tile_position=(0, 64), skip_group_check=True)
                    t.matmul(PA[0:64, p * 128:(p + 1) * 128],
                             RBUF[0:64, p * 64:(p + 1) * 64],
                             QT[0:64, p * 128:(p + 1) * 128],
                             start=False, stop=(p % 4 == 3),
                             tile_position=(0, 0), skip_group_check=True)
                    ins = t.matmul(PA[64:128, 1024 + p * 128:1024 + (p + 1) * 128],
                                   RBUF[64:128, p * 64:(p + 1) * 64],
                                   QT[64:128, p * 128:(p + 1) * 128],
                                   start=False, stop=(p % 4 == 3),
                                   tile_position=(64, 64), skip_group_check=True)
                st(ins)
                for p in range(NP):
                    t.matmul(PB[0:64, p * 64:(p + 1) * 64],
                             GIDT[0:64, p * 64:(p + 1) * 64],
                             RBUF[0:64, p * 64:(p + 1) * 64],
                             start=(p == 0), stop=False, tile_position=(0, 0),
                             skip_group_check=True)
                    t.matmul(PB[64:128, 512 + p * 64:512 + (p + 1) * 64],
                             GIDT[64:128, p * 64:(p + 1) * 64],
                             RBUF[64:128, p * 64:(p + 1) * 64],
                             start=(p == 0), stop=False, tile_position=(64, 64),
                             skip_group_check=True)
                    t.matmul(PB[0:64, p * 64:(p + 1) * 64],
                             KTB[:, p * 64:(p + 1) * 64],
                             VBUF[:, (2 * p) * 64:(2 * p + 1) * 64],
                             start=False, stop=(p == 7), tile_position=(0, 0),
                             skip_group_check=True)
                    ins = t.matmul(PB[64:128, 512 + p * 64:512 + (p + 1) * 64],
                                   KTB[:, 512 + p * 64:512 + (p + 1) * 64],
                                   VBUF[:, (2 * p + 1) * 64:(2 * p + 2) * 64],
                                   start=False, stop=(p == 7),
                                   tile_position=(0, 64), skip_group_check=True)
                st.last(ins)

            @self.block("vector", 4, waits=[("T", sG.total)], tag=f"l0aGv{c}")
            def sGv(v, st):
                st(v.tensor_copy(YBUF[0:64, :], PA[0:64, 0:1024]))
                st(v.tensor_copy(YBUF[64:128, :], PA[64:128, 1024:2048]))
                st(v.tensor_copy(RBUF[0:64, :], PB[0:64, 0:512]))
                st.last(v.tensor_copy(RBUF[64:128, :], PB[64:128, 512:1024]))

            if CUT < 9:
                continue
            @self.block("tensor", 1, waits=[("V", sGv.total)], tag=f"l0aH{c}")
            def sH(t, st):
                for ooc in range(8):
                    for fc in range(NF):
                        t.matmul(PA[:, ooc * 128:(ooc + 1) * 128],
                                 YBUF[:, fc * 128:(fc + 1) * 128],
                                 WOT[:, (ooc * 8 + fc) * 128:
                                     (ooc * 8 + fc + 1) * 128],
                                 start=(ooc % 4 == 0 and fc == 0), stop=False)
                for ooc in range(8):
                    ins = t.matmul(PA[:, ooc * 128:(ooc + 1) * 128],
                                   ONESROW[:, :],
                                   OBROW[:, ooc * 128:(ooc + 1) * 128],
                                   start=False, stop=(ooc % 4 == 3))
                st.last(ins)

            self.postln_residual(c, PA, D, sH.total, f"l0aI{c}")

        self.barrier("l0a")
        self.pop_scope()

    def prep_ternary(self, dram_name, nrows, ncols, dst, pref, mwb, mwi, tag):
        noc = nrows // 128
        nfc = ncols // 128
        dbuf = 2 if ncols <= 1024 else 1
        stg = self.sbuf(f"stgA_{dram_name}", [128, dbuf * ncols], F32)
        stgq = self.sbuf(f"stgQ_{dram_name}", [128, ncols], BF16)
        stgi = self.sbuf(f"stgI_{dram_name}", [128, ncols], mybir.dt.int8)
        SQJ = self.sb["SQJ"]
        CO = self.sb["COLS"]
        pab, acc = CO[:, 20:21], CO[:, 21:22]
        w = self.dram[dram_name].ap()
        PD, PB = self.ps["PD"], self.ps["PB"]

        @self.block("vector", 1, tag=tag + "z")
        def vz(v, st):
            st.last(v.memset(acc, 0.0))

        lastA = None
        for oc in range(noc):
            buf = oc % dbuf
            semn = f"{pref}{buf}"
            self.sem(semn)
            if not hasattr(self, "dma_load_sems"):
                self.dma_load_sems = set()
            prev_waits = ([(s, self.semc[s]) for s in self.dma_load_sems
                           if s != semn and self.semc[s] > 0]
                          if oc < dbuf else [])
            self.dma_load_sems.add(semn)
            self.semc[semn] += 16
            lw = self.semc[semn]

            @self.block("gpsimd", 0,
                        waits=[("S", self.val("S")),
                               ("V", self.val("V") if oc < dbuf else 0),
                               ("T", self.val("T") if oc < dbuf else 0)] + prev_waits,
                        tag=f"{tag}A{oc}g")
            def _(g, st, oc=oc, buf=buf, semn=semn):
                g.dma_start(out=stg[:, buf * ncols:(buf + 1) * ncols],
                            in_=w[oc * 128:(oc + 1) * 128, :]
                            ).then_inc(self.semh[semn], 16)

            @self.block("scalar", 1,
                        waits=[(semn, lw), ("S", self.val("S")),
                               ("V", self.val("V"))],
                        tag=f"{tag}A{oc}s")
            def sA(s, st, oc=oc, buf=buf):
                st.last(s.activation(SQJ[:, 0:ncols],
                                     stg[:, buf * ncols:(buf + 1) * ncols],
                                     AF.Abs, accum_out=pab))

            @self.block("vector", 1,
                        waits=[("S", sA.total), ("V", self.val("V"))],
                        tag=f"{tag}A{oc}v")
            def vA(v, st):
                st.last(v.tensor_add(acc, acc, pab))
            lastA = vA

        @self.block("vector", 1, waits=[("V", lastA.total)], tag=tag + "c")
        def vc(v, st):
            st.last(v.tensor_copy(CO[:, 22:23], acc))

        @self.block("tensor", 1, waits=[("V", vc.total)], tag=tag + "mm")
        def tmm(t, st):
            st.last(t.matmul(PB[0:1, 0:1], CO[:, 22:23], self.sb["ONESFC"][:, :],
                             start=True, stop=True))

        @self.block("vector", 1, waits=[("T", tmm.total)], tag=tag + "mv")
        def vmv(v, st):
            st.last(v.tensor_scalar(CO[0:1, 23:24], PB[0:1, 0:1],
                                    1.0 / (nrows * ncols), 1e-5, ALU.mult,
                                    ALU.max))

        @self.block("tensor", 1, waits=[("V", vmv.total)], tag=tag + "mb")
        def tmb(t, st):
            st.last(t.matmul(PB[:, 4:5], self.sb["ONESFR"][:, :],
                             CO[0:1, 23:24], start=True, stop=True))

        @self.block("vector", 2, waits=[("T", tmb.total)], tag=tag + "mw")
        def vmw(v, st):
            st(v.tensor_copy(mwb[:, :], PB[:, 4:5]))
            st.last(v.reciprocal(mwi[:, :], mwb[:, :]))

        nrounds = (nfc + 15) // 16
        for oc in range(noc):
            buf = oc % dbuf
            semn = f"{pref}{buf}"
            self.sem(semn)
            self.semc[semn] += 16
            lw = self.semc[semn]

            @self.block("gpsimd", 0,
                        waits=[("V", max(vmw.total, self.val("V") if oc >= dbuf else 0)),
                               ("S", self.val("S"))],
                        tag=f"{tag}B{oc}g")
            def _(g, st, oc=oc, buf=buf, semn=semn):
                g.dma_start(out=stg[:, buf * ncols:(buf + 1) * ncols],
                            in_=w[oc * 128:(oc + 1) * 128, :]
                            ).then_inc(self.semh[semn], 16)

            @self.block("scalar", 1,
                        waits=[(semn, lw), ("S", self.val("S")),
                               ("V", self.val("V"))],
                        tag=f"{tag}B{oc}sg")
            def sB(s, st, oc=oc, buf=buf):
                st.last(s.activation(SQJ[:, 0:ncols],
                                     stg[:, buf * ncols:(buf + 1) * ncols],
                                     AF.Sign))

            @self.block("vector", 5,
                        waits=[(semn, lw), ("V", max(vmw.total, self.val("V"))),
                               ("T", self.val("T")), ("S", sB.total)],
                        tag=f"{tag}B{oc}v")
            def vB(v, st, oc=oc, buf=buf):
                sl = stg[:, buf * ncols:(buf + 1) * ncols]
                sgn = SQJ[:, 0:ncols]
                st(v.tensor_scalar(sl, sl, mwi[:, :], 1.0, ALU.mult, ALU.min))
                st(v.tensor_scalar(sl, sl, -1.0, None, ALU.max))
                st(v.tensor_scalar(sgn, sgn, self.sb["CFH"][:, :], None,
                                   ALU.mult))
                st(v.scalar_tensor_tensor(stgi[:, :], sl, 0.0, sgn, ALU.add,
                                          ALU.add))
                st.last(v.tensor_copy(stgq[:, :], stgi[:, :]))

            for r in range(nrounds):
                f0 = 16 * r
                f1 = min(nfc, f0 + 16)

                @self.block("tensor", 1,
                            waits=[("V", vB.total if r == 0 else self.val("V"))],
                            tag=f"{tag}B{oc}t{r}")
                def tB(t, st, oc=oc, f0=f0, f1=f1):
                    for fc in range(f0, f1):
                        ins = t.transpose(PD[:, (fc - f0) * 128:(fc - f0 + 1) * 128],
                                          stgq[:, fc * 128:(fc + 1) * 128],
                                          self.sb["IDENT"][:, :])
                    st.last(ins)

                @self.block("vector", 1, waits=[("T", tB.total)],
                            tag=f"{tag}B{oc}e{r}")
                def vE(v, st, oc=oc, f0=f0, f1=f1):
                    st.last(v.tensor_copy(
                        dst[:, (oc * nfc + f0) * 128:(oc * nfc + f1) * 128],
                        PD[:, 0:(f1 - f0) * 128]))

    def phase_l0f(self):
        NC2 = 4 * D
        XS, CO, ST = self.sb["XS"], self.sb["COLS"], self.sb["STAT"]
        IDENT, ONESROW = self.sb["IDENT"], self.sb["ONESROW"]
        PA, PB, PD = (self.ps[k] for k in ("PA", "PB", "PD"))
        TMP = self.sb["TMPF"]
        qi2d = self.nc.dram_tensor("qi2_scratch", [S, NC2], mybir.dt.int8)
        import os
        DBG = int(os.environ.get("MK_DBG", "0"))
        if DBG:
            self.dout("dbg_h", [128, NC2])
            self.dout("dbg_t1", [128, 128])
            self.dout("dbg_mw", [1, 4])
            self.dout("dbg_qi", [128, D])
            self.dout("dbg_q2", [128, NC2])
        SCL2 = self.sbuf("SCL2", [128, NT], F32)
        MW2B = self.sbuf("MW2B", [128, 1], F32)
        MW2I = self.sbuf("MW2I", [128, 1], F32)
        _oldsqj = self.sb["SQJ"]
        self.sbuf("SQJA", [128, 4 * D], BF16)
        self.sb["SQJ"] = self.sb["SQJA"]
        SQJP = self.sb["SQJA"]

        self.push_scope()
        TERN1T = self.sbuf("TERN1T", [128, 32 * 8 * 128], BF16)
        MW1B = self.sbuf("MW1B", [128, 1], F32)
        MW1I = self.sbuf("MW1I", [128, 1], F32)
        B1ROW = self.sbuf("B1ROW", [1, NC2], BF16)
        QI = self.sbuf("QI", [128, D], BF16)
        QIT = self.sbuf("QIT", [128, D], BF16)
        SGN = self.sbuf("SGN", [128, NC2], BF16)
        HBUF = self.sbuf("HBUF", [128, NC2], BF16)
        T4K = self.sbuf("T4K", [128, NC2], F32)
        QI2C = self.sbuf("QI2C", [128, NC2], mybir.dt.int8)
        RSCB = self.sbuf("RSCB", [128, 1], BF16)
        RSROW = self.sbuf("RSROW", [1, 128], BF16)

        self.sem("LG1")
        self.semc["LG1"] += 16
        lg1 = self.semc["LG1"]

        @self.block("gpsimd", 0, tag="l0f_b1load")
        def _(g, st):
            g.dma_start(out=B1ROW[:, :], in_=self.dram["l0f_b1"].ap()
                        ).then_inc(self.semh["LG1"], 16)

        self.prep_ternary("l0f_w1", 4 * D, D, TERN1T, "LW1", MW1B, MW1I, "w1p")
        QI32 = self.sb["stgI_l0f_w1"]


        c_idx = {"su": 0, "sq": 1, "mean": 2, "sqd": 3, "m2": 4, "var": 5,
                 "veps": 6, "rcp1": 7, "istd": 13, "rv": 14, "rcp2": 15,
                 "r": 16, "d1": 17, "d2": 18, "mm": 19, "mx": 24, "mn": 25,
                 "t3": 26, "scale": 27, "alpha": 28, "beta": 29, "scol": 30,
                 "rscol": 31}

        def CC(n):
            i = c_idx[n]
            return CO[:, i:i + 1]

        for c in range(NT):
            @self.block("vector", 3, waits=[("V", self.val("V")),
                                            ("S", self.val("S"))],
                        tag=f"l0f1s{c}")
            def fv1(v, st, c=c):
                xt = XS[:, c * D:(c + 1) * D]
                st(v.reduce_sum(CC("su"), xt, axis=AX.X))
                st(v.reduce_max(CC("mx"), xt, axis=AX.X))
                st.last(v.tensor_reduce(CC("mn"), xt, axis=AX.X, op=ALU.min))

            @self.block("scalar", 1, waits=[("V", fv1.total)], tag=f"l0f1q{c}")
            def fs1(s, st, c=c):
                st.last(s.activation(SQJP[:, 0:D],
                                     XS[:, c * D:(c + 1) * D], AF.Square,
                                     accum_out=CC("sq")))

            @self.block("vector", 8, waits=[("S", fs1.total)], tag=f"l0f1c{c}")
            def fv2(v, st):
                st(v.tensor_scalar(CC("mean"), CC("su"), 1.0 / D, None, ALU.mult))
                st(v.tensor_scalar(CC("sqd"), CC("sq"), 1.0 / D, None, ALU.mult))
                st(v.tensor_tensor(CC("m2"), CC("mean"), CC("mean"), ALU.mult))
                st(v.tensor_tensor(CC("var"), CC("sqd"), CC("m2"), ALU.subtract))
                st(v.tensor_scalar(CC("veps"), CC("var"), EPS, None, ALU.add))
                st(v.reciprocal(CC("rcp1"), CC("veps")))
                st(v.tensor_tensor(CC("rv"), CC("var"), CC("rcp1"), ALU.mult))
                st.last(v.tensor_scalar(CC("rv"), CC("rv"), EPS, None, ALU.add))

            @self.block("scalar", 2, waits=[("V", fv2.total)], tag=f"l0f1r{c}")
            def fs2(s, st):
                st(s.activation(CC("istd"), CC("rcp1"), AF.Sqrt))
                st.last(s.activation(CC("rcp2"), CC("rv"), AF.Sqrt))

            @self.block("vector", 10, waits=[("S", fs2.total)], tag=f"l0f1d{c}")
            def fv3(v, st):
                st(v.reciprocal(CC("r"), CC("rcp2")))
                st(v.tensor_tensor(CC("d1"), CC("mx"), CC("mean"), ALU.subtract))
                st(v.tensor_tensor(CC("d2"), CC("mean"), CC("mn"), ALU.subtract))
                st(v.tensor_tensor(CC("mm"), CC("d1"), CC("d2"), ALU.max))
                st(v.tensor_tensor(CC("t3"), CC("mm"), CC("istd"), ALU.mult))
                st(v.tensor_tensor(CC("t3"), CC("t3"), CC("r"), ALU.mult))
                st(v.tensor_scalar(CC("t3"), CC("t3"), 1e-5, None, ALU.max))
                st(v.reciprocal(CC("scale"), CC("t3")))
                st(v.tensor_scalar(CC("scale"), CC("scale"), 127.0, None,
                                   ALU.mult))
                st.last(v.tensor_tensor(CC("alpha"), CC("istd"), CC("r"),
                                        ALU.mult))

            @self.block("vector", 1, waits=[("V", fv3.total)], tag=f"l0f1nm{c}")
            def fvnm(v, st):
                st.last(v.tensor_scalar(CC("d1"), CC("mean"), -1.0, None,
                                        ALU.mult))

            @self.block("scalar", 1, waits=[("V", fvnm.total)], tag=f"l0f1sg{c}")
            def fssg(s, st, c=c):
                st.last(s.activation(SGN[:, 0:D], XS[:, c * D:(c + 1) * D],
                                     AF.Sign, bias=CC("d1")))

            @self.block("vector", 7,
                        waits=[("S", fssg.total),
                               ("SDB", 64 if DBG and c == 1 else 0)],
                        tag=f"l0f1e{c}")
            def fv4(v, st, c=c):
                st(v.tensor_tensor(CC("alpha"), CC("alpha"), CC("scale"),
                                   ALU.mult))
                st(v.scalar_tensor_tensor(CC("beta"), CC("mean"), -1.0,
                                          CC("alpha"), ALU.mult, ALU.mult))
                st(v.tensor_scalar(TMP[:, :], XS[:, c * D:(c + 1) * D],
                                   CC("alpha"), CC("beta"), ALU.mult, ALU.add))
                st(v.tensor_scalar(SGN[:, 0:D], SGN[:, 0:D],
                                   self.sb["CFH"][:, :], None, ALU.mult))
                st(v.scalar_tensor_tensor(QI32[:, :], TMP[:, :], 0.0,
                                          SGN[:, 0:D], ALU.add, ALU.add))
                st(v.tensor_copy(QI[:, :], QI32[:, :]))
                st.last(v.scalar_tensor_tensor(CC("rscol"), CC("scale"), 1.0,
                                               MW1I[:, :], ALU.mult, ALU.mult))

            @self.block("tensor", 1, waits=[("V", fv4.total)], tag=f"l0f1t{c}")
            def ft1(t, st):
                for j in range(NF):
                    ins = t.transpose(PD[:, j * 128:(j + 1) * 128],
                                      QI[:, j * 128:(j + 1) * 128], IDENT[:, :])
                st.last(ins)

            @self.block("vector", 2, waits=[("T", ft1.total)], tag=f"l0f1u{c}")
            def fv5(v, st):
                st(v.tensor_copy(QIT[:, :], PD[:, 0:1024]))
                st.last(v.tensor_copy(RSCB[:, :], CC("rscol")))

            @self.block("tensor", 1, waits=[("V", fv5.total)], tag=f"l0f1w{c}")
            def ft2(t, st):
                st.last(t.matmul(PB[0:1, 8:136], RSCB[:, :], IDENT[:, :],
                                 start=True, stop=True))

            @self.block("vector", 1, waits=[("T", ft2.total)], tag=f"l0f1x{c}")
            def fv6(v, st):
                st.last(v.tensor_copy(RSROW[:, :], PB[0:1, 8:136]))

            @self.block("vector", 2, waits=[("V", fv6.total)], tag=f"l0f1y{c}")
            def fv7(v, st):
                st(v.reciprocal(CC("t3"), CC("scale")))
                st.last(v.tensor_tensor(CC("scol"), MW1B[:, :], CC("t3"), ALU.mult))

            for half in range(2):
                @self.block("tensor", 1,
                            waits=[("V", fv7.total), ("LG1", lg1),
                                   ("S", self.val("S") if half == 1 else 0)],
                            tag=f"l0f1m{c}h{half}")
                def ft3(t, st, half=half):
                    for o16 in range(16):
                        oc = half * 16 + o16
                        for fc in range(NF):
                            t.matmul(PA[:, o16 * 128:(o16 + 1) * 128],
                                     QIT[:, fc * 128:(fc + 1) * 128],
                                     TERN1T[:, (oc * 8 + fc) * 128:
                                            (oc * 8 + fc + 1) * 128],
                                     start=(o16 % 4 == 0 and fc == 0), stop=False)
                    for o16 in range(16):
                        oc = half * 16 + o16
                        ins = t.matmul(PA[:, o16 * 128:(o16 + 1) * 128],
                                       RSROW[:, :],
                                       B1ROW[:, oc * 128:(oc + 1) * 128],
                                       start=False, stop=(o16 % 4 == 3))
                    st.last(ins)

                @self.block("scalar", 4,
                            waits=[("T", ft3.total),
                                   ("SDB", 64 if DBG and c == 1 else 0)],
                            tag=f"l0f1g{c}h{half}")
                def fs3(s, st, half=half):
                    for q in range(4):
                        st(s.activation(
                            HBUF[:, half * 2048 + q * 512:half * 2048 + (q + 1) * 512],
                            PA[:, q * 512:(q + 1) * 512], AF.Gelu,
                            scale=CC("scol")))

            @self.block("scalar", 1, waits=[("S", self.val("S"))],
                        tag=f"l0f1h{c}")
            def fs4(s, st):
                st.last(s.activation(SQJP[:, 0:NC2], HBUF[:, :],
                                     AF.Square, accum_out=CC("sq")))

            @self.block("vector", 3, waits=[("S", fs4.total)], tag=f"l0f1i{c}")
            def fv8(v, st):
                st(v.reduce_max(CC("mx"), HBUF[:, :], axis=AX.X,
                                apply_absolute_value=True))
                st(v.tensor_scalar(CC("rv"), CC("sq"), 1.0 / NC2, EPS,
                                   ALU.mult, ALU.add))
                st.last(v.reciprocal(CC("rcp2"), CC("rv")))

            @self.block("scalar", 1, waits=[("V", fv8.total)], tag=f"l0f1j{c}")
            def fs5(s, st):
                st.last(s.activation(CC("rcp2"), CC("rcp2"), AF.Sqrt))

            @self.block("scalar", 1, waits=[("S", fs5.total)], tag=f"l0f1sg2{c}")
            def fssg2(s, st):
                st.last(s.activation(SGN[:, 0:NC2], HBUF[:, :], AF.Sign))

            @self.block("vector", 7, waits=[("S", fssg2.total)], tag=f"l0f1k{c}")
            def fv9(v, st, c=c):
                st(v.tensor_tensor(CC("t3"), CC("mx"), CC("rcp2"), ALU.mult))
                st(v.tensor_scalar(CC("t3"), CC("t3"), 1e-5, None, ALU.max))
                st(v.reciprocal(CC("scale"), CC("t3")))
                st(v.tensor_scalar(CC("scale"), CC("scale"), 127.0, None,
                                   ALU.mult))
                st(v.tensor_copy(SCL2[:, c:c + 1], CC("scale")))
                st(v.tensor_tensor(CC("alpha"), CC("rcp2"), CC("scale"),
                                   ALU.mult))
                st.last(v.tensor_scalar(T4K[:, :], HBUF[:, :], CC("alpha"),
                                        None, ALU.mult))

            @self.block("vector", 2,
                        waits=[("V", fv9.total), ("SQ2", self.val("SQ2")),
                               ("SDB", 64 if DBG and c == 1 else 0)],
                        tag=f"l0f1l{c}")
            def fv10(v, st):
                st(v.tensor_scalar(SGN[:, 0:NC2], SGN[:, 0:NC2],
                                   self.sb["CFH"][:, :], None, ALU.mult))
                st.last(v.scalar_tensor_tensor(QI2C[:, :], T4K[:, :], 0.0,
                                               SGN[:, 0:NC2], ALU.add, ALU.add))

            self.sem("SQ2")
            self.semc["SQ2"] += 16
            sq2t = self.semc["SQ2"]

            @self.block("sync", 0, waits=[("V", fv10.total)], tag=f"l0f1z{c}")
            def _(e, st, c=c):
                e.dma_start(out=qi2d.ap()[c * 128:(c + 1) * 128, :],
                            in_=QI2C[:, :]).then_inc(self.semh["SQ2"], 16)

            if DBG and c == 0:
                self.sem("SDB")
                self.sem("SDC")
                self.semc["SDB"] += 64
                self.semc["SDC"] += 16

                @self.block("gpsimd", 0, waits=[("V", fv10.total)], tag="dbg0")
                def _(g, st):
                    g.dma_start(out=self.dram["dbg_h"].ap(), in_=HBUF[:, :]
                                ).then_inc(self.semh["SDB"], 16)
                    g.dma_start(out=self.dram["dbg_t1"].ap(),
                                in_=TERN1T[:, 0:128]).then_inc(self.semh["SDB"], 16)
                    g.dma_start(out=self.dram["dbg_qi"].ap(), in_=QI[:, :]
                                ).then_inc(self.semh["SDB"], 16)
                    g.dma_start(out=self.dram["dbg_q2"].ap(),
                                in_=QI2C[:, :]).then_inc(self.semh["SDB"], 16)

                @self.block("vector", 1, waits=[], tag="dbgmw")
                def dmw(v, st):
                    st.last(v.tensor_copy(self.sb["COLS"][0:1, 26:27],
                                          MW1B[0:1, 0:1]))

                @self.block("sync", 0, waits=[("V", dmw.total)], tag="dbg1")
                def _(e, st):
                    e.dma_start(out=self.dram["dbg_mw"].ap()[:, 0:1],
                                in_=MW1B[0:1, 0:1]).then_inc(self.semh["SDC"], 16)

                @self.block("gpsimd", 0, waits=[("SDB", 64)], tag="dbg2")
                def _(g, st):
                    pass

        self.barrier("l0f1", extra=[("SQ2", self.val("SQ2"))] +
                     ([("SDB", self.val("SDB")), ("SDC", self.val("SDC"))]
                      if DBG else []))
        self.pop_scope()

        self.push_scope()
        TERN2T = self.sbuf("TERN2T", [128, 8 * 32 * 128], BF16)
        self.sbuf("SQJB", [128, 4 * D], BF16)
        self.sb["SQJ"] = self.sb["SQJB"]
        B2ROW = self.sbuf("B2ROW", [1, D], BF16)
        QB2 = self.sbuf("QB2", [128, NC2], BF16)
        QIT2 = self.sbuf("QIT2", [128, NC2], BF16)
        QI2L = self.sbuf("QI2L", [128, NC2], mybir.dt.int8)
        RSCB2 = self.sbuf("RSCB2", [128, 1], BF16)
        RSROW2 = self.sbuf("RSROW2", [1, 128], BF16)

        self.sem("LG2")
        self.semc["LG2"] += 16
        lg2 = self.semc["LG2"]

        @self.block("gpsimd", 0, tag="l0f_b2load")
        def _(g, st):
            g.dma_start(out=B2ROW[:, :], in_=self.dram["l0f_b2"].ap()
                        ).then_inc(self.semh["LG2"], 16)

        self.prep_ternary("l0f_w2", D, NC2, TERN2T, "LW2", MW2B, MW2I, "w2p")

        for c in range(NT):
            self.sem("LQ2")
            self.semc["LQ2"] += 16
            lq2 = self.semc["LQ2"]

            @self.block("gpsimd", 0, waits=[("V", self.val("V")),
                                            ("SQ2", 16 * (c + 1))],
                        tag=f"l0f2l{c}")
            def _(g, st, c=c):
                g.dma_start(out=QI2L[:, :],
                            in_=qi2d.ap()[c * 128:(c + 1) * 128, :]
                            ).then_inc(self.semh["LQ2"], 16)

            @self.block("vector", 4, waits=[("LQ2", lq2), ("SQ2", self.val("SQ2"))],
                        tag=f"l0f2c{c}")
            def gv1(v, st, c=c):
                st(v.tensor_copy(QB2[:, :], QI2L[:, :]))
                st(v.reciprocal(CC("t3"), SCL2[:, c:c + 1]))
                st(v.tensor_tensor(CC("scol"), MW2B[:, :], CC("t3"), ALU.mult))
                st.last(v.scalar_tensor_tensor(CC("rscol"), SCL2[:, c:c + 1],
                                               1.0, MW2I[:, :], ALU.mult, ALU.mult))

            for r in range(2):
                @self.block("tensor", 1,
                            waits=[("V", gv1.total if r == 0 else self.val("V"))],
                            tag=f"l0f2t{c}r{r}")
                def gt1(t, st, r=r):
                    for j in range(16):
                        ins = t.transpose(PD[:, j * 128:(j + 1) * 128],
                                          QB2[:, (16 * r + j) * 128:
                                              (16 * r + j + 1) * 128],
                                          IDENT[:, :])
                    st.last(ins)

                @self.block("vector", 1, waits=[("T", gt1.total)],
                            tag=f"l0f2u{c}r{r}")
                def gv2(v, st, r=r):
                    st.last(v.tensor_copy(QIT2[:, r * 2048:(r + 1) * 2048],
                                          PD[:, 0:2048]))

            @self.block("vector", 1, waits=[("V", gv2.total)], tag=f"l0f2v{c}")
            def gv3(v, st):
                st.last(v.tensor_copy(RSCB2[:, :], CC("rscol")))

            @self.block("tensor", 1, waits=[("V", gv3.total)], tag=f"l0f2w{c}")
            def gt2(t, st):
                st.last(t.matmul(PB[0:1, 8:136], RSCB2[:, :], IDENT[:, :],
                                 start=True, stop=True))

            @self.block("vector", 1, waits=[("T", gt2.total)], tag=f"l0f2x{c}")
            def gv4(v, st):
                st.last(v.tensor_copy(RSROW2[:, :], PB[0:1, 8:136]))

            @self.block("tensor", 1, waits=[("V", gv4.total), ("LG2", lg2)],
                        tag=f"l0f2m{c}")
            def gt3(t, st):
                for o2c in range(8):
                    for fc2 in range(32):
                        t.matmul(PA[:, o2c * 128:(o2c + 1) * 128],
                                 QIT2[:, fc2 * 128:(fc2 + 1) * 128],
                                 TERN2T[:, (o2c * 32 + fc2) * 128:
                                        (o2c * 32 + fc2 + 1) * 128],
                                 start=(o2c % 4 == 0 and fc2 == 0), stop=False)
                for o2c in range(8):
                    ins = t.matmul(PA[:, o2c * 128:(o2c + 1) * 128],
                                   RSROW2[:, :],
                                   B2ROW[:, o2c * 128:(o2c + 1) * 128],
                                   start=False, stop=(o2c % 4 == 3))
                st.last(ins)

            self.postln_residual(c, PA, D, gt3.total, f"l0f2I{c}",
                                 scale_col=CC("scol"))

        self.barrier("l0f2")
        self.pop_scope()

    def phase_l1a(self):
        self.sb["SQJ"] = self.sb.get("SQJ_global", self.sb["SQJ"])
        XS, CO, ST = self.sb["XS"], self.sb["COLS"], self.sb["STAT"]
        IDENT, ONESROW = self.sb["IDENT"], self.sb["ONESROW"]
        PA, PB, PD = (self.ps[k] for k in ("PA", "PB", "PD"))
        TMP = self.sb["TMPF"]

        self.push_scope()
        ABUF = self.sbuf("ABUF", [128, NF * 2048], BF16)
        XSC = self.sbuf("XSC", [128, NF * 2048], BF16)
        GB = self.sbuf("GB", [128, 24], F32)
        SGT = self.sbuf("SGT", [128, 2048], BF16)

        self.sem("LG3")
        self.semc["LG3"] += 48
        lg3 = self.semc["LG3"]

        @self.block("gpsimd", 0, tag="l1a_bload")
        def _(g, st):
            for i, nm in enumerate(("l1a_bg", "l1a_bv", "l1a_bd")):
                g.dma_start(out=GB[:, 8 * i:8 * (i + 1)],
                            in_=self.dram[nm].ap().rearrange("o (j p) -> p (o j)", p=128),
                            allow_slow_non_contiguous=True
                            ).then_inc(self.semh["LG3"], 16)

        sub1 = contextlib.ExitStack()
        old_es = self.cur_es
        self.cur_es = sub1
        XNT = self.sbuf("XNT1a", [128, NF * 2048], BF16)
        XN1 = self.sbuf("XN1a", [128, D], BF16)
        WOC = self.sbuf("WOC1a", [128, 8 * 1024], BF16)

        stats_done = self.bulk_ln_stats("XS", "V", self.val("V"), "l1a_st")

        def ISTD(c):
            return ST[:, 3 * NT + c:3 * NT + c + 1]

        def NM(c):
            return ST[:, 4 * NT + c:4 * NT + c + 1]

        for c in range(NT):
            @self.block("vector", 1,
                        waits=[("V", stats_done), ("T", self.val("T"))],
                        tag=f"l1aA{c}")
            def aA(v, st, c=c):
                st.last(v.tensor_scalar(XN1[:, :], XS[:, c * D:(c + 1) * D],
                                        ISTD(c), NM(c), ALU.mult, ALU.add))

            @self.block("tensor", 1, waits=[("V", aA.total)], tag=f"l1aB{c}")
            def aB(t, st, c=c):
                for j in range(NF):
                    ins = t.transpose(PD[:, j * 128:(j + 1) * 128],
                                      XN1[:, j * 128:(j + 1) * 128], IDENT[:, :])
                st.last(ins)

            @self.block("vector", 1, waits=[("T", aB.total)], tag=f"l1aBv{c}")
            def aBv(v, st, c=c):
                pdv = PD[:, 0:1024].rearrange("p (f q) -> p f q", q=128)
                xv = XNT[:, :].rearrange("p (f q) -> p f q", q=2048)[
                    :, :, c * 128:(c + 1) * 128]
                st.last(v.tensor_copy(xv, pdv))

        for mi, (wnm, bcol0, kind) in enumerate(
                (("l1a_wv", 8, "tanh"), ("l1a_wg", 0, "sigmul"),
                 ("l1a_wd", 16, "sigaff"))):
            self.prep_weight_ocmajor(wnm, D, WOC, f"LWg{mi}", stgname="stg_l1a")
            for oc in range(NF):
                @self.block("tensor", 1,
                            waits=[("V", self.val("V")), ("S", self.val("S")),
                                   ("LG3", lg3)],
                            tag=f"l1aP{mi}o{oc}")
                def aP(t, st, oc=oc):
                    for fc in range(NF):
                        for q in range(4):
                            ins = t.matmul(
                                PA[:, q * 512:(q + 1) * 512],
                                WOC[:, (oc * 8 + fc) * 128:(oc * 8 + fc + 1) * 128],
                                XNT[:, fc * 2048 + q * 512:fc * 2048 + (q + 1) * 512],
                                start=(fc == 0), stop=(fc == NF - 1))
                    st.last(ins)

                @self.block("scalar", 1, waits=[("T", aP.total)],
                            tag=f"l1aE{mi}o{oc}")
                def aE(s, st, oc=oc, mi=mi, bcol0=bcol0, kind=kind):
                    fn = AF.Tanh if kind == "tanh" else AF.Sigmoid
                    dstb = XSC if kind == "tanh" else (SGT if kind == "sigmul"
                                                       else ABUF)
                    dst = (dstb[:, oc * 2048:(oc + 1) * 2048] if kind != "sigmul"
                           else SGT[:, :])
                    st.last(s.activation(dst, PA[:, :], fn,
                                         bias=GB[:, bcol0 + oc:bcol0 + oc + 1]))

                if kind == "sigmul":
                    @self.block("vector", 1, waits=[("S", aE.total)],
                                tag=f"l1aM{mi}o{oc}")
                    def aM(v, st, oc=oc):
                        sl = XSC[:, oc * 2048:(oc + 1) * 2048]
                        st.last(v.tensor_tensor(sl, sl, SGT[:, :], ALU.mult))
                elif kind == "sigaff":
                    @self.block("vector", 1, waits=[("S", aE.total)],
                                tag=f"l1aM{mi}o{oc}")
                    def aM(v, st, oc=oc):
                        sl = ABUF[:, oc * 2048:(oc + 1) * 2048]
                        st.last(v.tensor_scalar(sl, sl, 0.998, 0.001, ALU.mult,
                                                ALU.add))

        self.barrier("l1a_sub1")
        sub1.close()
        self.cur_es = old_es

        sub2 = contextlib.ExitStack()
        self.cur_es = sub2
        HTM = self.sbuf("HTM", [128, NT * D], BF16)
        HFC = self.sbuf("HFC", [128, 2048], BF16)

        for fc in range(NF):
            @self.block("vector", 1, waits=[("T", self.val("T"))],
                        tag=f"l1aS{fc}")
            def aS(v, st, fc=fc):
                st.last(v.tensor_tensor_scan(HFC[:, :],
                                             ABUF[:, fc * 2048:(fc + 1) * 2048],
                                             XSC[:, fc * 2048:(fc + 1) * 2048],
                                             0.0, ALU.mult, ALU.add))

            @self.block("tensor", 1, waits=[("V", aS.total)], tag=f"l1aT{fc}")
            def aT(t, st, fc=fc):
                for tt_ in range(NT):
                    ins = t.transpose(PD[:, tt_ * 128:(tt_ + 1) * 128],
                                      HFC[:, tt_ * 128:(tt_ + 1) * 128],
                                      IDENT[:, :])
                st.last(ins)

            @self.block("vector", 1, waits=[("T", aT.total)], tag=f"l1aTv{fc}")
            def aTv(v, st, fc=fc):
                pdv = PD[:, 0:2048].rearrange("p (t q) -> p t q", q=128)
                hv = HTM[:, :].rearrange("p (t q) -> p t q", q=1024)[
                    :, :, fc * 128:(fc + 1) * 128]
                st.last(v.tensor_copy(hv, pdv))

        for c in range(NT):
            self.postln_residual(c, HTM[:, c * D:(c + 1) * D], D,
                                 self.val("T"), f"l1aI{c}", wV=self.val("V"))

        self.barrier("l1a")
        sub2.close()
        self.cur_es = old_es
        self.pop_scope()

    def phase_l1f(self):
        REG = 5
        XS, CO, ST = self.sb["XS"], self.sb["COLS"], self.sb["STAT"]
        IDENT, ONESROW = self.sb["IDENT"], self.sb["ONESROW"]
        ONESFR = self.sb["ONESFR"]
        PA, PB, PD = (self.ps[k] for k in ("PA", "PB", "PD"))
        TMP = self.sb["TMPF"]
        SQJ = self.sb["SQJ_global"]
        self.sb["SQJ"] = SQJ
        nc = self.nc
        qf_d = nc.dram_tensor("qf_scratch", [D, 2048], BF16)
        kf_d = nc.dram_tensor("kf_scratch", [D, 2048], BF16)
        oc_d = nc.dram_tensor("ocf_scratch", [D, 2048], BF16)
        wq_d = nc.dram_tensor("wq_scratch", [128, 16], BF16)
        cc_in = self.dram["cc_in"]
        cc_out = self.dram["cc_out"]
        wqkv = self.dram["l1f_wqkv"].ap()

        self.push_scope()
        VPM = self.sbuf("VPM", [128, 16 * 1040], BF16)
        VAM = self.sbuf("VAM", [128, 16 * 1040], BF16)
        SIMS = self.sbuf("SIMS", [128, NT], F32)
        PMK = self.sbuf("PMK", [128, NT], F32)
        WQCL = self.sbuf("WQCL", [128, NT], F32)
        RGW = self.sbuf("RGW", [128, NT], F32)
        MAXN2 = self.sbuf("MAXN2", [128, 16], F32)
        GHATB = self.sbuf("GHATB", [128, D], BF16)
        BCOLS = self.sbuf("BCOLS", [128, 16], F32)
        THRB = self.sbuf("THRB", [128, 1], F32)
        CCS = self.sbuf("CCS", [1, 4], F32)
        SCL = self.sbuf("SCL", [128, 16], F32)
        DTC = self.sbuf("DTC", [128, 16], F32)
        KMXC = self.sbuf("KMXC", [128, 8], F32)
        SCB = self.sbuf("SCB", [128, 16], BF16)
        CCB = self.sbuf("CCB", [2, 1], F32)
        SRW = self.sbuf("SRW", [16, 1], F32)
        SRB = self.sbuf("SRB", [16, 1], BF16)
        SRR = self.sbuf("SRR", [1, 16], F32)

        self.sem("LG4")
        self.semc["LG4"] += 16
        lg4 = self.semc["LG4"]

        @self.block("gpsimd", 0, waits=[("V", self.val("V"))], tag="l1f_gload")
        def _(g, st):
            g.dma_start(out=GHATB[:, :],
                        in_=self.dram["l1f_g"].ap().partition_broadcast(128),
                        allow_slow_non_contiguous=True
                        ).then_inc(self.semh["LG4"], 16)

        @self.block("vector", 3, waits=[("LG4", lg4), ("S", self.val("S")),
                                        ("T", self.val("T"))], tag="l1f_gn1")
        def gn1(v, st):
            st(v.tensor_tensor(TMP[:, :], GHATB[:, :], GHATB[:, :], ALU.mult))
            st(v.reduce_sum(SCL[:, :].rearrange("p (h o) -> p h o", o=1),
                            TMP[:, :].rearrange("p (h e) -> p h e", e=64),
                            axis=AX.X))
            st.last(v.reciprocal(SCL[:, :], SCL[:, :]))

        @self.block("scalar", 1, waits=[("V", gn1.total)], tag="l1f_gn2")
        def gn2(s, st):
            st.last(s.activation(SCL[:, :], SCL[:, :], AF.Sqrt))

        @self.block("vector", 2, waits=[("S", gn2.total)], tag="l1f_gn3")
        def gn3(v, st):
            gv = GHATB[:, :].rearrange("p (h e) -> p h e", e=64)
            sv = SCL[:, :].rearrange("p (h o) -> p h o", o=1)
            st(v.memset(MAXN2[:, :], 0.0))
            st.last(v.tensor_tensor(gv, gv, sv.broadcast_to([128, 16, 64]),
                                    ALU.mult))

        sub1 = contextlib.ExitStack()
        outer1 = self.cur_es
        self.cur_es = sub1
        XNT = self.sbuf("XNT1f", [128, NF * 2048], BF16)
        XN1 = self.sbuf("XN1f", [128, D], BF16)
        WPART = self.sbuf("WPART", [128, 8 * 1024], BF16)
        WQS = self.sbuf("WQS", [128, 2 * 1024], BF16)
        QKE = self.sbuf("QKE", [128, 2048], BF16)
        QTM = self.sbuf("QTM", [128, D], BF16)
        QSQ = TMP

        stats_done = self.bulk_ln_stats("XS", "V", self.val("V"), "l1f_st")

        def ISTD(c):
            return ST[:, 3 * NT + c:3 * NT + c + 1]

        def NM(c):
            return ST[:, 4 * NT + c:4 * NT + c + 1]

        for c in range(NT):
            @self.block("vector", 1,
                        waits=[("V", stats_done), ("T", self.val("T"))],
                        tag=f"l1fA{c}")
            def fA(v, st, c=c):
                st.last(v.tensor_scalar(XN1[:, :], XS[:, c * D:(c + 1) * D],
                                        ISTD(c), NM(c), ALU.mult, ALU.add))

            @self.block("tensor", 1, waits=[("V", fA.total)], tag=f"l1fB{c}")
            def fB(t, st, c=c):
                for j in range(NF):
                    ins = t.transpose(PD[:, j * 128:(j + 1) * 128],
                                      XN1[:, j * 128:(j + 1) * 128], IDENT[:, :])
                st.last(ins)

            @self.block("vector", 1, waits=[("T", fB.total)], tag=f"l1fBv{c}")
            def fBv(v, st, c=c):
                pdv = PD[:, 0:1024].rearrange("p (f q) -> p f q", q=128)
                xv = XNT[:, :].rearrange("p (f q) -> p f q", q=2048)[
                    :, :, c * 128:(c + 1) * 128]
                st.last(v.tensor_copy(xv, pdv))

        self.sem("LWF0")
        self.dma_load_sems.add("LWF0")
        self.semc["LWF0"] += 16
        lwf_q = self.semc["LWF0"]

        @self.block("gpsimd", 0,
                    waits=[("V", self.val("V")), ("S", self.val("S")),
                           ("T", self.val("T"))] +
                    [(s, self.semc[s]) for s in sorted(self.dma_load_sems)
                     if s != "LWF0" and self.semc[s] > 0],
                    tag="l1f_wqload")
        def _(g, st):
            g.dma_start(out=WPART[:, :].rearrange("p (f o) -> p f o", o=1024),
                        in_=wqkv[:, 0:1024].rearrange("(f p) o -> p f o", p=128)
                        ).then_inc(self.semh["LWF0"], 16)

        for c in range(NT):
            @self.block("tensor", 1,
                        waits=[("LWF0", lwf_q), ("V", self.val("V")),
                               ("S", self.val("S"))], tag=f"l1fq{c}")
            def sq1(t, st, c=c):
                for og in range(2):
                    for fc in range(NF):
                        ins = t.matmul(
                            PB[:, og * 512:(og + 1) * 512],
                            XNT[:, fc * 2048 + c * 128:fc * 2048 + (c + 1) * 128],
                            WPART[:, fc * 1024 + og * 512:fc * 1024 + (og + 1) * 512],
                            start=(fc == 0), stop=(fc == NF - 1))
                st.last(ins)

            @self.block("vector", 7, waits=[("T", sq1.total)], tag=f"l1fqv{c}")
            def sq2(v, st, c=c):
                st(v.tensor_copy(QTM[:, :], PB[:, 0:1024]))
                st(v.tensor_tensor(QSQ[:, :], QTM[:, :], QTM[:, :], ALU.mult))
                st(v.reduce_sum(SCL[:, :].rearrange("p (h o) -> p h o", o=1),
                                QSQ[:, :].rearrange("p (h e) -> p h e", e=64),
                                axis=AX.X))
                st(v.tensor_tensor(MAXN2[:, :], MAXN2[:, :], SCL[:, :], ALU.max))
                st(v.tensor_tensor(QSQ[:, :], QTM[:, :], GHATB[:, :], ALU.mult))
                st(v.reduce_sum(DTC[:, :].rearrange("p (h o) -> p h o", o=1),
                                QSQ[:, :].rearrange("p (h e) -> p h e", e=64),
                                axis=AX.X))
                st.last(v.reciprocal(SCL[:, :], SCL[:, :]))

            @self.block("scalar", 1, waits=[("V", sq2.total)], tag=f"l1fqs{c}")
            def sq3(s, st):
                st.last(s.activation(SCL[:, :], SCL[:, :], AF.Sqrt))

            @self.block("vector", 2, waits=[("S", sq3.total)], tag=f"l1fqx{c}")
            def sq4(v, st, c=c):
                st(v.tensor_tensor(DTC[:, :], DTC[:, :], SCL[:, :], ALU.mult))
                st.last(v.reduce_sum(SIMS[:, c:c + 1], DTC[:, :], axis=AX.X))

        @self.block("vector", 5, waits=[("V", self.val("V"))], tag="l1f_mm")
        def smm(v, st):
            st(v.tensor_copy(SIMS[0:REG, 0:1], SIMS[0:REG, 1:2]))
            st(v.reduce_max(SCL[:, 0:1], SIMS[:, :], axis=AX.X))
            st(v.tensor_reduce(SCL[:, 1:2], SIMS[:, :], axis=AX.X, op=ALU.min))
            st(v.tensor_scalar(SCL[:, 1:2], SCL[:, 1:2], -1.0, None, ALU.mult))
            st.last(v.tensor_copy(SCB[:, 0:2], SCL[:, 0:2]))

        @self.block("tensor", 1, waits=[("V", smm.total)], tag="l1f_mmt")
        def smt(t, st):
            st.last(t.transpose(PD[0:2, 0:128], SCB[:, 0:2], IDENT[:, :]))

        @self.block("vector", 1, waits=[("T", smt.total)], tag="l1f_mmv")
        def smv(v, st):
            st.last(v.reduce_max(CCB[:, :], PD[0:2, 0:128], axis=AX.X))

        self.sem("LCC")
        self.sem("GCC")
        self.semc["LCC"] += 32
        self.semc["GCC"] += 1
        lcc = self.semc["LCC"]
        gcc = self.semc["GCC"]

        @self.block("gpsimd", 0, waits=[("V", smv.total)], tag="l1f_cc")
        def _(g, st):
            g.dma_start(out=cc_in.ap(), in_=CCB[:, :]).then_inc(self.semh["LCC"], 16)
            g.wait_ge(self.semh["LCC"], lcc - 16)
            g.collective_compute(
                "AllReduce", ALU.max,
                replica_groups=[list(range(8))],
                ins=[cc_in.ap().opt()],
                outs=[cc_out.ap().opt()],
            ).then_inc(self.semh["GCC"], 1)
            g.wait_ge(self.semh["GCC"], gcc)
            g.dma_start(out=CCS[:, 0:2], in_=cc_out.ap()).then_inc(self.semh["LCC"], 16)

        @self.block("vector", 2, waits=[("LCC", lcc)], tag="l1f_thr")
        def thr(v, st):
            st(v.tensor_scalar(CCS[:, 2:3], CCS[:, 0:1], 0.9, None, ALU.mult))
            st.last(v.scalar_tensor_tensor(CCS[:, 3:4], CCS[:, 1:2], -0.1,
                                           CCS[:, 2:3], ALU.mult, ALU.add))

        @self.block("tensor", 1, waits=[("V", thr.total)], tag="l1f_thrb")
        def thb(t, st):
            st.last(t.matmul(PB[:, 0:1], ONESFR[:, :], CCS[0:1, 3:4],
                             start=True, stop=True))

        @self.block("vector", 7, waits=[("T", thb.total)], tag="l1f_msk")
        def msk(v, st):
            st(v.tensor_copy(THRB[:, :], PB[:, 0:1]))
            st(v.tensor_scalar(PMK[:, :], SIMS[:, :], THRB[:, :], None, ALU.is_gt))
            st(v.memset(PMK[0:REG, 0:1], 1.0))
            st(v.tensor_copy(WQCL[:, :], PMK[:, :]))
            st(v.memset(WQCL[0:REG, 0:1], 0.5))
            st(v.memset(RGW[:, :], 1.0))
            st.last(v.memset(RGW[0:REG, 0:1], 2.0))

        for oc in range(16):
            isq = oc < 8
            if not isq:
                semn = f"LWK{oc % 2}"
                self.sem(semn)
                self.dma_load_sems.add(semn)
                self.semc[semn] += 16
                lwk = self.semc[semn]

                @self.block("gpsimd", 0,
                            waits=[("V", self.val("V") if oc >= 10 else 0),
                                   ("S", self.val("S") if oc < 10 else 0)],
                            tag=f"l1fk{oc}g")
                def _(g, st, oc=oc, semn=semn):
                    ko = 1024 + (oc - 8) * 128
                    g.dma_start(out=WQS[:, (oc % 2) * 1024:(oc % 2) * 1024 + 1024]
                                .rearrange("p (f o) -> p f o", o=128),
                                in_=wqkv[:, ko:ko + 128]
                                .rearrange("(f p) o -> p f o", p=128)
                                ).then_inc(self.semh[semn], 16)

            @self.block("tensor", 1,
                        waits=([(f"LWK{oc % 2}", self.semc[f"LWK{oc % 2}"])]
                               if not isq else []) +
                        [("V", self.val("V")), ("S", self.val("S"))],
                        tag=f"l1fk{oc}t")
            def qk1(t, st, oc=oc, isq=isq):
                for fc in range(NF):
                    for q in range(4):
                        if isq:
                            lh = WPART[:, fc * 1024 + oc * 128:fc * 1024 + (oc + 1) * 128]
                        else:
                            lh = WQS[:, (oc % 2) * 1024 + fc * 128:
                                     (oc % 2) * 1024 + (fc + 1) * 128]
                        ins = t.matmul(
                            PA[:, q * 512:(q + 1) * 512], lh,
                            XNT[:, fc * 2048 + q * 512:fc * 2048 + (q + 1) * 512],
                            start=(fc == 0), stop=(fc == NF - 1))
                st.last(ins)

            @self.block("vector", 2 if not isq else 1,
                        waits=[("T", qk1.total), ("SQF", self.val("SQF"))],
                        tag=f"l1fk{oc}v")
            def qk2(v, st, oc=oc, isq=isq):
                if isq:
                    st.last(v.tensor_scalar(QKE[:, :], PA[:, :], 0.125, None,
                                            ALU.mult))
                else:
                    st(v.tensor_copy(QKE[:, :], PA[:, :]))
                    st.last(v.reduce_max(KMXC[:, oc - 8:oc - 7], QKE[:, :],
                                         axis=AX.X, apply_absolute_value=True))

            self.sem("SQF")
            self.semc["SQF"] += 16
            sqf = self.semc["SQF"]

            @self.block("sync", 0, waits=[("V", qk2.total)], tag=f"l1fk{oc}s")
            def _(e, st, oc=oc, isq=isq):
                dst = qf_d if isq else kf_d
                ro = (oc % 8) * 128
                e.dma_start(out=dst.ap()[ro:ro + 128, :], in_=QKE[:, :]
                            ).then_inc(self.semh["SQF"], 16)

        @self.block("vector", 2, waits=[("V", self.val("V"))], tag="l1f_b1")
        def sb1(v, st):
            st(v.reduce_max(KMXC[:, 0:1], KMXC[:, :], axis=AX.X))
            st.last(v.tensor_copy(SCB[:, 0:1], KMXC[:, 0:1]))

        @self.block("tensor", 1, waits=[("V", sb1.total)], tag="l1f_b2")
        def sb2(t, st):
            st.last(t.transpose(PD[0:1, 0:128], SCB[:, 0:1], IDENT[:, :]))

        @self.block("vector", 2, waits=[("T", sb2.total)], tag="l1f_b3")
        def sb3(v, st):
            st(v.reduce_max(CCS[0:1, 2:3], PD[0:1, 0:128], axis=AX.X))
            st.last(v.tensor_copy(SCB[:, 0:16], MAXN2[:, :]))

        @self.block("tensor", 1, waits=[("V", sb3.total)], tag="l1f_b4")
        def sb4(t, st):
            st.last(t.transpose(PD[0:16, 0:128], SCB[:, 0:16], IDENT[:, :]))

        @self.block("vector", 1, waits=[("T", sb4.total)], tag="l1f_b5")
        def sb5(v, st):
            st.last(v.reduce_max(SRW[:, :], PD[0:16, 0:128], axis=AX.X))

        @self.block("scalar", 1, waits=[("V", sb5.total)], tag="l1f_b6")
        def sb6(s, st):
            st.last(s.activation(SRW[:, :], SRW[:, :], AF.Sqrt))

        @self.block("tensor", 1, waits=[("S", sb6.total)], tag="l1f_b7")
        def sb7(t, st):
            st.last(t.matmul(PB[0:16, 4:5], ONESFR[0:1, 0:16], CCS[0:1, 2:3],
                             start=True, stop=True))

        @self.block("vector", 2, waits=[("T", sb7.total)], tag="l1f_b8")
        def sb8(v, st):
            st(v.scalar_tensor_tensor(SRW[:, :], SRW[:, :], -1.0, PB[0:16, 4:5],
                                      ALU.mult, ALU.mult))
            st.last(v.tensor_copy(SRB[:, :], SRW[:, :]))

        @self.block("tensor", 1, waits=[("V", sb8.total)], tag="l1f_b9")
        def sb9(t, st):
            st.last(t.transpose(PD[0:1, 128:144], SRB[:, :],
                                IDENT[0:16, 0:16]))

        @self.block("vector", 1, waits=[("T", sb9.total)], tag="l1f_b10")
        def sb10(v, st):
            st.last(v.tensor_copy(SRR[:, :], PD[0:1, 128:144]))

        @self.block("tensor", 1, waits=[("V", sb10.total)], tag="l1f_b11")
        def sb11(t, st):
            st.last(t.matmul(PB[:, 8:24], ONESFR[:, :], SRR[:, :],
                             start=True, stop=True))

        @self.block("vector", 1, waits=[("T", sb11.total)], tag="l1f_b12")
        def sb12(v, st):
            st.last(v.tensor_copy(BCOLS[:, :], PB[:, 8:24]))

        self.semc["LWF0"] += 16
        lwf_v = self.semc["LWF0"]

        @self.block("gpsimd", 0,
                    waits=[("T", self.val("T")), ("SQF", self.val("SQF"))],
                    tag="l1f_wvload")
        def _(g, st):
            g.dma_start(out=WPART[:, :].rearrange("p (f o) -> p f o", o=1024),
                        in_=wqkv[:, 2048:3072].rearrange("(f p) o -> p f o", p=128)
                        ).then_inc(self.semh["LWF0"], 16)

        for c in range(NT):
            for og in range(2):
                @self.block("tensor", 1,
                            waits=[("LWF0", lwf_v), ("V", self.val("V"))],
                            tag=f"l1fv{c}o{og}")
                def vv1(t, st, c=c, og=og):
                    for fc in range(NF):
                        ins = t.matmul(
                            PB[:, 0:512],
                            XNT[:, fc * 2048 + c * 128:fc * 2048 + (c + 1) * 128],
                            WPART[:, fc * 1024 + og * 512:fc * 1024 + (og + 1) * 512],
                            start=(fc == 0), stop=(fc == NF - 1))
                    st.last(ins)

                @self.block("vector", 2, waits=[("T", vv1.total),
                                                ("V", msk.total)],
                            tag=f"l1fv{c}o{og}v")
                def vv2(v, st, c=c, og=og):
                    pbv = PB[:, 0:512].rearrange("p (h e) -> p h e", e=64)
                    vamv = VAM[:, c * 1040 + og * 520:c * 1040 + (og + 1) * 520] \
                        .rearrange("p (h e) -> p h e", e=65)[:, :, 0:64]
                    vpmv = VPM[:, c * 1040 + og * 520:c * 1040 + (og + 1) * 520] \
                        .rearrange("p (h e) -> p h e", e=65)[:, :, 0:64]
                    st(v.tensor_scalar(vamv, pbv, RGW[:, c:c + 1], None, ALU.mult))
                    st.last(v.tensor_scalar(vpmv, pbv, PMK[:, c:c + 1], None,
                                            ALU.mult))

            @self.block("vector", 2, waits=[("V", self.val("V"))],
                        tag=f"l1fv{c}m")
            def vv3(v, st, c=c):
                vamc = VAM[:, c * 1040:(c + 1) * 1040].rearrange(
                    "p (h e) -> p h e", e=65)[:, :, 64:65]
                vpmc = VPM[:, c * 1040:(c + 1) * 1040].rearrange(
                    "p (h e) -> p h e", e=65)[:, :, 64:65]
                st(v.tensor_copy(vamc, RGW[:, c:c + 1].rearrange(
                    "p o -> p o ()" if False else "p (a o) -> p a o", o=1)
                    .broadcast_to([128, 16, 1])))
                st.last(v.tensor_copy(vpmc, PMK[:, c:c + 1].rearrange(
                    "p (a o) -> p a o", o=1).broadcast_to([128, 16, 1])))

        @self.block("vector", 1, waits=[("V", self.val("V"))], tag="l1f_wqb")
        def wqb(v, st):
            st.last(v.tensor_copy(SCB[:, 0:16], WQCL[:, :]))

        self.sem("LWQR")
        self.semc["LWQR"] += 16
        lwqr1 = self.semc["LWQR"]

        @self.block("sync", 0, waits=[("V", wqb.total)], tag="l1f_wqs")
        def _(e, st):
            e.dma_start(out=wq_d.ap(), in_=SCB[:, 0:16]
                        ).then_inc(self.semh["LWQR"], 16)

        self.barrier("l1f_s1", extra=[("SQF", self.val("SQF")),
                                      ("LWQR", self.val("LWQR"))])
        sub1.close()
        self.cur_es = outer1

        sub2 = contextlib.ExitStack()
        self.cur_es = sub2
        PTB = self.sbuf("PTB", [128, 16 * 1024], BF16)
        KFH = self.sbuf("KFH", [64, 2048], BF16)
        QFH = self.sbuf("QFH", [64, 2048], BF16)
        OPOSS = self.sbuf("OPOSS", [64, 1024], BF16)
        ONEGS = self.sbuf("ONEGS", [64, 1024], BF16)
        OCH = OPOSS
        R2 = self.sbuf("R2", [1, 1024], F32)
        R1B = self.sbuf("R1B", [1, 1024], BF16)
        R2B = self.sbuf("R2B", [1, 1024], BF16)
        WQROW = self.sbuf("WQROW", [1, 2048], BF16)
        WQ1ROW = self.sbuf("WQ1ROW", [1, 2048], BF16)

        self.semc["LWQR"] += 16
        lwqr2 = self.semc["LWQR"]

        @self.block("gpsimd", 0, waits=[("LWQR", lwqr1)], tag="l1f_wql")
        def _(g, st):
            g.dma_start(out=WQROW[:, :].rearrange("a (c p) -> a c p", p=128),
                        in_=wq_d.ap().rearrange("p c -> () c p"),
                        allow_slow_non_contiguous=True
                        ).then_inc(self.semh["LWQR"], 16)

        @self.block("vector", 1, waits=[("LWQR", lwqr2)], tag="l1f_wqr")
        def wqr(v, st):
            st.last(v.tensor_scalar(WQ1ROW[:, :], WQROW[:, :], -1.0, 1.0,
                                    ALU.mult, ALU.add))

        self.sem("SOC")
        self.sem("LKH")
        exp_hist = [0, 0]
        hs1_last = {}
        hcc_prev = 0
        for h in range(H):
            self.semc["LKH"] += 32
            lkh = self.semc["LKH"]
            kbuf = 0

            @self.block("gpsimd", 0,
                        waits=[("T", hs1_last.get(h - 1, 0))],
                        tag=f"l1fh{h}g")
            def _(g, st, h=h, kbuf=kbuf):
                g.dma_start(out=KFH[:, kbuf:kbuf + 2048],
                            in_=kf_d.ap()[64 * h:64 * h + 64, :]
                            ).then_inc(self.semh["LKH"], 16)
                g.dma_start(out=QFH[:, kbuf:kbuf + 2048],
                            in_=qf_d.ap()[64 * h:64 * h + 64, :]
                            ).then_inc(self.semh["LKH"], 16)

            for half in range(2):
                for kt in range(NT):
                    for q in range(2):
                        gstep = kt * 2 + q
                        bank = gstep % 2

                        @self.block("tensor", 1,
                                    waits=[("LKH", lkh if gstep == 0 else 0),
                                           ("S", exp_hist[-2])],
                                    tag=f"l1fh{h}f{half}s{gstep}")
                        def hs1(t, st, h=h, half=half, kt=kt, q=q, bank=bank,
                                kbuf=kbuf):
                            st.last(t.matmul(
                                PB[:, bank * 512:(bank + 1) * 512],
                                KFH[:, kbuf + kt * 128:kbuf + (kt + 1) * 128],
                                QFH[:, kbuf + half * 1024 + q * 512:
                                    kbuf + half * 1024 + (q + 1) * 512],
                                start=True, stop=True))

                        @self.block("scalar", 1, waits=[("T", hs1.total)],
                                    tag=f"l1fh{h}f{half}e{gstep}")
                        def hs2(s, st, h=h, kt=kt, q=q, bank=bank):
                            st.last(s.activation(
                                PTB[:, kt * 1024 + q * 512:
                                    kt * 1024 + (q + 1) * 512],
                                PB[:, bank * 512:(bank + 1) * 512], AF.Exp,
                                bias=BCOLS[:, h:h + 1]))
                        exp_hist.append(hs2.total)
                        hs1_last[h] = hs1.total

                @self.block("tensor", 1, waits=[("S", exp_hist[-1]),
                                                ("V", hcc_prev)],
                            tag=f"l1fh{h}f{half}pv")
                def hpv(t, st, h=h):
                    for kt in range(NT):
                        for q in range(2):
                            t.matmul(PA[0:65, q * 512:(q + 1) * 512],
                                     VPM[:, kt * 1040 + h * 65:
                                         kt * 1040 + (h + 1) * 65],
                                     PTB[:, kt * 1024 + q * 512:
                                         kt * 1024 + (q + 1) * 512],
                                     start=(kt == 0), stop=(kt == NT - 1),
                                     skip_group_check=True)
                            ins = t.matmul(PA[0:65, 1024 + q * 512:
                                              1024 + (q + 1) * 512],
                                           VAM[:, kt * 1040 + h * 65:
                                               kt * 1040 + (h + 1) * 65],
                                           PTB[:, kt * 1024 + q * 512:
                                               kt * 1024 + (q + 1) * 512],
                                           start=(kt == 0), stop=(kt == NT - 1),
                                           skip_group_check=True)
                    st.last(ins)

                @self.block("vector", 5, waits=[("T", hpv.total),
                                                ("SOC", self.val("SOC"))],
                            tag=f"l1fh{h}f{half}ca")
                def hcbA(v, st, h=h, half=half):
                    st(v.tensor_copy(OPOSS[:, :], PA[0:64, 0:1024]))
                    st(v.tensor_tensor(ONEGS[:, :], PA[0:64, 1024:2048],
                                       OPOSS[:, :], ALU.subtract))
                    with self.nc.allow_low_precision(reason="bf16 softmax recip row"):
                        st(v.reciprocal(R1B[:, :], PA[64:65, 0:1024]))
                    st(v.tensor_copy(R2[:, :], PA[64:65, 0:1024]))
                    st.last(v.tensor_tensor(R2[:, :], PA[64:65, 1024:2048],
                                            R2[:, :], ALU.subtract))

                @self.block("vector", 3, waits=[], tag=f"l1fh{h}f{half}cb")
                def hcbB(v, st, h=h, half=half):
                    st(v.reciprocal(R2[:, :], R2[:, :]))
                    st(v.tensor_tensor(R1B[:, :], R1B[:, :],
                                       WQROW[:, half * 1024:(half + 1) * 1024],
                                       ALU.mult))
                    st.last(v.tensor_tensor(R2B[:, :], R2[:, :],
                                            WQ1ROW[:, half * 1024:(half + 1) * 1024],
                                            ALU.mult))

                @self.block("tensor", 2, waits=[("V", hcbB.total)],
                            tag=f"l1fh{h}f{half}bc")
                def hbc(t, st):
                    t.matmul(PA[0:64, 0:512], ONESROW[0:1, 0:64], R1B[:, 0:512],
                             start=True, stop=True)
                    st(t.matmul(PA[0:64, 512:1024], ONESROW[0:1, 0:64],
                                R1B[:, 512:1024], start=True, stop=True))
                    t.matmul(PA[0:64, 1024:1536], ONESROW[0:1, 0:64],
                             R2B[:, 0:512], start=True, stop=True)
                    st.last(t.matmul(PA[0:64, 1536:2048], ONESROW[0:1, 0:64],
                                     R2B[:, 512:1024], start=True, stop=True))

                @self.block("vector", 3, waits=[("T", hbc.total)],
                            tag=f"l1fh{h}f{half}cc")
                def hcc(v, st):
                    st(v.tensor_tensor(OPOSS[:, :], OPOSS[:, :], PA[0:64, 0:1024],
                                       ALU.mult))
                    st(v.tensor_tensor(ONEGS[:, :], ONEGS[:, :],
                                       PA[0:64, 1024:2048], ALU.mult))
                    st.last(v.tensor_tensor(OCH[:, :], OPOSS[:, :], ONEGS[:, :],
                                            ALU.add))
                hcc_prev = hcc.total

                self.semc["SOC"] += 16
                soc = self.semc["SOC"]

                @self.block("sync", 0, waits=[("V", hcc.total)],
                            tag=f"l1fh{h}f{half}so")
                def _(e, st, h=h, half=half):
                    e.dma_start(out=oc_d.ap()[64 * h:64 * h + 64,
                                              half * 1024:(half + 1) * 1024],
                                in_=OCH[:, :]).then_inc(self.semh["SOC"], 16)

        self.barrier("l1f_s2", extra=[("SOC", self.val("SOC"))])
        sub2.close()
        sub3 = contextlib.ExitStack()
        self.cur_es = sub3
        WPROJ = self.sbuf("WPROJ", [128, 8 * 1024], BF16)
        OCT = self.sbuf("OCT", [128, 1024], BF16)
        BPROW = self.sbuf("BPROW", [1, 1024], BF16)

        self.sem("LG5")
        self.semc["LG5"] += 32
        lg5 = self.semc["LG5"]

        @self.block("gpsimd", 0, waits=[("SOC", self.val("SOC"))],
                    tag="l1f_wpload")
        def _(g, st):
            g.dma_start(out=WPROJ[:, :].rearrange("p (f o) -> p f o", o=1024),
                        in_=self.dram["l1f_wproj"].ap()
                        .rearrange("(f p) o -> p f o", p=128)
                        ).then_inc(self.semh["LG5"], 16)
            g.dma_start(out=BPROW[:, :], in_=self.dram["l1f_bproj"].ap()
                        ).then_inc(self.semh["LG5"], 16)

        for c in range(NT):
            self.sem("LOC")
            self.semc["LOC"] += 16
            loc = self.semc["LOC"]

            @self.block("gpsimd", 0,
                        waits=[("T", self.val("T")), ("SOC", self.val("SOC"))],
                        tag=f"l1fo{c}g")
            def _(g, st, c=c):
                g.dma_start(out=OCT[:, :].rearrange("p (f q) -> p f q", q=128),
                            in_=oc_d.ap()[:, c * 128:(c + 1) * 128]
                            .rearrange("(f p) q -> p f q", p=128)
                            ).then_inc(self.semh["LOC"], 16)

            @self.block("tensor", 1, waits=[("LOC", loc), ("LG5", lg5),
                                            ("V", self.val("V"))],
                        tag=f"l1fo{c}t")
            def op1(t, st, c=c):
                for fc in range(NF):
                    for q in range(2):
                        t.matmul(PA[:, q * 512:(q + 1) * 512],
                                 OCT[:, fc * 128:(fc + 1) * 128],
                                 WPROJ[:, fc * 1024 + q * 512:
                                       fc * 1024 + (q + 1) * 512],
                                 start=(fc == 0), stop=False)
                for q in range(2):
                    ins = t.matmul(PA[:, q * 512:(q + 1) * 512], ONESROW[:, :],
                                   BPROW[:, q * 512:(q + 1) * 512],
                                   start=False, stop=(q == 1))
                st.last(ins)

            self.postln_residual(c, PA, D, op1.total, f"l1fI{c}")

        self.barrier("l1f")
        sub3.close()
        self.cur_es = outer1
        self.pop_scope()

    def phase_store(self):
        outap = self.dram["out"].ap().rearrange("(t p) d -> p t d", p=128)
        self.sem("STO")
        self.semc["STO"] = 16

        @self.block("sync", 0, waits=[("V", self.val("V"))] +
                    ([("SDC", self.val("SDC"))] if "SDC" in self.semh else []),
                    tag="store")
        def _(e, st):
            e.dma_start(out=outap,
                        in_=self.sb["XS"][:, :].rearrange("p (t d) -> p t d", d=D)
                        ).then_inc(self.semh["STO"], 16)
            e.wait_ge(self.semh["STO"], 16)


def build(upto="l0a"):
    return MK(upto).build()


def input_map(inp, core, upto="l0a"):
    m = {
        "x": np.ascontiguousarray(inp["L0_attn_x"][core]),
        "l0a_wqkv": inp["L0_attn_Wqkv_w"],
        "l0a_bqkv": inp["L0_attn_Wqkv_b"].reshape(1, -1),
        "l0a_wo": inp["L0_attn_out_w"],
        "l0a_bo": inp["L0_attn_out_b"].reshape(1, -1),
        "l0a_gamma": inp["L0_attn_gamma_raw"].reshape(1, -1),
    }
    nb = ["l0a", "l0f", "l1a", "l1f"].index(upto) + 1
    if nb >= 2:
        m.update({"l0f_w1": inp["L0_ffn_w1"], "l0f_b1": inp["L0_ffn_b1"].reshape(1, -1),
                  "l0f_w2": inp["L0_ffn_w2"], "l0f_b2": inp["L0_ffn_b2"].reshape(1, -1)})
    if nb >= 3:
        m.update({"l1a_wg": inp["L1_attn_Wg"], "l1a_bg": inp["L1_attn_bg"].reshape(1, -1),
                  "l1a_wv": inp["L1_attn_Wv"], "l1a_bv": inp["L1_attn_bv"].reshape(1, -1),
                  "l1a_wd": inp["L1_attn_Wd"], "l1a_bd": inp["L1_attn_bd"].reshape(1, -1)})
    if nb >= 4:
        m.update({"l1f_wqkv": inp["L1_ffn_W_qkv"],
                  "l1f_g": inp["L1_ffn_g_info"][0, 0].reshape(1, -1),
                  "l1f_wproj": inp["L1_ffn_W_proj"],
                  "l1f_bproj": inp["L1_ffn_b_proj"].reshape(1, -1)})
    return m



def kernel(**inputs):
    from concourse.bass_utils import run_bass_kernel_spmd
    nc = build("l1f")
    in_maps = [input_map(inputs, core, "l1f") for core in range(8)]
    res = run_bass_kernel_spmd(nc, in_maps, core_ids=list(range(8)))
    out = np.stack([np.asarray(r["out"], np.float32) for r in res.results], axis=0)
    return out


# version 04
# speedup vs baseline: 1.1903x; 1.0609x over previous
import contextlib
import numpy as np
import concourse.bass as bass
import concourse.mybir as mybir

F32 = mybir.dt.float32
BF16 = mybir.dt.bfloat16
AF = mybir.ActivationFunctionType
ALU = mybir.AluOpType
AX = mybir.AxisListType

S, D, H, HD = 2048, 1024, 16, 64
NT, NF, NP = 16, 8, 8
C = 128
EPS = 1e-5


import os


def _needs_drain(ins):
    try:
        for o in ins.ins.outs:
            if "float32" not in str(o.dtype):
                return True
            n = 1
            for pair in list(o.ap)[1:]:
                n *= list(pair)[1]
            if n < 512:
                return True
        return False
    except Exception:
        return True


class Stepper:
    def __init__(self, eng, sem, base, semname=""):
        self.e, self.sem, self.v = eng, sem, base
        self.sw = semname == "V"

    def __call__(self, ins):
        ins.then_inc(self.sem, 1)
        self.v += 1
        if self.sw and _needs_drain(ins):
            self.e.wait_ge(self.sem, self.v)
        return ins

    last = __call__


class MK:
    ENGINES = ("sync", "gpsimd", "vector", "scalar", "tensor")
    SEMMAP = {"vector": "V", "scalar": "S", "tensor": "T", "gpsimd": "G",
              "sync": "Y"}

    def __init__(self, upto="l1f"):
        self.upto = upto
        self.nc = bass.Bass("TRN2", target_bir_lowering=False)
        self.ops = {e: [] for e in self.ENGINES}
        self.semc = {}
        self.semh = {}
        self.es = contextlib.ExitStack()
        self.sb = {}
        self.ps = {}
        self.dram = {}

    def sem(self, name):
        if name not in self.semh:
            self.semh[name] = self.es.enter_context(self.nc.semaphore(name))
            self.semc[name] = 0
        return self.semh[name]

    def val(self, name):
        self.sem(name)
        return self.semc[name]

    def block(self, eng, nsteps, waits=(), tag=""):
        semname = self.SEMMAP[eng]
        self.sem(semname)
        base = self.semc[semname]
        self.semc[semname] += nsteps
        tot = self.semc[semname]
        wl = [(w[0], w[1]) for w in waits if w[0] != semname]

        class Res:
            total = tot

        def deco(fn):
            def run(e, fn=fn, wl=wl, base=base, tot=tot, semname=semname):
                for s, v in wl:
                    if v > 0:
                        e.wait_ge(self.semh[s], v)
                st = Stepper(e, self.semh[semname], base, semname)
                fn(e, st)
                assert st.v == tot, f"{tag}: steps {st.v-base} != {tot-base}"
            self.ops[eng].append(run)
            return Res

        deco.total = tot
        return deco

    def sbuf(self, name, shape, dtype, scope=None):
        es = scope if scope is not None else getattr(self, "cur_es", None) or self.es
        h = es.enter_context(self.nc.sbuf_tensor(name, list(shape), dtype))
        self.sb[name] = h
        return h

    def push_scope(self):
        self.cur_es = contextlib.ExitStack()
        return self.cur_es

    def pop_scope(self):
        self.cur_es.close()
        self.cur_es = None

    def barrier(self, tag, extra=()):
        tv = {s: self.val(s) for s in ("T", "V", "S", "G")}
        ex = list(extra)
        for eng in ("tensor", "vector", "scalar", "gpsimd"):
            @self.block(eng, 0, waits=list(tv.items()) + ex,
                        tag=f"barw{tag}{eng}")
            def _(e, st):
                pass

    def psum(self, name, shape, dtype):
        h = self.es.enter_context(self.nc.psum_tensor(name, list(shape), dtype))
        self.ps[name] = h
        return h

    def din(self, name, shape, dtype=F32):
        self.dram[name] = self.nc.dram_tensor(name, list(shape), dtype,
                                              kind="ExternalInput")

    def dout(self, name, shape, dtype=F32):
        self.dram[name] = self.nc.dram_tensor(name, list(shape), dtype,
                                              kind="ExternalOutput")

    def declare_io(self):
        d = self.din
        d("x", [S, D])
        d("l0a_wqkv", [3 * D, D]); d("l0a_bqkv", [1, 3 * D])
        d("l0a_wo", [D, D]); d("l0a_bo", [1, D]); d("l0a_gamma", [1, H])
        order = ["l0a", "l0f", "l1a", "l1f"]
        self.nblocks = order.index(self.upto) + 1
        if self.nblocks >= 2:
            d("l0f_w1", [4 * D, D]); d("l0f_b1", [1, 4 * D])
            d("l0f_w2", [D, 4 * D]); d("l0f_b2", [1, D])
        if self.nblocks >= 3:
            d("l1a_wg", [D, D]); d("l1a_bg", [1, D])
            d("l1a_wv", [D, D]); d("l1a_bv", [1, D])
            d("l1a_wd", [D, D]); d("l1a_bd", [1, D])
        if self.nblocks >= 4:
            d("l1f_wqkv", [D, 3 * D]); d("l1f_g", [1, D])
            d("l1f_wproj", [D, D]); d("l1f_bproj", [1, D])
            self.dram["cc_in"] = self.nc.dram_tensor("cc_in", [1, 2], F32)
            self.dram["cc_out"] = self.nc.dram_tensor("cc_out", [1, 2], F32)
        self.dout("out", [S, D])

    def build(self):
        self.declare_io()
        with self.es:
            self.sbuf("XS", [128, NT * D], F32)
            self.sbuf("IDENT", [128, 128], BF16)
            self.sbuf("ONESROW", [1, 128], BF16)
            self.sbuf("SQJ", [128, D], BF16)
            self.sb["SQJ_global"] = self.sb["SQJ"]
            self.sbuf("STAT", [128, 6 * NT], F32)
            self.sbuf("COLS", [128, 32], F32)
            self.sbuf("TMPF", [128, D], F32)
            self.sbuf("ONESFR", [1, 128], F32)
            self.sbuf("ONESFC", [128, 1], F32)
            self.sbuf("CFH", [128, 1], F32)
            self.sbuf("CFK", [128, 2], F32)
            self.psum("PA", [128, 2048], F32)
            self.psum("PB", [128, 1024], F32)
            self.psum("PD", [128, 2048], BF16)

            self.phase_init()
            self.phase_l0a()
            if self.nblocks >= 2:
                self.phase_l0f()
            if self.nblocks >= 3:
                self.phase_l1a()
            if self.nblocks >= 4:
                self.phase_l1f()
            self.phase_store()

            block = self.es.enter_context(self.nc.Block())
            for eng in self.ENGINES:
                fns = self.ops[eng]
                getattr(block, eng)(lambda e, fns=fns: [f(e) for f in fns])
        return self.nc

    def prep_weight_ocmajor(self, dram_name, nrows, dst, pref, stgname=None):
        noc = nrows // 128
        stgname = stgname or f"stg_{dram_name}"
        stg = self.sb[stgname] if stgname in self.sb and \
            self.sb[stgname].shape[1] == 2 * D else \
            self.sbuf(stgname, [128, 2 * D], BF16)
        w = self.dram[dram_name].ap()
        for oc in range(noc):
            buf = oc % 2
            semn = f"{pref}{buf}"
            self.sem(semn)
            if not hasattr(self, "dma_load_sems"):
                self.dma_load_sems = set()
            prev_waits = ([(s, self.semc[s]) for s in self.dma_load_sems
                           if s != semn and self.semc[s] > 0]
                          if oc < 2 else [])
            self.dma_load_sems.add(semn)
            self.semc[semn] += 16
            lw = self.semc[semn]

            @self.block("gpsimd", 0,
                        waits=[("V", self.val("V") if oc >= 2 else 0),
                               ("S", self.val("S") if oc < 2 else 0),
                               ("T", self.val("T") if oc < 2 else 0)] + prev_waits,
                        tag=f"prep{dram_name}{oc}g")
            def _(g, st, oc=oc, buf=buf, semn=semn):
                g.dma_start(out=stg[:, buf * D:(buf + 1) * D],
                            in_=w[oc * 128:(oc + 1) * 128, :]
                            ).then_inc(self.semh[semn], 16)

            @self.block("tensor", 1,
                        waits=[(semn, lw), ("V", self.val("V")),
                               ("G", 5 if oc == 0 else 0)],
                        tag=f"prep{dram_name}{oc}t")
            def _(t, st, oc=oc, buf=buf):
                for fc in range(NF):
                    ins = t.transpose(self.ps["PD"][:, fc * 128:(fc + 1) * 128],
                                      stg[:, buf * D + fc * 128:buf * D + (fc + 1) * 128],
                                      self.sb["IDENT"][:, :])
                st.last(ins)

            @self.block("vector", 1, waits=[("T", self.val("T"))],
                        tag=f"prep{dram_name}{oc}v")
            def _(v, st, oc=oc):
                st.last(v.tensor_copy(dst[:, oc * 1024:(oc + 1) * 1024],
                                      self.ps["PD"][:, 0:1024]))

    def bulk_ln_stats(self, src_name, wait_semL, wait_valL, tag):
        ST = self.sb["STAT"]
        XS = self.sb[src_name]
        SQJ = self.sb["SQJ"]

        @self.block("vector", 1, waits=[(wait_semL, wait_valL)], tag=tag + "v1")
        def bv1(v, st):
            st.last(v.reduce_sum(ST[:, 0:NT],
                                 XS[:, :].rearrange("p (t d) -> p t d", d=D),
                                 axis=AX.X))

        @self.block("scalar", NT, waits=[(wait_semL, wait_valL)], tag=tag + "s1")
        def bs1(s, st):
            for i in range(NT):
                st(s.activation(SQJ[:, 0:D], XS[:, i * D:(i + 1) * D],
                                AF.Square, accum_out=ST[:, NT + i:NT + i + 1]))

        @self.block("vector", 4, waits=[("V", bv1.total), ("S", bs1.total)],
                    tag=tag + "v2")
        def bv2(v, st):
            mean, sq = ST[:, 0:NT], ST[:, NT:2 * NT]
            var, rcp = ST[:, 2 * NT:3 * NT], ST[:, 3 * NT:4 * NT]
            st(v.tensor_tensor(var, mean, mean, ALU.mult))
            st(v.scalar_tensor_tensor(var, var, -1.0 / D, sq, ALU.mult, ALU.add))
            st(v.tensor_scalar(var, var, 1.0 / D, EPS, ALU.mult, ALU.add))
            st.last(v.reciprocal(rcp, var))

        @self.block("scalar", 1, waits=[("V", bv2.total)], tag=tag + "s2")
        def bs2(s, st):
            st.last(s.activation(ST[:, 3 * NT:4 * NT], ST[:, 3 * NT:4 * NT],
                                 AF.Sqrt))

        @self.block("vector", 1, waits=[("S", bs2.total)], tag=tag + "v3")
        def bv3(v, st):
            st.last(v.scalar_tensor_tensor(ST[:, 4 * NT:5 * NT], ST[:, 0:NT],
                                           -1.0 / D, ST[:, 3 * NT:4 * NT],
                                           ALU.mult, ALU.mult))
        return bv3.total

    def postln_residual(self, c, psrc, width, wT, tag, scale_col=None, wV=0):
        CO = self.sb["COLS"]
        su, sq, var = CO[:, 8:9], CO[:, 9:10], CO[:, 10:11]
        istd, nm = CO[:, 11:12], CO[:, 12:13]
        XS = self.sb["XS"]
        TMP = self.sb["TMPF"]
        SQJ = self.sb["SQJ"]

        @self.block("vector", 2, waits=[("T", wT), ("V", wV)], tag=tag + "v1")
        def pv1(v, st):
            if scale_col is None:
                st(v.tensor_copy(TMP[:, 0:width], psrc[:, 0:width]))
            else:
                st(v.tensor_scalar(TMP[:, 0:width], psrc[:, 0:width],
                                   scale_col, None, ALU.mult))
            st.last(v.reduce_sum(su, TMP[:, 0:width], axis=AX.X))

        @self.block("scalar", 1, waits=[("V", pv1.total)], tag=tag + "s1")
        def ps1(s, st):
            st(s.activation(SQJ[:, 0:width], TMP[:, 0:width],
                            AF.Square, accum_out=sq))

        @self.block("vector", 4, waits=[("S", ps1.total)], tag=tag + "v2")
        def pv2(v, st):
            st(v.tensor_tensor(var, su, su, ALU.mult))
            st(v.scalar_tensor_tensor(var, var, -1.0 / width, sq, ALU.mult, ALU.add))
            st(v.tensor_scalar(var, var, 1.0 / width, EPS, ALU.mult, ALU.add))
            st.last(v.reciprocal(istd, var))

        @self.block("scalar", 1, waits=[("V", pv2.total)], tag=tag + "s2")
        def ps2(s, st):
            st.last(s.activation(istd, istd, AF.Sqrt))

        @self.block("vector", 3, waits=[("S", ps2.total)], tag=tag + "v3")
        def pv3(v, st, c=c):
            st(v.scalar_tensor_tensor(nm, su, -1.0 / width, istd, ALU.mult,
                                      ALU.mult))
            st(v.tensor_scalar(TMP[:, 0:width], TMP[:, 0:width], istd, nm,
                               ALU.mult, ALU.add))
            st.last(v.tensor_tensor(XS[:, c * D:(c + 1) * D],
                                    XS[:, c * D:(c + 1) * D], TMP[:, 0:width],
                                    ALU.add))
        return pv3.total

    def phase_init(self):
        xin = self.dram["x"].ap().rearrange("(t p) d -> p t d", p=128)
        self.sem("LX")
        self.semc["LX"] = 16

        @self.block("sync", 0, tag="loadx")
        def _(e, st, xin=xin):
            e.dma_start(out=self.sb["XS"][:, :].rearrange("p (t d) -> p t d", d=D),
                        in_=xin).then_inc(self.semh["LX"], 16)

        @self.block("vector", 4, tag="init_cfh")
        def icf(v, st):
            CFK, CFH = self.sb["CFK"], self.sb["CFH"]
            st(v.memset(CFK[:, 0:1], 0.3))
            st(v.tensor_scalar(CFK[:, 1:2], CFK[:, 0:1], 1.5 * 2.0 ** 23,
                               -(1.5 * 2.0 ** 23), ALU.add, ALU.add))
            st(v.tensor_tensor(CFH[:, :], CFK[:, 0:1], CFK[:, 1:2], ALU.is_equal))
            st.last(v.tensor_scalar(CFH[:, :], CFH[:, :], 0.5, None, ALU.mult))

        @self.block("gpsimd", 5, tag="init")
        def _(g, st):
            st(g.memset(self.sb["IDENT"][:, :], 1.0))
            st(g.memset(self.sb["ONESROW"][:, :], 1.0))
            st(g.memset(self.sb["ONESFR"][:, :], 1.0))
            st(g.memset(self.sb["ONESFC"][:, :], 1.0))
            st.last(g.affine_select(self.sb["IDENT"][:, :], self.sb["IDENT"][:, :],
                                    pattern=[[-1, 128]], base=0,
                                    channel_multiplier=1,
                                    compare_op=ALU.is_equal, fill=0.0))

    def phase_l0a(self):
        import os
        CUT = int(os.environ.get("MK_CUT", "9"))
        if CUT == 0:
            return
        scope = self.push_scope()
        WQKVT = self.sbuf("WQKVT", [128, 24 * 1024], BF16)
        WOT = self.sbuf("WOT", [128, 8 * 1024], BF16)
        QB = self.sbuf("QB", [128, 24], F32)
        KB8 = self.sbuf("KB8", [128, 8], F32)
        VBROW = self.sbuf("VBROW", [1, 1024], BF16)
        OBROW = self.sbuf("OBROW", [1, 1024], BF16)
        GRB = self.sbuf("GRB", [128, 16], F32)
        LNG = self.sbuf("LNG", [128, 16], F32)
        GCB = self.sbuf("GCB", [128, 16], F32)
        IJT = self.sbuf("IJT", [128, 128], F32)
        IP1 = self.sbuf("IP1", [128, 128], F32)
        JT64 = self.sbuf("JT64", [128, 64], F32)
        DECAY = self.sbuf("DECAY", [128, 2048], BF16)
        GAMT = self.sbuf("GAMT", [128, 1024], BF16)
        GKT = self.sbuf("GKT", [128, 1024], BF16)
        GIDT = self.sbuf("GIDT", [128, 512], BF16)
        XN = self.sbuf("XN", [128, D], BF16)
        XNT = self.sbuf("XNT", [128, 1024], BF16)
        QBUF = self.sbuf("QBUF", [128, 1024], BF16)
        KBUF = self.sbuf("KBUF", [128, 1024], BF16)
        QT = self.sbuf("QT", [128, 1024], BF16)
        VBUF = self.sbuf("VBUF", [128, 1024], BF16)
        SMT = self.sbuf("SMT", [128, 2048], BF16)
        KTB = self.sbuf("KTB", [128, 1024], BF16)
        RBUF = self.sbuf("RBUF", [128, 512], BF16)
        YBUF = self.sbuf("YBUF", [128, 1024], BF16)
        PA, PB, PD = (self.ps[k] for k in ("PA", "PB", "PD"))
        IDENT, ONESROW = self.sb["IDENT"], self.sb["ONESROW"]
        XS, ST = self.sb["XS"], self.sb["STAT"]

        self.sem("LG0")
        self.semc["LG0"] += 64
        lg0 = self.semc["LG0"]

        @self.block("gpsimd", 0, tag="l0a_smallloads")
        def _(g, st):
            bq = self.dram["l0a_bqkv"].ap()
            g.dma_start(out=QB[:, :],
                        in_=bq.rearrange("o (j p) -> p (o j)", p=128),
                        allow_slow_non_contiguous=True
                        ).then_inc(self.semh["LG0"], 16)
            g.dma_start(out=VBROW[:, :], in_=bq[:, 2048:3072]
                        ).then_inc(self.semh["LG0"], 16)
            g.dma_start(out=OBROW[:, :], in_=self.dram["l0a_bo"].ap()
                        ).then_inc(self.semh["LG0"], 16)
            g.dma_start(out=GRB[:, :],
                        in_=self.dram["l0a_gamma"].ap().partition_broadcast(128),
                        allow_slow_non_contiguous=True
                        ).then_inc(self.semh["LG0"], 16)

        @self.block("scalar", 3, waits=[("LG0", lg0)], tag="l0a_gamma")
        def gs(s, st):
            st(s.activation(GRB[:, :], GRB[:, :], AF.Sigmoid))
            st(s.activation(LNG[:, :], GRB[:, :], AF.Ln))
            st.last(s.activation(GCB[:, :], LNG[:, :], AF.Exp, scale=float(C)))
        gb = gs

        @self.block("gpsimd", 4, tag="l0a_iotas")
        def gi(g, st):
            st(g.iota(IJT[:, :], pattern=[[1, 128]], base=0, channel_multiplier=-1,
                      allow_small_or_imprecise_dtypes=True))
            st(g.affine_select(IJT[:, :], IJT[:, :], pattern=[[1, 128]], base=0,
                               channel_multiplier=-1, compare_op=ALU.is_ge,
                               fill=1e30))
            st(g.iota(IP1[:, :], pattern=[[1, 128]], base=1, channel_multiplier=0,
                      allow_small_or_imprecise_dtypes=True))
            st.last(g.iota(JT64[:, :], pattern=[[0, 64]], base=C - 1,
                           channel_multiplier=-1,
                           allow_small_or_imprecise_dtypes=True))

        @self.block("scalar", 1, waits=[("G", gi.total)], tag="l0a_decays")
        def ds(s, st):
            for h in range(H):
                bI = (h % 2) * 8 + (h // 2)
                s.activation(DECAY[:, bI * 128:(bI + 1) * 128], IJT[:, :], AF.Exp,
                             scale=LNG[:, h:h + 1])
            for p in range(NP):
                s.activation(GAMT[0:64, p * 128:(p + 1) * 128], IP1[0:64, :],
                             AF.Exp, scale=LNG[0:64, 2 * p:2 * p + 1])
                s.activation(GAMT[64:128, p * 128:(p + 1) * 128], IP1[64:128, :],
                             AF.Exp, scale=LNG[64:128, 2 * p + 1:2 * p + 2])
            for h in range(H):
                kc = 64 * (h // 2) + 512 * (h % 2)
                ins = s.activation(GKT[:, kc:kc + 64], JT64[:, 0:64],
                                   AF.Exp, scale=LNG[:, h:h + 1])
            st.last(ins)

        @self.block("vector", 2, waits=[("LG0", lg0), ("G", 5), ("S", ds.total)], tag="l0a_misc")
        def vm(v, st):
            st(v.tensor_scalar(KB8[:, :], QB[:, 8:16], 0.125, None, ALU.mult))
            for p in range(NP):
                v.tensor_scalar(GIDT[0:64, p * 64:(p + 1) * 64],
                                IDENT[0:64, 0:64],
                                GCB[0:64, 2 * p:2 * p + 1], None, ALU.mult)
                ins = v.tensor_scalar(GIDT[64:128, p * 64:(p + 1) * 64],
                                      IDENT[0:64, 0:64],
                                      GCB[64:128, 2 * p + 1:2 * p + 2], None,
                                      ALU.mult)
            st.last(ins)

        @self.block("vector", 1, tag="l0a_rinit")
        def rz(v, st):
            st.last(v.memset(RBUF[:, :], 0.0))

        if CUT >= 2:
            self.prep_weight_ocmajor("l0a_wqkv", 3 * D, WQKVT, "LWA")
            self.prep_weight_ocmajor("l0a_wo", D, WOT, "LWB")

        if CUT < 3:
            return
        stats_done = self.bulk_ln_stats("XS", "LX", 16, "l0a_st")

        def ISTD(c):
            return ST[:, 3 * NT + c:3 * NT + c + 1]

        def NM(c):
            return ST[:, 4 * NT + c:4 * NT + c + 1]

        if CUT < 4:
            return
        for c in range(NT):
            @self.block("vector", 1,
                        waits=[("V", stats_done), ("T", self.val("T"))],
                        tag=f"l0aA{c}")
            def sA(v, st, c=c):
                st.last(v.tensor_scalar(XN[:, :], XS[:, c * D:(c + 1) * D],
                                        ISTD(c), NM(c), ALU.mult, ALU.add))

            @self.block("tensor", 1, waits=[("V", sA.total)], tag=f"l0aB{c}")
            def sB(t, st):
                for j in range(NF):
                    ins = t.transpose(PD[:, j * 128:(j + 1) * 128],
                                      XN[:, j * 128:(j + 1) * 128], IDENT[:, :])
                st.last(ins)

            @self.block("vector", 1, waits=[("T", sB.total)], tag=f"l0aBv{c}")
            def sBv(v, st):
                st.last(v.tensor_copy(XNT[:, :], PD[:, 0:1024]))

            if CUT < 5:
                continue
            @self.block("tensor", 1, waits=[("V", sBv.total)], tag=f"l0aC{c}")
            def sC(t, st):
                for oc in range(16):
                    for fc in range(NF):
                        ins = t.matmul(
                            PA[:, oc * 128:(oc + 1) * 128],
                            WQKVT[:, (oc * 8 + fc) * 128:(oc * 8 + fc + 1) * 128],
                            XNT[:, fc * 128:(fc + 1) * 128],
                            start=(oc % 4 == 0 and fc == 0),
                            stop=(oc % 4 == 3 and fc == NF - 1))
                st.last(ins)

            @self.block("vector", 1, waits=[("T", sC.total)], tag=f"l0aCv{c}")
            def sCv(v, st):
                for p in range(NP):
                    v.tensor_scalar(QBUF[:, p * 128:(p + 1) * 128],
                                    PA[:, p * 128:(p + 1) * 128],
                                    QB[:, p:p + 1], None, ALU.add)
                for p in range(NP):
                    ins = v.tensor_scalar(KBUF[:, p * 128:(p + 1) * 128],
                                          PA[:, (8 + p) * 128:(9 + p) * 128],
                                          0.125, KB8[:, p:p + 1], ALU.mult,
                                          ALU.add)
                st.last(ins)

            if CUT < 6:
                continue
            @self.block("tensor", 1, waits=[("V", sCv.total)], tag=f"l0aD{c}")
            def sD(t, st):
                for voc in range(8):
                    for fc in range(NF):
                        t.matmul(PB[:, voc * 128:(voc + 1) * 128],
                                 XNT[:, fc * 128:(fc + 1) * 128],
                                 WQKVT[:, ((16 + voc) * 8 + fc) * 128:
                                       ((16 + voc) * 8 + fc + 1) * 128],
                                 start=(voc % 4 == 0 and fc == 0), stop=False)
                for voc in range(8):
                    ins = t.matmul(PB[:, voc * 128:(voc + 1) * 128],
                                   ONESROW[:, :],
                                   VBROW[:, voc * 128:(voc + 1) * 128],
                                   start=False, stop=(voc % 4 == 3))
                st.last(ins)

            @self.block("vector", 1, waits=[("T", sD.total)], tag=f"l0aDv{c}")
            def sDv(v, st):
                st.last(v.tensor_copy(VBUF[:, :], PB[:, :]))

            if CUT < 7:
                continue
            @self.block("tensor", 1, waits=[("V", sCv.total)], tag=f"l0aE{c}")
            def sE(t, st):
                for p in range(NP):
                    t.matmul(PA[:, p * 128:(p + 1) * 128],
                             KBUF[0:64, p * 128:(p + 1) * 128],
                             QBUF[0:64, p * 128:(p + 1) * 128],
                             start=(p % 4 == 0), stop=(p % 4 == 3),
                             tile_position=(0, 0))
                    ins = t.matmul(PA[:, 1024 + p * 128:1024 + (p + 1) * 128],
                                   KBUF[64:128, p * 128:(p + 1) * 128],
                                   QBUF[64:128, p * 128:(p + 1) * 128],
                                   start=(p % 4 == 0), stop=(p % 4 == 3),
                                   tile_position=(64, 0))
                st.last(ins)

            @self.block("vector", 2, waits=[("T", sE.total)], tag=f"l0aEv{c}")
            def sEv(v, st):
                st(v.tensor_tensor(SMT[:, :], PA[:, :], DECAY[:, :], ALU.mult))
                st.last(v.tensor_tensor(QT[:, :], QBUF[:, :], GAMT[:, :],
                                        ALU.mult))

            @self.block("tensor", 1, waits=[("V", sBv.total)], tag=f"l0aF{c}")
            def sF(t, st):
                for h in range(H):
                    rh = (h % 2) * 64
                    p = h // 2
                    kc = 64 * p + 1024 * (h % 2)
                    ins = t.transpose(PD[:, kc:kc + 64],
                                      KBUF[rh:rh + 64, p * 128:(p + 1) * 128],
                                      IDENT[rh:rh + 64, rh:rh + 64])
                st.last(ins)

            @self.block("vector", 1, waits=[("T", sF.total)], tag=f"l0aFv{c}")
            def sFv(v, st):
                pdv = PD[:, :].rearrange("p (b q) -> p b q", q=1024)[:, :, 0:512]
                ktv = KTB[:, :].rearrange("p (b q) -> p b q", q=512)
                gkv = GKT[:, :].rearrange("p (b q) -> p b q", q=512)
                st.last(v.tensor_tensor(ktv, pdv, gkv, ALU.mult))

            if CUT < 8:
                continue
            @self.block("tensor", 2,
                        waits=[("V", max(sEv.total, sFv.total, sDv.total))],
                        tag=f"l0aG{c}")
            def sG(t, st):
                for p in range(NP):
                    t.matmul(PA[0:64, p * 128:(p + 1) * 128],
                             VBUF[:, (2 * p) * 64:(2 * p + 1) * 64],
                             SMT[:, p * 128:(p + 1) * 128],
                             start=(p % 4 == 0), stop=False,
                             tile_position=(0, 0), skip_group_check=True)
                    t.matmul(PA[64:128, 1024 + p * 128:1024 + (p + 1) * 128],
                             VBUF[:, (2 * p + 1) * 64:(2 * p + 2) * 64],
                             SMT[:, 1024 + p * 128:1024 + (p + 1) * 128],
                             start=(p % 4 == 0), stop=False,
                             tile_position=(0, 64), skip_group_check=True)
                    t.matmul(PA[0:64, p * 128:(p + 1) * 128],
                             RBUF[0:64, p * 64:(p + 1) * 64],
                             QT[0:64, p * 128:(p + 1) * 128],
                             start=False, stop=(p % 4 == 3),
                             tile_position=(0, 0), skip_group_check=True)
                    ins = t.matmul(PA[64:128, 1024 + p * 128:1024 + (p + 1) * 128],
                                   RBUF[64:128, p * 64:(p + 1) * 64],
                                   QT[64:128, p * 128:(p + 1) * 128],
                                   start=False, stop=(p % 4 == 3),
                                   tile_position=(64, 64), skip_group_check=True)
                st(ins)
                for p in range(NP):
                    t.matmul(PB[0:64, p * 64:(p + 1) * 64],
                             GIDT[0:64, p * 64:(p + 1) * 64],
                             RBUF[0:64, p * 64:(p + 1) * 64],
                             start=(p == 0), stop=False, tile_position=(0, 0),
                             skip_group_check=True)
                    t.matmul(PB[64:128, 512 + p * 64:512 + (p + 1) * 64],
                             GIDT[64:128, p * 64:(p + 1) * 64],
                             RBUF[64:128, p * 64:(p + 1) * 64],
                             start=(p == 0), stop=False, tile_position=(64, 64),
                             skip_group_check=True)
                    t.matmul(PB[0:64, p * 64:(p + 1) * 64],
                             KTB[:, p * 64:(p + 1) * 64],
                             VBUF[:, (2 * p) * 64:(2 * p + 1) * 64],
                             start=False, stop=(p == 7), tile_position=(0, 0),
                             skip_group_check=True)
                    ins = t.matmul(PB[64:128, 512 + p * 64:512 + (p + 1) * 64],
                                   KTB[:, 512 + p * 64:512 + (p + 1) * 64],
                                   VBUF[:, (2 * p + 1) * 64:(2 * p + 2) * 64],
                                   start=False, stop=(p == 7),
                                   tile_position=(0, 64), skip_group_check=True)
                st.last(ins)

            @self.block("vector", 4, waits=[("T", sG.total)], tag=f"l0aGv{c}")
            def sGv(v, st):
                st(v.tensor_copy(YBUF[0:64, :], PA[0:64, 0:1024]))
                st(v.tensor_copy(YBUF[64:128, :], PA[64:128, 1024:2048]))
                st(v.tensor_copy(RBUF[0:64, :], PB[0:64, 0:512]))
                st.last(v.tensor_copy(RBUF[64:128, :], PB[64:128, 512:1024]))

            if CUT < 9:
                continue
            @self.block("tensor", 1, waits=[("V", sGv.total)], tag=f"l0aH{c}")
            def sH(t, st):
                for ooc in range(8):
                    for fc in range(NF):
                        t.matmul(PA[:, ooc * 128:(ooc + 1) * 128],
                                 YBUF[:, fc * 128:(fc + 1) * 128],
                                 WOT[:, (ooc * 8 + fc) * 128:
                                     (ooc * 8 + fc + 1) * 128],
                                 start=(ooc % 4 == 0 and fc == 0), stop=False)
                for ooc in range(8):
                    ins = t.matmul(PA[:, ooc * 128:(ooc + 1) * 128],
                                   ONESROW[:, :],
                                   OBROW[:, ooc * 128:(ooc + 1) * 128],
                                   start=False, stop=(ooc % 4 == 3))
                st.last(ins)

            self.postln_residual(c, PA, D, sH.total, f"l0aI{c}")

        self.barrier("l0a")
        self.pop_scope()

    def prep_ternary(self, dram_name, nrows, ncols, dst, pref, mwb, mwi, tag,
                     fcmajor=False):
        noc = nrows // 128
        nfc = ncols // 128
        dbuf = 2 if ncols <= 1024 else 1
        stg = self.sbuf(f"stgA_{dram_name}", [128, dbuf * ncols], F32)
        stgq = self.sbuf(f"stgQ_{dram_name}", [128, ncols], BF16)
        stgi = self.sbuf(f"stgI_{dram_name}", [128, ncols], mybir.dt.int8)
        SQJ = self.sb["SQJ"]
        CO = self.sb["COLS"]
        pab, acc = CO[:, 20:21], CO[:, 21:22]
        w = self.dram[dram_name].ap()
        PD, PB = self.ps["PD"], self.ps["PB"]

        @self.block("vector", 1, tag=tag + "z")
        def vz(v, st):
            st.last(v.memset(acc, 0.0))

        lastA = None
        for oc in range(noc):
            buf = oc % dbuf
            semn = f"{pref}{buf}"
            self.sem(semn)
            if not hasattr(self, "dma_load_sems"):
                self.dma_load_sems = set()
            prev_waits = ([(s, self.semc[s]) for s in self.dma_load_sems
                           if s != semn and self.semc[s] > 0]
                          if oc < dbuf else [])
            self.dma_load_sems.add(semn)
            self.semc[semn] += 16
            lw = self.semc[semn]

            @self.block("gpsimd", 0,
                        waits=[("S", self.val("S")),
                               ("V", self.val("V") if oc < dbuf else 0),
                               ("T", self.val("T") if oc < dbuf else 0)] + prev_waits,
                        tag=f"{tag}A{oc}g")
            def _(g, st, oc=oc, buf=buf, semn=semn):
                g.dma_start(out=stg[:, buf * ncols:(buf + 1) * ncols],
                            in_=w[oc * 128:(oc + 1) * 128, :]
                            ).then_inc(self.semh[semn], 16)

            @self.block("scalar", 1,
                        waits=[(semn, lw), ("S", self.val("S")),
                               ("V", self.val("V"))],
                        tag=f"{tag}A{oc}s")
            def sA(s, st, oc=oc, buf=buf):
                st.last(s.activation(SQJ[:, 0:ncols],
                                     stg[:, buf * ncols:(buf + 1) * ncols],
                                     AF.Abs, accum_out=pab))

            @self.block("vector", 1,
                        waits=[("S", sA.total), ("V", self.val("V"))],
                        tag=f"{tag}A{oc}v")
            def vA(v, st):
                st.last(v.tensor_add(acc, acc, pab))
            lastA = vA

        @self.block("vector", 1, waits=[("V", lastA.total)], tag=tag + "c")
        def vc(v, st):
            st.last(v.tensor_copy(CO[:, 22:23], acc))

        @self.block("tensor", 1, waits=[("V", vc.total)], tag=tag + "mm")
        def tmm(t, st):
            st.last(t.matmul(PB[0:1, 0:1], CO[:, 22:23], self.sb["ONESFC"][:, :],
                             start=True, stop=True))

        @self.block("vector", 1, waits=[("T", tmm.total)], tag=tag + "mv")
        def vmv(v, st):
            st.last(v.tensor_scalar(CO[0:1, 23:24], PB[0:1, 0:1],
                                    1.0 / (nrows * ncols), 1e-5, ALU.mult,
                                    ALU.max))

        @self.block("tensor", 1, waits=[("V", vmv.total)], tag=tag + "mb")
        def tmb(t, st):
            st.last(t.matmul(PB[:, 4:5], self.sb["ONESFR"][:, :],
                             CO[0:1, 23:24], start=True, stop=True))

        @self.block("vector", 2, waits=[("T", tmb.total)], tag=tag + "mw")
        def vmw(v, st):
            st(v.tensor_copy(mwb[:, :], PB[:, 4:5]))
            st.last(v.reciprocal(mwi[:, :], mwb[:, :]))

        nrounds = (nfc + 15) // 16
        for oc in range(noc):
            buf = oc % dbuf
            semn = f"{pref}{buf}"
            self.sem(semn)
            self.semc[semn] += 16
            lw = self.semc[semn]

            @self.block("gpsimd", 0,
                        waits=[("V", max(vmw.total, self.val("V") if oc >= dbuf else 0)),
                               ("S", self.val("S"))],
                        tag=f"{tag}B{oc}g")
            def _(g, st, oc=oc, buf=buf, semn=semn):
                g.dma_start(out=stg[:, buf * ncols:(buf + 1) * ncols],
                            in_=w[oc * 128:(oc + 1) * 128, :]
                            ).then_inc(self.semh[semn], 16)

            @self.block("scalar", 1,
                        waits=[(semn, lw), ("S", self.val("S")),
                               ("V", self.val("V"))],
                        tag=f"{tag}B{oc}sg")
            def sB(s, st, oc=oc, buf=buf):
                st.last(s.activation(SQJ[:, 0:ncols],
                                     stg[:, buf * ncols:(buf + 1) * ncols],
                                     AF.Sign))

            @self.block("vector", 5,
                        waits=[(semn, lw), ("V", max(vmw.total, self.val("V"))),
                               ("T", self.val("T")), ("S", sB.total)],
                        tag=f"{tag}B{oc}v")
            def vB(v, st, oc=oc, buf=buf):
                sl = stg[:, buf * ncols:(buf + 1) * ncols]
                sgn = SQJ[:, 0:ncols]
                st(v.tensor_scalar(sl, sl, mwi[:, :], 1.0, ALU.mult, ALU.min))
                st(v.tensor_scalar(sl, sl, -1.0, None, ALU.max))
                st(v.tensor_scalar(sgn, sgn, self.sb["CFH"][:, :], None,
                                   ALU.mult))
                st(v.scalar_tensor_tensor(stgi[:, :], sl, 0.0, sgn, ALU.add,
                                          ALU.add))
                st.last(v.tensor_copy(stgq[:, :], stgi[:, :]))

            for r in range(nrounds):
                f0 = 16 * r
                f1 = min(nfc, f0 + 16)

                @self.block("tensor", 1,
                            waits=[("V", vB.total if r == 0 else self.val("V"))],
                            tag=f"{tag}B{oc}t{r}")
                def tB(t, st, oc=oc, f0=f0, f1=f1):
                    for fc in range(f0, f1):
                        ins = t.transpose(PD[:, (fc - f0) * 128:(fc - f0 + 1) * 128],
                                          stgq[:, fc * 128:(fc + 1) * 128],
                                          self.sb["IDENT"][:, :])
                    st.last(ins)

                @self.block("vector", 1, waits=[("T", tB.total)],
                            tag=f"{tag}B{oc}e{r}")
                def vE(v, st, oc=oc, f0=f0, f1=f1):
                    if fcmajor:
                        dstv = dst[:, :].rearrange("p (f o) -> p f o",
                                                   o=noc * 128)
                        st.last(v.tensor_copy(
                            dstv[:, f0:f1, oc * 128:(oc + 1) * 128],
                            PD[:, 0:(f1 - f0) * 128]
                            .rearrange("p (f o) -> p f o", o=128)))
                    else:
                        st.last(v.tensor_copy(
                            dst[:, (oc * nfc + f0) * 128:(oc * nfc + f1) * 128],
                            PD[:, 0:(f1 - f0) * 128]))

    def phase_l0f(self):
        NC2 = 4 * D
        XS, CO, ST = self.sb["XS"], self.sb["COLS"], self.sb["STAT"]
        IDENT, ONESROW = self.sb["IDENT"], self.sb["ONESROW"]
        PA, PB, PD = (self.ps[k] for k in ("PA", "PB", "PD"))
        TMP = self.sb["TMPF"]
        qi2d = self.nc.dram_tensor("qi2_scratch", [S, NC2], mybir.dt.int8)
        import os
        DBG = int(os.environ.get("MK_DBG", "0"))
        if DBG:
            self.dout("dbg_h", [128, NC2])
            self.dout("dbg_t1", [128, 128])
            self.dout("dbg_mw", [1, 4])
            self.dout("dbg_qi", [128, D])
            self.dout("dbg_q2", [128, NC2])
        SCL2 = self.sbuf("SCL2", [128, NT], F32)
        MW2B = self.sbuf("MW2B", [128, 1], F32)
        MW2I = self.sbuf("MW2I", [128, 1], F32)
        _oldsqj = self.sb["SQJ"]
        self.sbuf("SQJA", [128, 4 * D], BF16)
        self.sb["SQJ"] = self.sb["SQJA"]
        SQJP = self.sb["SQJA"]

        self.push_scope()
        TERN1T = self.sbuf("TERN1T", [128, 32 * 8 * 128], BF16)
        MW1B = self.sbuf("MW1B", [128, 1], F32)
        MW1I = self.sbuf("MW1I", [128, 1], F32)
        B1ROW = self.sbuf("B1ROW", [1, NC2], BF16)
        QI = self.sbuf("QI", [128, D], BF16)
        QIT = self.sbuf("QIT", [128, D], BF16)
        SGN = self.sbuf("SGN", [128, NC2], BF16)
        HBUF = self.sbuf("HBUF", [128, NC2], BF16)
        T4K = self.sbuf("T4K", [128, NC2], F32)
        QI2C = self.sbuf("QI2C", [128, NC2], mybir.dt.int8)
        RSCB = self.sbuf("RSCB", [128, 1], BF16)
        RSROW = self.sbuf("RSROW", [1, 128], BF16)

        self.sem("LG1")
        self.semc["LG1"] += 16
        lg1 = self.semc["LG1"]

        @self.block("gpsimd", 0, tag="l0f_b1load")
        def _(g, st):
            g.dma_start(out=B1ROW[:, :], in_=self.dram["l0f_b1"].ap()
                        ).then_inc(self.semh["LG1"], 16)

        self.prep_ternary("l0f_w1", 4 * D, D, TERN1T, "LW1", MW1B, MW1I, "w1p",
                          fcmajor=True)
        QI32 = self.sb["stgI_l0f_w1"]


        c_idx = {"su": 0, "sq": 1, "mean": 2, "sqd": 3, "m2": 4, "var": 5,
                 "veps": 6, "rcp1": 7, "istd": 13, "rv": 14, "rcp2": 15,
                 "r": 16, "d1": 17, "d2": 18, "mm": 19, "mx": 24, "mn": 25,
                 "t3": 26, "scale": 27, "alpha": 28, "beta": 29, "scol": 30,
                 "rscol": 31}

        def CC(n):
            i = c_idx[n]
            return CO[:, i:i + 1]

        gelu_hist = []
        fssg_prev = 0
        for c in range(NT):
            @self.block("vector", 3, waits=[("S", fssg_prev)],
                        tag=f"l0f1s{c}")
            def fv1(v, st, c=c):
                xt = XS[:, c * D:(c + 1) * D]
                st(v.reduce_sum(CC("su"), xt, axis=AX.X))
                st(v.reduce_max(CC("mx"), xt, axis=AX.X))
                st.last(v.tensor_reduce(CC("mn"), xt, axis=AX.X, op=ALU.min))

            @self.block("scalar", 1, waits=[("V", fv1.total)], tag=f"l0f1q{c}")
            def fs1(s, st, c=c):
                st.last(s.activation(SQJP[:, 0:D],
                                     XS[:, c * D:(c + 1) * D], AF.Square,
                                     accum_out=CC("sq")))

            @self.block("vector", 8, waits=[("S", fs1.total)], tag=f"l0f1c{c}")
            def fv2(v, st):
                st(v.tensor_scalar(CC("mean"), CC("su"), 1.0 / D, None, ALU.mult))
                st(v.tensor_scalar(CC("sqd"), CC("sq"), 1.0 / D, None, ALU.mult))
                st(v.tensor_tensor(CC("m2"), CC("mean"), CC("mean"), ALU.mult))
                st(v.tensor_tensor(CC("var"), CC("sqd"), CC("m2"), ALU.subtract))
                st(v.tensor_scalar(CC("veps"), CC("var"), EPS, None, ALU.add))
                st(v.reciprocal(CC("rcp1"), CC("veps")))
                st(v.tensor_tensor(CC("rv"), CC("var"), CC("rcp1"), ALU.mult))
                st.last(v.tensor_scalar(CC("rv"), CC("rv"), EPS, None, ALU.add))

            @self.block("scalar", 2, waits=[("V", fv2.total)], tag=f"l0f1r{c}")
            def fs2(s, st):
                st(s.activation(CC("istd"), CC("rcp1"), AF.Sqrt))
                st.last(s.activation(CC("rcp2"), CC("rv"), AF.Sqrt))

            @self.block("vector", 10, waits=[("S", fs2.total)], tag=f"l0f1d{c}")
            def fv3(v, st):
                st(v.reciprocal(CC("r"), CC("rcp2")))
                st(v.tensor_tensor(CC("d1"), CC("mx"), CC("mean"), ALU.subtract))
                st(v.tensor_tensor(CC("d2"), CC("mean"), CC("mn"), ALU.subtract))
                st(v.tensor_tensor(CC("mm"), CC("d1"), CC("d2"), ALU.max))
                st(v.tensor_tensor(CC("t3"), CC("mm"), CC("istd"), ALU.mult))
                st(v.tensor_tensor(CC("t3"), CC("t3"), CC("r"), ALU.mult))
                st(v.tensor_scalar(CC("t3"), CC("t3"), 1e-5, None, ALU.max))
                st(v.reciprocal(CC("scale"), CC("t3")))
                st(v.tensor_scalar(CC("scale"), CC("scale"), 127.0, None,
                                   ALU.mult))
                st.last(v.tensor_tensor(CC("alpha"), CC("istd"), CC("r"),
                                        ALU.mult))

            @self.block("vector", 1, waits=[("V", fv3.total)], tag=f"l0f1nm{c}")
            def fvnm(v, st):
                st.last(v.tensor_scalar(CC("d1"), CC("mean"), -1.0, None,
                                        ALU.mult))

            @self.block("scalar", 1, waits=[("V", fvnm.total)], tag=f"l0f1sg{c}")
            def fssg(s, st, c=c):
                st.last(s.activation(SGN[:, 0:D], XS[:, c * D:(c + 1) * D],
                                     AF.Sign, bias=CC("d1")))
            fssg_prev = fssg.total

            @self.block("vector", 7,
                        waits=[("S", fssg.total),
                               ("SDB", 64 if DBG and c == 1 else 0)],
                        tag=f"l0f1e{c}")
            def fv4(v, st, c=c):
                st(v.tensor_tensor(CC("alpha"), CC("alpha"), CC("scale"),
                                   ALU.mult))
                st(v.scalar_tensor_tensor(CC("beta"), CC("mean"), -1.0,
                                          CC("alpha"), ALU.mult, ALU.mult))
                st(v.tensor_scalar(TMP[:, :], XS[:, c * D:(c + 1) * D],
                                   CC("alpha"), CC("beta"), ALU.mult, ALU.add))
                st(v.tensor_scalar(SGN[:, 0:D], SGN[:, 0:D],
                                   self.sb["CFH"][:, :], None, ALU.mult))
                st(v.scalar_tensor_tensor(QI32[:, :], TMP[:, :], 0.0,
                                          SGN[:, 0:D], ALU.add, ALU.add))
                st(v.tensor_copy(QI[:, :], QI32[:, :]))
                st.last(v.scalar_tensor_tensor(CC("rscol"), CC("scale"), 1.0,
                                               MW1I[:, :], ALU.mult, ALU.mult))

            @self.block("tensor", 1, waits=[("V", fv4.total)], tag=f"l0f1t{c}")
            def ft1(t, st):
                for j in range(NF):
                    ins = t.transpose(PD[:, j * 128:(j + 1) * 128],
                                      QI[:, j * 128:(j + 1) * 128], IDENT[:, :])
                st.last(ins)

            @self.block("vector", 2, waits=[("T", ft1.total)], tag=f"l0f1u{c}")
            def fv5(v, st):
                st(v.tensor_copy(QIT[:, :], PD[:, 0:1024]))
                st.last(v.tensor_copy(RSCB[:, :], CC("rscol")))

            @self.block("tensor", 1, waits=[("V", fv5.total)], tag=f"l0f1w{c}")
            def ft2(t, st):
                st.last(t.matmul(PB[0:1, 8:136], RSCB[:, :], IDENT[:, :],
                                 start=True, stop=True))

            @self.block("vector", 1, waits=[("T", ft2.total)], tag=f"l0f1x{c}")
            def fv6(v, st):
                st.last(v.tensor_copy(RSROW[:, :], PB[0:1, 8:136]))

            @self.block("vector", 2,
                        waits=[("S", gelu_hist[-1] if gelu_hist else 0)],
                        tag=f"l0f1y{c}")
            def fv7(v, st):
                st(v.reciprocal(CC("t3"), CC("scale")))
                st.last(v.tensor_tensor(CC("scol"), MW1B[:, :], CC("t3"), ALU.mult))

            for og in range(8):
                bank = og % 4

                @self.block("tensor", 1,
                            waits=[("V", fv7.total if og == 0 else 0),
                                   ("LG1", lg1 if og == 0 and c == 0 else 0),
                                   ("S", gelu_hist[-4] if len(gelu_hist) >= 4
                                    else 0),
                                   ("SDB", 64 if DBG and c == 1 and og == 0
                                    else 0)],
                            tag=f"l0f1m{c}g{og}")
                def ft3(t, st, og=og, bank=bank):
                    for fc in range(NF):
                        t.matmul(PA[:, bank * 512:(bank + 1) * 512],
                                 QIT[:, fc * 128:(fc + 1) * 128],
                                 TERN1T[:, (fc * 32 + og * 4) * 128:
                                        (fc * 32 + og * 4 + 4) * 128],
                                 start=(fc == 0), stop=False)
                    st.last(t.matmul(PA[:, bank * 512:(bank + 1) * 512],
                                     RSROW[:, :], B1ROW[:, og * 512:(og + 1) * 512],
                                     start=False, stop=True))

                @self.block("scalar", 1, waits=[("T", ft3.total)],
                            tag=f"l0f1g{c}o{og}")
                def fs3(s, st, og=og, bank=bank):
                    st.last(s.activation(
                        HBUF[:, og * 512:(og + 1) * 512],
                        PA[:, bank * 512:(bank + 1) * 512], AF.Gelu,
                        scale=CC("scol")))
                gelu_hist.append(fs3.total)

            @self.block("scalar", 1, waits=[("S", self.val("S"))],
                        tag=f"l0f1h{c}")
            def fs4(s, st):
                st.last(s.activation(SQJP[:, 0:NC2], HBUF[:, :],
                                     AF.Square, accum_out=CC("sq")))

            @self.block("vector", 3, waits=[("S", fs4.total)], tag=f"l0f1i{c}")
            def fv8(v, st):
                st(v.reduce_max(CC("mx"), HBUF[:, :], axis=AX.X,
                                apply_absolute_value=True))
                st(v.tensor_scalar(CC("rv"), CC("sq"), 1.0 / NC2, EPS,
                                   ALU.mult, ALU.add))
                st.last(v.reciprocal(CC("rcp2"), CC("rv")))

            @self.block("scalar", 1, waits=[("V", fv8.total)], tag=f"l0f1j{c}")
            def fs5(s, st):
                st.last(s.activation(CC("rcp2"), CC("rcp2"), AF.Sqrt))

            @self.block("scalar", 1, waits=[("S", fs5.total)], tag=f"l0f1sg2{c}")
            def fssg2(s, st):
                st.last(s.activation(SGN[:, 0:NC2], HBUF[:, :], AF.Sign))

            @self.block("vector", 7, waits=[("S", fssg2.total)], tag=f"l0f1k{c}")
            def fv9(v, st, c=c):
                st(v.tensor_tensor(CC("t3"), CC("mx"), CC("rcp2"), ALU.mult))
                st(v.tensor_scalar(CC("t3"), CC("t3"), 1e-5, None, ALU.max))
                st(v.reciprocal(CC("scale"), CC("t3")))
                st(v.tensor_scalar(CC("scale"), CC("scale"), 127.0, None,
                                   ALU.mult))
                st(v.tensor_copy(SCL2[:, c:c + 1], CC("scale")))
                st(v.tensor_tensor(CC("alpha"), CC("rcp2"), CC("scale"),
                                   ALU.mult))
                st.last(v.tensor_scalar(T4K[:, :], HBUF[:, :], CC("alpha"),
                                        None, ALU.mult))

            @self.block("vector", 2,
                        waits=[("V", fv9.total), ("SQ2", self.val("SQ2")),
                               ("SDB", 64 if DBG and c == 1 else 0)],
                        tag=f"l0f1l{c}")
            def fv10(v, st):
                st(v.tensor_scalar(SGN[:, 0:NC2], SGN[:, 0:NC2],
                                   self.sb["CFH"][:, :], None, ALU.mult))
                st.last(v.scalar_tensor_tensor(QI2C[:, :], T4K[:, :], 0.0,
                                               SGN[:, 0:NC2], ALU.add, ALU.add))

            self.sem("SQ2")
            self.semc["SQ2"] += 16
            sq2t = self.semc["SQ2"]

            @self.block("sync", 0, waits=[("V", fv10.total)], tag=f"l0f1z{c}")
            def _(e, st, c=c):
                e.dma_start(out=qi2d.ap()[c * 128:(c + 1) * 128, :],
                            in_=QI2C[:, :]).then_inc(self.semh["SQ2"], 16)

            if DBG and c == 0:
                self.sem("SDB")
                self.sem("SDC")
                self.semc["SDB"] += 64
                self.semc["SDC"] += 16

                @self.block("gpsimd", 0, waits=[("V", fv10.total)], tag="dbg0")
                def _(g, st):
                    g.dma_start(out=self.dram["dbg_h"].ap(), in_=HBUF[:, :]
                                ).then_inc(self.semh["SDB"], 16)
                    g.dma_start(out=self.dram["dbg_t1"].ap(),
                                in_=TERN1T[:, 0:128]).then_inc(self.semh["SDB"], 16)
                    g.dma_start(out=self.dram["dbg_qi"].ap(), in_=QI[:, :]
                                ).then_inc(self.semh["SDB"], 16)
                    g.dma_start(out=self.dram["dbg_q2"].ap(),
                                in_=QI2C[:, :]).then_inc(self.semh["SDB"], 16)

                @self.block("vector", 1, waits=[], tag="dbgmw")
                def dmw(v, st):
                    st.last(v.tensor_copy(self.sb["COLS"][0:1, 26:27],
                                          MW1B[0:1, 0:1]))

                @self.block("sync", 0, waits=[("V", dmw.total)], tag="dbg1")
                def _(e, st):
                    e.dma_start(out=self.dram["dbg_mw"].ap()[:, 0:1],
                                in_=MW1B[0:1, 0:1]).then_inc(self.semh["SDC"], 16)

                @self.block("gpsimd", 0, waits=[("SDB", 64)], tag="dbg2")
                def _(g, st):
                    pass

        self.barrier("l0f1", extra=[("SQ2", self.val("SQ2"))] +
                     ([("SDB", self.val("SDB")), ("SDC", self.val("SDC"))]
                      if DBG else []))
        self.pop_scope()

        self.push_scope()
        TERN2T = self.sbuf("TERN2T", [128, 8 * 32 * 128], BF16)
        self.sbuf("SQJB", [128, 4 * D], BF16)
        self.sb["SQJ"] = self.sb["SQJB"]
        B2ROW = self.sbuf("B2ROW", [1, D], BF16)
        QB2 = self.sbuf("QB2", [128, NC2], BF16)
        QIT2 = self.sbuf("QIT2", [128, NC2], BF16)
        QI2L = self.sbuf("QI2L", [128, NC2], mybir.dt.int8)
        RSCB2 = self.sbuf("RSCB2", [128, 1], BF16)
        RSROW2 = self.sbuf("RSROW2", [1, 128], BF16)

        self.sem("LG2")
        self.semc["LG2"] += 16
        lg2 = self.semc["LG2"]

        @self.block("gpsimd", 0, tag="l0f_b2load")
        def _(g, st):
            g.dma_start(out=B2ROW[:, :], in_=self.dram["l0f_b2"].ap()
                        ).then_inc(self.semh["LG2"], 16)

        self.prep_ternary("l0f_w2", D, NC2, TERN2T, "LW2", MW2B, MW2I, "w2p",
                          fcmajor=True)

        gv1_prev = 0
        gt1_prev = 0
        for c in range(NT):
            self.sem("LQ2")
            self.semc["LQ2"] += 16
            lq2 = self.semc["LQ2"]

            @self.block("gpsimd", 0, waits=[("V", gv1_prev),
                                            ("SQ2", 16 * (c + 1))],
                        tag=f"l0f2l{c}")
            def _(g, st, c=c):
                g.dma_start(out=QI2L[:, :],
                            in_=qi2d.ap()[c * 128:(c + 1) * 128, :]
                            ).then_inc(self.semh["LQ2"], 16)

            @self.block("vector", 4, waits=[("LQ2", lq2), ("SQ2", self.val("SQ2")),
                                            ("T", gt1_prev)],
                        tag=f"l0f2c{c}")
            def gv1(v, st, c=c):
                st(v.tensor_copy(QB2[:, :], QI2L[:, :]))
                st(v.reciprocal(CC("t3"), SCL2[:, c:c + 1]))
                st(v.tensor_tensor(CC("scol"), MW2B[:, :], CC("t3"), ALU.mult))
                st.last(v.scalar_tensor_tensor(CC("rscol"), SCL2[:, c:c + 1],
                                               1.0, MW2I[:, :], ALU.mult, ALU.mult))
            gv1_prev = gv1.total

            for r in range(2):
                @self.block("tensor", 1,
                            waits=[("V", gv1.total if r == 0 else self.val("V"))],
                            tag=f"l0f2t{c}r{r}")
                def gt1(t, st, r=r):
                    for j in range(16):
                        ins = t.transpose(PD[:, j * 128:(j + 1) * 128],
                                          QB2[:, (16 * r + j) * 128:
                                              (16 * r + j + 1) * 128],
                                          IDENT[:, :])
                    st.last(ins)

                gt1_prev = gt1.total

                @self.block("vector", 1, waits=[("T", gt1.total)],
                            tag=f"l0f2u{c}r{r}")
                def gv2(v, st, r=r):
                    st.last(v.tensor_copy(QIT2[:, r * 2048:(r + 1) * 2048],
                                          PD[:, 0:2048]))

            @self.block("vector", 1, waits=[("V", gv2.total)], tag=f"l0f2v{c}")
            def gv3(v, st):
                st.last(v.tensor_copy(RSCB2[:, :], CC("rscol")))

            @self.block("tensor", 1, waits=[("V", gv3.total)], tag=f"l0f2w{c}")
            def gt2(t, st):
                st.last(t.matmul(PB[0:1, 8:136], RSCB2[:, :], IDENT[:, :],
                                 start=True, stop=True))

            @self.block("vector", 1, waits=[("T", gt2.total)], tag=f"l0f2x{c}")
            def gv4(v, st):
                st.last(v.tensor_copy(RSROW2[:, :], PB[0:1, 8:136]))

            @self.block("tensor", 1, waits=[("V", gv4.total), ("LG2", lg2)],
                        tag=f"l0f2m{c}")
            def gt3(t, st):
                for og2 in range(2):
                    for fc2 in range(32):
                        t.matmul(PA[:, og2 * 512:(og2 + 1) * 512],
                                 QIT2[:, fc2 * 128:(fc2 + 1) * 128],
                                 TERN2T[:, (fc2 * 8 + og2 * 4) * 128:
                                        (fc2 * 8 + og2 * 4 + 4) * 128],
                                 start=(fc2 == 0), stop=False)
                    ins = t.matmul(PA[:, og2 * 512:(og2 + 1) * 512],
                                   RSROW2[:, :],
                                   B2ROW[:, og2 * 512:(og2 + 1) * 512],
                                   start=False, stop=True)
                st.last(ins)

            self.postln_residual(c, PA, D, gt3.total, f"l0f2I{c}",
                                 scale_col=CC("scol"))

        self.barrier("l0f2")
        self.pop_scope()

    def phase_l1a(self):
        self.sb["SQJ"] = self.sb.get("SQJ_global", self.sb["SQJ"])
        XS, CO, ST = self.sb["XS"], self.sb["COLS"], self.sb["STAT"]
        IDENT, ONESROW = self.sb["IDENT"], self.sb["ONESROW"]
        PA, PB, PD = (self.ps[k] for k in ("PA", "PB", "PD"))
        TMP = self.sb["TMPF"]

        self.push_scope()
        ABUF = self.sbuf("ABUF", [128, NF * 2048], BF16)
        XSC = self.sbuf("XSC", [128, NF * 2048], BF16)
        GB = self.sbuf("GB", [128, 24], F32)
        SGT = self.sbuf("SGT", [128, 2048], BF16)

        self.sem("LG3")
        self.semc["LG3"] += 48
        lg3 = self.semc["LG3"]

        @self.block("gpsimd", 0, tag="l1a_bload")
        def _(g, st):
            for i, nm in enumerate(("l1a_bg", "l1a_bv", "l1a_bd")):
                g.dma_start(out=GB[:, 8 * i:8 * (i + 1)],
                            in_=self.dram[nm].ap().rearrange("o (j p) -> p (o j)", p=128),
                            allow_slow_non_contiguous=True
                            ).then_inc(self.semh["LG3"], 16)

        sub1 = contextlib.ExitStack()
        old_es = self.cur_es
        self.cur_es = sub1
        XNT = self.sbuf("XNT1a", [128, NF * 2048], BF16)
        XN1 = self.sbuf("XN1a", [128, D], BF16)
        WOC = self.sbuf("WOC1a", [128, 8 * 1024], BF16)

        stats_done = self.bulk_ln_stats("XS", "V", self.val("V"), "l1a_st")

        def ISTD(c):
            return ST[:, 3 * NT + c:3 * NT + c + 1]

        def NM(c):
            return ST[:, 4 * NT + c:4 * NT + c + 1]

        for c in range(NT):
            @self.block("vector", 1,
                        waits=[("V", stats_done), ("T", self.val("T"))],
                        tag=f"l1aA{c}")
            def aA(v, st, c=c):
                st.last(v.tensor_scalar(XN1[:, :], XS[:, c * D:(c + 1) * D],
                                        ISTD(c), NM(c), ALU.mult, ALU.add))

            @self.block("tensor", 1, waits=[("V", aA.total)], tag=f"l1aB{c}")
            def aB(t, st, c=c):
                for j in range(NF):
                    ins = t.transpose(PD[:, j * 128:(j + 1) * 128],
                                      XN1[:, j * 128:(j + 1) * 128], IDENT[:, :])
                st.last(ins)

            @self.block("vector", 1, waits=[("T", aB.total)], tag=f"l1aBv{c}")
            def aBv(v, st, c=c):
                pdv = PD[:, 0:1024].rearrange("p (f q) -> p f q", q=128)
                xv = XNT[:, :].rearrange("p (f q) -> p f q", q=2048)[
                    :, :, c * 128:(c + 1) * 128]
                st.last(v.tensor_copy(xv, pdv))

        for mi, (wnm, bcol0, kind) in enumerate(
                (("l1a_wv", 8, "tanh"), ("l1a_wg", 0, "sigmul"),
                 ("l1a_wd", 16, "sigaff"))):
            self.prep_weight_ocmajor(wnm, D, WOC, f"LWg{mi}", stgname="stg_l1a")
            for oc in range(NF):
                @self.block("tensor", 1,
                            waits=[("V", self.val("V")), ("S", self.val("S")),
                                   ("LG3", lg3)],
                            tag=f"l1aP{mi}o{oc}")
                def aP(t, st, oc=oc):
                    for fc in range(NF):
                        for q in range(4):
                            ins = t.matmul(
                                PA[:, q * 512:(q + 1) * 512],
                                WOC[:, (oc * 8 + fc) * 128:(oc * 8 + fc + 1) * 128],
                                XNT[:, fc * 2048 + q * 512:fc * 2048 + (q + 1) * 512],
                                start=(fc == 0), stop=(fc == NF - 1))
                    st.last(ins)

                @self.block("scalar", 1, waits=[("T", aP.total)],
                            tag=f"l1aE{mi}o{oc}")
                def aE(s, st, oc=oc, mi=mi, bcol0=bcol0, kind=kind):
                    fn = AF.Tanh if kind == "tanh" else AF.Sigmoid
                    dstb = XSC if kind == "tanh" else (SGT if kind == "sigmul"
                                                       else ABUF)
                    dst = (dstb[:, oc * 2048:(oc + 1) * 2048] if kind != "sigmul"
                           else SGT[:, :])
                    st.last(s.activation(dst, PA[:, :], fn,
                                         bias=GB[:, bcol0 + oc:bcol0 + oc + 1]))

                if kind == "sigmul":
                    @self.block("vector", 1, waits=[("S", aE.total)],
                                tag=f"l1aM{mi}o{oc}")
                    def aM(v, st, oc=oc):
                        sl = XSC[:, oc * 2048:(oc + 1) * 2048]
                        st.last(v.tensor_tensor(sl, sl, SGT[:, :], ALU.mult))
                elif kind == "sigaff":
                    @self.block("vector", 1, waits=[("S", aE.total)],
                                tag=f"l1aM{mi}o{oc}")
                    def aM(v, st, oc=oc):
                        sl = ABUF[:, oc * 2048:(oc + 1) * 2048]
                        st.last(v.tensor_scalar(sl, sl, 0.998, 0.001, ALU.mult,
                                                ALU.add))

        self.barrier("l1a_sub1")
        sub1.close()
        self.cur_es = old_es

        sub2 = contextlib.ExitStack()
        self.cur_es = sub2
        HTM = self.sbuf("HTM", [128, NT * D], BF16)
        HFC = self.sbuf("HFC", [128, 2048], BF16)

        for fc in range(NF):
            @self.block("vector", 1, waits=[("T", self.val("T"))],
                        tag=f"l1aS{fc}")
            def aS(v, st, fc=fc):
                st.last(v.tensor_tensor_scan(HFC[:, :],
                                             ABUF[:, fc * 2048:(fc + 1) * 2048],
                                             XSC[:, fc * 2048:(fc + 1) * 2048],
                                             0.0, ALU.mult, ALU.add))

            @self.block("tensor", 1, waits=[("V", aS.total)], tag=f"l1aT{fc}")
            def aT(t, st, fc=fc):
                for tt_ in range(NT):
                    ins = t.transpose(PD[:, tt_ * 128:(tt_ + 1) * 128],
                                      HFC[:, tt_ * 128:(tt_ + 1) * 128],
                                      IDENT[:, :])
                st.last(ins)

            @self.block("vector", 1, waits=[("T", aT.total)], tag=f"l1aTv{fc}")
            def aTv(v, st, fc=fc):
                pdv = PD[:, 0:2048].rearrange("p (t q) -> p t q", q=128)
                hv = HTM[:, :].rearrange("p (t q) -> p t q", q=1024)[
                    :, :, fc * 128:(fc + 1) * 128]
                st.last(v.tensor_copy(hv, pdv))

        for c in range(NT):
            self.postln_residual(c, HTM[:, c * D:(c + 1) * D], D,
                                 self.val("T"), f"l1aI{c}", wV=self.val("V"))

        self.barrier("l1a")
        sub2.close()
        self.cur_es = old_es
        self.pop_scope()

    def phase_l1f(self):
        REG = 5
        XS, CO, ST = self.sb["XS"], self.sb["COLS"], self.sb["STAT"]
        IDENT, ONESROW = self.sb["IDENT"], self.sb["ONESROW"]
        ONESFR = self.sb["ONESFR"]
        PA, PB, PD = (self.ps[k] for k in ("PA", "PB", "PD"))
        TMP = self.sb["TMPF"]
        SQJ = self.sb["SQJ_global"]
        self.sb["SQJ"] = SQJ
        nc = self.nc
        qf_d = nc.dram_tensor("qf_scratch", [D, 2048], BF16)
        kf_d = nc.dram_tensor("kf_scratch", [D, 2048], BF16)
        oc_d = nc.dram_tensor("ocf_scratch", [D, 2048], BF16)
        wq_d = nc.dram_tensor("wq_scratch", [128, 16], BF16)
        cc_in = self.dram["cc_in"]
        cc_out = self.dram["cc_out"]
        wqkv = self.dram["l1f_wqkv"].ap()

        self.push_scope()
        VPM = self.sbuf("VPM", [128, 16 * 1040], BF16)
        VAM = self.sbuf("VAM", [128, 16 * 1040], BF16)
        SIMS = self.sbuf("SIMS", [128, NT], F32)
        PMK = self.sbuf("PMK", [128, NT], F32)
        WQCL = self.sbuf("WQCL", [128, NT], F32)
        RGW = self.sbuf("RGW", [128, NT], F32)
        MAXN2 = self.sbuf("MAXN2", [128, 16], F32)
        GHATB = self.sbuf("GHATB", [128, D], BF16)
        BCOLS = self.sbuf("BCOLS", [128, 16], F32)
        THRB = self.sbuf("THRB", [128, 1], F32)
        CCS = self.sbuf("CCS", [1, 4], F32)
        SCL = self.sbuf("SCL", [128, 16], F32)
        DTC = self.sbuf("DTC", [128, 16], F32)
        KMXC = self.sbuf("KMXC", [128, 8], F32)
        SCB = self.sbuf("SCB", [128, 16], BF16)
        CCB = self.sbuf("CCB", [2, 1], F32)
        SRW = self.sbuf("SRW", [16, 1], F32)
        SRB = self.sbuf("SRB", [16, 1], BF16)
        SRR = self.sbuf("SRR", [1, 16], F32)

        self.sem("LG4")
        self.semc["LG4"] += 16
        lg4 = self.semc["LG4"]

        @self.block("gpsimd", 0, waits=[("V", self.val("V"))], tag="l1f_gload")
        def _(g, st):
            g.dma_start(out=GHATB[:, :],
                        in_=self.dram["l1f_g"].ap().partition_broadcast(128),
                        allow_slow_non_contiguous=True
                        ).then_inc(self.semh["LG4"], 16)

        @self.block("vector", 3, waits=[("LG4", lg4), ("S", self.val("S")),
                                        ("T", self.val("T"))], tag="l1f_gn1")
        def gn1(v, st):
            st(v.tensor_tensor(TMP[:, :], GHATB[:, :], GHATB[:, :], ALU.mult))
            st(v.reduce_sum(SCL[:, :].rearrange("p (h o) -> p h o", o=1),
                            TMP[:, :].rearrange("p (h e) -> p h e", e=64),
                            axis=AX.X))
            st.last(v.reciprocal(SCL[:, :], SCL[:, :]))

        @self.block("scalar", 1, waits=[("V", gn1.total)], tag="l1f_gn2")
        def gn2(s, st):
            st.last(s.activation(SCL[:, :], SCL[:, :], AF.Sqrt))

        @self.block("vector", 2, waits=[("S", gn2.total)], tag="l1f_gn3")
        def gn3(v, st):
            gv = GHATB[:, :].rearrange("p (h e) -> p h e", e=64)
            sv = SCL[:, :].rearrange("p (h o) -> p h o", o=1)
            st(v.memset(MAXN2[:, :], 0.0))
            st.last(v.tensor_tensor(gv, gv, sv.broadcast_to([128, 16, 64]),
                                    ALU.mult))

        sub1 = contextlib.ExitStack()
        outer1 = self.cur_es
        self.cur_es = sub1
        XNT = self.sbuf("XNT1f", [128, NF * 2048], BF16)
        XN1 = self.sbuf("XN1f", [128, D], BF16)
        WPART = self.sbuf("WPART", [128, 8 * 1024], BF16)
        WQS = self.sbuf("WQS", [128, 2 * 1024], BF16)
        QKE = self.sbuf("QKE", [128, 2048], BF16)
        QTM = self.sbuf("QTM", [128, D], BF16)
        QSQ = TMP

        stats_done = self.bulk_ln_stats("XS", "V", self.val("V"), "l1f_st")

        def ISTD(c):
            return ST[:, 3 * NT + c:3 * NT + c + 1]

        def NM(c):
            return ST[:, 4 * NT + c:4 * NT + c + 1]

        for c in range(NT):
            @self.block("vector", 1,
                        waits=[("V", stats_done), ("T", self.val("T"))],
                        tag=f"l1fA{c}")
            def fA(v, st, c=c):
                st.last(v.tensor_scalar(XN1[:, :], XS[:, c * D:(c + 1) * D],
                                        ISTD(c), NM(c), ALU.mult, ALU.add))

            @self.block("tensor", 1, waits=[("V", fA.total)], tag=f"l1fB{c}")
            def fB(t, st, c=c):
                for j in range(NF):
                    ins = t.transpose(PD[:, j * 128:(j + 1) * 128],
                                      XN1[:, j * 128:(j + 1) * 128], IDENT[:, :])
                st.last(ins)

            @self.block("vector", 1, waits=[("T", fB.total)], tag=f"l1fBv{c}")
            def fBv(v, st, c=c):
                pdv = PD[:, 0:1024].rearrange("p (f q) -> p f q", q=128)
                xv = XNT[:, :].rearrange("p (f q) -> p f q", q=2048)[
                    :, :, c * 128:(c + 1) * 128]
                st.last(v.tensor_copy(xv, pdv))

        self.sem("LWF0")
        self.dma_load_sems.add("LWF0")
        self.semc["LWF0"] += 16
        lwf_q = self.semc["LWF0"]

        @self.block("gpsimd", 0,
                    waits=[("V", self.val("V")), ("S", self.val("S")),
                           ("T", self.val("T"))] +
                    [(s, self.semc[s]) for s in sorted(self.dma_load_sems)
                     if s != "LWF0" and self.semc[s] > 0],
                    tag="l1f_wqload")
        def _(g, st):
            g.dma_start(out=WPART[:, :].rearrange("p (f o) -> p f o", o=1024),
                        in_=wqkv[:, 0:1024].rearrange("(f p) o -> p f o", p=128)
                        ).then_inc(self.semh["LWF0"], 16)

        for c in range(NT):
            @self.block("tensor", 1,
                        waits=[("LWF0", lwf_q), ("V", self.val("V")),
                               ("S", self.val("S"))], tag=f"l1fq{c}")
            def sq1(t, st, c=c):
                for og in range(2):
                    for fc in range(NF):
                        ins = t.matmul(
                            PB[:, og * 512:(og + 1) * 512],
                            XNT[:, fc * 2048 + c * 128:fc * 2048 + (c + 1) * 128],
                            WPART[:, fc * 1024 + og * 512:fc * 1024 + (og + 1) * 512],
                            start=(fc == 0), stop=(fc == NF - 1))
                st.last(ins)

            @self.block("vector", 7, waits=[("T", sq1.total)], tag=f"l1fqv{c}")
            def sq2(v, st, c=c):
                st(v.tensor_copy(QTM[:, :], PB[:, 0:1024]))
                st(v.tensor_tensor(QSQ[:, :], QTM[:, :], QTM[:, :], ALU.mult))
                st(v.reduce_sum(SCL[:, :].rearrange("p (h o) -> p h o", o=1),
                                QSQ[:, :].rearrange("p (h e) -> p h e", e=64),
                                axis=AX.X))
                st(v.tensor_tensor(MAXN2[:, :], MAXN2[:, :], SCL[:, :], ALU.max))
                st(v.tensor_tensor(QSQ[:, :], QTM[:, :], GHATB[:, :], ALU.mult))
                st(v.reduce_sum(DTC[:, :].rearrange("p (h o) -> p h o", o=1),
                                QSQ[:, :].rearrange("p (h e) -> p h e", e=64),
                                axis=AX.X))
                st.last(v.reciprocal(SCL[:, :], SCL[:, :]))

            @self.block("scalar", 1, waits=[("V", sq2.total)], tag=f"l1fqs{c}")
            def sq3(s, st):
                st.last(s.activation(SCL[:, :], SCL[:, :], AF.Sqrt))

            @self.block("vector", 2, waits=[("S", sq3.total)], tag=f"l1fqx{c}")
            def sq4(v, st, c=c):
                st(v.tensor_tensor(DTC[:, :], DTC[:, :], SCL[:, :], ALU.mult))
                st.last(v.reduce_sum(SIMS[:, c:c + 1], DTC[:, :], axis=AX.X))

        @self.block("vector", 5, waits=[("V", self.val("V"))], tag="l1f_mm")
        def smm(v, st):
            st(v.tensor_copy(SIMS[0:REG, 0:1], SIMS[0:REG, 1:2]))
            st(v.reduce_max(SCL[:, 0:1], SIMS[:, :], axis=AX.X))
            st(v.tensor_reduce(SCL[:, 1:2], SIMS[:, :], axis=AX.X, op=ALU.min))
            st(v.tensor_scalar(SCL[:, 1:2], SCL[:, 1:2], -1.0, None, ALU.mult))
            st.last(v.tensor_copy(SCB[:, 0:2], SCL[:, 0:2]))

        @self.block("tensor", 1, waits=[("V", smm.total)], tag="l1f_mmt")
        def smt(t, st):
            st.last(t.transpose(PD[0:2, 0:128], SCB[:, 0:2], IDENT[:, :]))

        @self.block("vector", 1, waits=[("T", smt.total)], tag="l1f_mmv")
        def smv(v, st):
            st.last(v.reduce_max(CCB[:, :], PD[0:2, 0:128], axis=AX.X))

        self.sem("LCC")
        self.sem("GCC")
        self.semc["LCC"] += 32
        self.semc["GCC"] += 1
        lcc = self.semc["LCC"]
        gcc = self.semc["GCC"]

        @self.block("gpsimd", 0, waits=[("V", smv.total)], tag="l1f_cc")
        def _(g, st):
            g.dma_start(out=cc_in.ap(), in_=CCB[:, :]).then_inc(self.semh["LCC"], 16)
            g.wait_ge(self.semh["LCC"], lcc - 16)
            g.collective_compute(
                "AllReduce", ALU.max,
                replica_groups=[list(range(8))],
                ins=[cc_in.ap().opt()],
                outs=[cc_out.ap().opt()],
            ).then_inc(self.semh["GCC"], 1)
            g.wait_ge(self.semh["GCC"], gcc)
            g.dma_start(out=CCS[:, 0:2], in_=cc_out.ap()).then_inc(self.semh["LCC"], 16)

        @self.block("vector", 2, waits=[("LCC", lcc)], tag="l1f_thr")
        def thr(v, st):
            st(v.tensor_scalar(CCS[:, 2:3], CCS[:, 0:1], 0.9, None, ALU.mult))
            st.last(v.scalar_tensor_tensor(CCS[:, 3:4], CCS[:, 1:2], -0.1,
                                           CCS[:, 2:3], ALU.mult, ALU.add))

        @self.block("tensor", 1, waits=[("V", thr.total)], tag="l1f_thrb")
        def thb(t, st):
            st.last(t.matmul(PB[:, 0:1], ONESFR[:, :], CCS[0:1, 3:4],
                             start=True, stop=True))

        @self.block("vector", 7, waits=[("T", thb.total)], tag="l1f_msk")
        def msk(v, st):
            st(v.tensor_copy(THRB[:, :], PB[:, 0:1]))
            st(v.tensor_scalar(PMK[:, :], SIMS[:, :], THRB[:, :], None, ALU.is_gt))
            st(v.memset(PMK[0:REG, 0:1], 1.0))
            st(v.tensor_copy(WQCL[:, :], PMK[:, :]))
            st(v.memset(WQCL[0:REG, 0:1], 0.5))
            st(v.memset(RGW[:, :], 1.0))
            st.last(v.memset(RGW[0:REG, 0:1], 2.0))

        for oc in range(16):
            isq = oc < 8
            if not isq:
                semn = f"LWK{oc % 2}"
                self.sem(semn)
                self.dma_load_sems.add(semn)
                self.semc[semn] += 16
                lwk = self.semc[semn]

                @self.block("gpsimd", 0,
                            waits=[("V", self.val("V") if oc >= 10 else 0),
                                   ("S", self.val("S") if oc < 10 else 0)],
                            tag=f"l1fk{oc}g")
                def _(g, st, oc=oc, semn=semn):
                    ko = 1024 + (oc - 8) * 128
                    g.dma_start(out=WQS[:, (oc % 2) * 1024:(oc % 2) * 1024 + 1024]
                                .rearrange("p (f o) -> p f o", o=128),
                                in_=wqkv[:, ko:ko + 128]
                                .rearrange("(f p) o -> p f o", p=128)
                                ).then_inc(self.semh[semn], 16)

            @self.block("tensor", 1,
                        waits=([(f"LWK{oc % 2}", self.semc[f"LWK{oc % 2}"])]
                               if not isq else []) +
                        [("V", self.val("V")), ("S", self.val("S"))],
                        tag=f"l1fk{oc}t")
            def qk1(t, st, oc=oc, isq=isq):
                for fc in range(NF):
                    for q in range(4):
                        if isq:
                            lh = WPART[:, fc * 1024 + oc * 128:fc * 1024 + (oc + 1) * 128]
                        else:
                            lh = WQS[:, (oc % 2) * 1024 + fc * 128:
                                     (oc % 2) * 1024 + (fc + 1) * 128]
                        ins = t.matmul(
                            PA[:, q * 512:(q + 1) * 512], lh,
                            XNT[:, fc * 2048 + q * 512:fc * 2048 + (q + 1) * 512],
                            start=(fc == 0), stop=(fc == NF - 1))
                st.last(ins)

            @self.block("vector", 2 if not isq else 1,
                        waits=[("T", qk1.total), ("SQF", self.val("SQF"))],
                        tag=f"l1fk{oc}v")
            def qk2(v, st, oc=oc, isq=isq):
                if isq:
                    st.last(v.tensor_scalar(QKE[:, :], PA[:, :], 0.125, None,
                                            ALU.mult))
                else:
                    st(v.tensor_copy(QKE[:, :], PA[:, :]))
                    st.last(v.reduce_max(KMXC[:, oc - 8:oc - 7], QKE[:, :],
                                         axis=AX.X, apply_absolute_value=True))

            self.sem("SQF")
            self.semc["SQF"] += 16
            sqf = self.semc["SQF"]

            @self.block("sync", 0, waits=[("V", qk2.total)], tag=f"l1fk{oc}s")
            def _(e, st, oc=oc, isq=isq):
                dst = qf_d if isq else kf_d
                ro = (oc % 8) * 128
                e.dma_start(out=dst.ap()[ro:ro + 128, :], in_=QKE[:, :]
                            ).then_inc(self.semh["SQF"], 16)

        @self.block("vector", 2, waits=[("V", self.val("V"))], tag="l1f_b1")
        def sb1(v, st):
            st(v.reduce_max(KMXC[:, 0:1], KMXC[:, :], axis=AX.X))
            st.last(v.tensor_copy(SCB[:, 0:1], KMXC[:, 0:1]))

        @self.block("tensor", 1, waits=[("V", sb1.total)], tag="l1f_b2")
        def sb2(t, st):
            st.last(t.transpose(PD[0:1, 0:128], SCB[:, 0:1], IDENT[:, :]))

        @self.block("vector", 2, waits=[("T", sb2.total)], tag="l1f_b3")
        def sb3(v, st):
            st(v.reduce_max(CCS[0:1, 2:3], PD[0:1, 0:128], axis=AX.X))
            st.last(v.tensor_copy(SCB[:, 0:16], MAXN2[:, :]))

        @self.block("tensor", 1, waits=[("V", sb3.total)], tag="l1f_b4")
        def sb4(t, st):
            st.last(t.transpose(PD[0:16, 0:128], SCB[:, 0:16], IDENT[:, :]))

        @self.block("vector", 1, waits=[("T", sb4.total)], tag="l1f_b5")
        def sb5(v, st):
            st.last(v.reduce_max(SRW[:, :], PD[0:16, 0:128], axis=AX.X))

        @self.block("scalar", 1, waits=[("V", sb5.total)], tag="l1f_b6")
        def sb6(s, st):
            st.last(s.activation(SRW[:, :], SRW[:, :], AF.Sqrt))

        @self.block("tensor", 1, waits=[("S", sb6.total)], tag="l1f_b7")
        def sb7(t, st):
            st.last(t.matmul(PB[0:16, 4:5], ONESFR[0:1, 0:16], CCS[0:1, 2:3],
                             start=True, stop=True))

        @self.block("vector", 2, waits=[("T", sb7.total)], tag="l1f_b8")
        def sb8(v, st):
            st(v.scalar_tensor_tensor(SRW[:, :], SRW[:, :], -1.0, PB[0:16, 4:5],
                                      ALU.mult, ALU.mult))
            st.last(v.tensor_copy(SRB[:, :], SRW[:, :]))

        @self.block("tensor", 1, waits=[("V", sb8.total)], tag="l1f_b9")
        def sb9(t, st):
            st.last(t.transpose(PD[0:1, 128:144], SRB[:, :],
                                IDENT[0:16, 0:16]))

        @self.block("vector", 1, waits=[("T", sb9.total)], tag="l1f_b10")
        def sb10(v, st):
            st.last(v.tensor_copy(SRR[:, :], PD[0:1, 128:144]))

        @self.block("tensor", 1, waits=[("V", sb10.total)], tag="l1f_b11")
        def sb11(t, st):
            st.last(t.matmul(PB[:, 8:24], ONESFR[:, :], SRR[:, :],
                             start=True, stop=True))

        @self.block("vector", 1, waits=[("T", sb11.total)], tag="l1f_b12")
        def sb12(v, st):
            st.last(v.tensor_copy(BCOLS[:, :], PB[:, 8:24]))

        self.semc["LWF0"] += 16
        lwf_v = self.semc["LWF0"]

        @self.block("gpsimd", 0,
                    waits=[("T", self.val("T")), ("SQF", self.val("SQF"))],
                    tag="l1f_wvload")
        def _(g, st):
            g.dma_start(out=WPART[:, :].rearrange("p (f o) -> p f o", o=1024),
                        in_=wqkv[:, 2048:3072].rearrange("(f p) o -> p f o", p=128)
                        ).then_inc(self.semh["LWF0"], 16)

        for c in range(NT):
            for og in range(2):
                @self.block("tensor", 1,
                            waits=[("LWF0", lwf_v), ("V", self.val("V"))],
                            tag=f"l1fv{c}o{og}")
                def vv1(t, st, c=c, og=og):
                    for fc in range(NF):
                        ins = t.matmul(
                            PB[:, 0:512],
                            XNT[:, fc * 2048 + c * 128:fc * 2048 + (c + 1) * 128],
                            WPART[:, fc * 1024 + og * 512:fc * 1024 + (og + 1) * 512],
                            start=(fc == 0), stop=(fc == NF - 1))
                    st.last(ins)

                @self.block("vector", 2, waits=[("T", vv1.total),
                                                ("V", msk.total)],
                            tag=f"l1fv{c}o{og}v")
                def vv2(v, st, c=c, og=og):
                    pbv = PB[:, 0:512].rearrange("p (h e) -> p h e", e=64)
                    vamv = VAM[:, c * 1040 + og * 520:c * 1040 + (og + 1) * 520] \
                        .rearrange("p (h e) -> p h e", e=65)[:, :, 0:64]
                    vpmv = VPM[:, c * 1040 + og * 520:c * 1040 + (og + 1) * 520] \
                        .rearrange("p (h e) -> p h e", e=65)[:, :, 0:64]
                    st(v.tensor_scalar(vamv, pbv, RGW[:, c:c + 1], None, ALU.mult))
                    st.last(v.tensor_scalar(vpmv, pbv, PMK[:, c:c + 1], None,
                                            ALU.mult))

            @self.block("vector", 2, waits=[("V", self.val("V"))],
                        tag=f"l1fv{c}m")
            def vv3(v, st, c=c):
                vamc = VAM[:, c * 1040:(c + 1) * 1040].rearrange(
                    "p (h e) -> p h e", e=65)[:, :, 64:65]
                vpmc = VPM[:, c * 1040:(c + 1) * 1040].rearrange(
                    "p (h e) -> p h e", e=65)[:, :, 64:65]
                st(v.tensor_copy(vamc, RGW[:, c:c + 1].rearrange(
                    "p o -> p o ()" if False else "p (a o) -> p a o", o=1)
                    .broadcast_to([128, 16, 1])))
                st.last(v.tensor_copy(vpmc, PMK[:, c:c + 1].rearrange(
                    "p (a o) -> p a o", o=1).broadcast_to([128, 16, 1])))

        @self.block("vector", 1, waits=[("V", self.val("V"))], tag="l1f_wqb")
        def wqb(v, st):
            st.last(v.tensor_copy(SCB[:, 0:16], WQCL[:, :]))

        self.sem("LWQR")
        self.semc["LWQR"] += 16
        lwqr1 = self.semc["LWQR"]

        @self.block("sync", 0, waits=[("V", wqb.total)], tag="l1f_wqs")
        def _(e, st):
            e.dma_start(out=wq_d.ap(), in_=SCB[:, 0:16]
                        ).then_inc(self.semh["LWQR"], 16)

        self.barrier("l1f_s1", extra=[("SQF", self.val("SQF")),
                                      ("LWQR", self.val("LWQR"))])
        sub1.close()
        self.cur_es = outer1

        sub2 = contextlib.ExitStack()
        self.cur_es = sub2
        PTB = self.sbuf("PTB", [128, 16 * 1024], BF16)
        KFH = self.sbuf("KFH", [64, 2048], BF16)
        QFH = self.sbuf("QFH", [64, 2048], BF16)
        OPOSS = self.sbuf("OPOSS", [64, 1024], BF16)
        ONEGS = self.sbuf("ONEGS", [64, 1024], BF16)
        OCH = OPOSS
        R2 = self.sbuf("R2", [1, 1024], F32)
        R1B = self.sbuf("R1B", [1, 1024], BF16)
        R2B = self.sbuf("R2B", [1, 1024], BF16)
        WQROW = self.sbuf("WQROW", [1, 2048], BF16)
        WQ1ROW = self.sbuf("WQ1ROW", [1, 2048], BF16)

        self.semc["LWQR"] += 16
        lwqr2 = self.semc["LWQR"]

        @self.block("gpsimd", 0, waits=[("LWQR", lwqr1)], tag="l1f_wql")
        def _(g, st):
            g.dma_start(out=WQROW[:, :].rearrange("a (c p) -> a c p", p=128),
                        in_=wq_d.ap().rearrange("p c -> () c p"),
                        allow_slow_non_contiguous=True
                        ).then_inc(self.semh["LWQR"], 16)

        @self.block("vector", 1, waits=[("LWQR", lwqr2)], tag="l1f_wqr")
        def wqr(v, st):
            st.last(v.tensor_scalar(WQ1ROW[:, :], WQROW[:, :], -1.0, 1.0,
                                    ALU.mult, ALU.add))

        self.sem("SOC")
        self.sem("LKH")
        exp_hist = [0, 0]
        hs1_last = {}
        hcbA_prev = 0
        pending_hbc = [None]
        for h in range(H):
            self.semc["LKH"] += 32
            lkh = self.semc["LKH"]
            kbuf = 0

            @self.block("gpsimd", 0,
                        waits=[("T", hs1_last.get(h - 1, 0))],
                        tag=f"l1fh{h}g")
            def _(g, st, h=h, kbuf=kbuf):
                g.dma_start(out=KFH[:, kbuf:kbuf + 2048],
                            in_=kf_d.ap()[64 * h:64 * h + 64, :]
                            ).then_inc(self.semh["LKH"], 16)
                g.dma_start(out=QFH[:, kbuf:kbuf + 2048],
                            in_=qf_d.ap()[64 * h:64 * h + 64, :]
                            ).then_inc(self.semh["LKH"], 16)

            for half in range(2):
                for kt in range(NT):
                    for q in range(2):
                        gstep = kt * 2 + q
                        bank = gstep % 2

                        @self.block("tensor", 1,
                                    waits=[("LKH", lkh if gstep == 0 else 0),
                                           ("S", exp_hist[-2])],
                                    tag=f"l1fh{h}f{half}s{gstep}")
                        def hs1(t, st, h=h, half=half, kt=kt, q=q, bank=bank,
                                kbuf=kbuf):
                            st.last(t.matmul(
                                PB[:, bank * 512:(bank + 1) * 512],
                                KFH[:, kbuf + kt * 128:kbuf + (kt + 1) * 128],
                                QFH[:, kbuf + half * 1024 + q * 512:
                                    kbuf + half * 1024 + (q + 1) * 512],
                                start=True, stop=True))

                        @self.block("scalar", 1, waits=[("T", hs1.total)],
                                    tag=f"l1fh{h}f{half}e{gstep}")
                        def hs2(s, st, h=h, kt=kt, q=q, bank=bank):
                            st.last(s.activation(
                                PTB[:, kt * 1024 + q * 512:
                                    kt * 1024 + (q + 1) * 512],
                                PB[:, bank * 512:(bank + 1) * 512], AF.Exp,
                                bias=BCOLS[:, h:h + 1]))
                        exp_hist.append(hs2.total)
                        hs1_last[h] = hs1.total
                        if gstep == 12 and pending_hbc[0] is not None:
                            pending_hbc[0]()
                            pending_hbc[0] = None

                @self.block("tensor", 1, waits=[("S", exp_hist[-1]),
                                                ("V", getattr(self, "_hcc_last", 0))],
                            tag=f"l1fh{h}f{half}pv")
                def hpv(t, st, h=h):
                    for kt in range(NT):
                        for q in range(2):
                            t.matmul(PA[0:65, q * 512:(q + 1) * 512],
                                     VPM[:, kt * 1040 + h * 65:
                                         kt * 1040 + (h + 1) * 65],
                                     PTB[:, kt * 1024 + q * 512:
                                         kt * 1024 + (q + 1) * 512],
                                     start=(kt == 0), stop=(kt == NT - 1),
                                     skip_group_check=True)
                            ins = t.matmul(PA[0:65, 1024 + q * 512:
                                              1024 + (q + 1) * 512],
                                           VAM[:, kt * 1040 + h * 65:
                                               kt * 1040 + (h + 1) * 65],
                                           PTB[:, kt * 1024 + q * 512:
                                               kt * 1024 + (q + 1) * 512],
                                           start=(kt == 0), stop=(kt == NT - 1),
                                           skip_group_check=True)
                    st.last(ins)

                @self.block("vector", 5, waits=[("T", hpv.total),
                                                ("SOC", self.val("SOC"))],
                            tag=f"l1fh{h}f{half}ca")
                def hcbA(v, st, h=h, half=half):
                    st(v.tensor_copy(OPOSS[:, :], PA[0:64, 0:1024]))
                    st(v.tensor_tensor(ONEGS[:, :], PA[0:64, 1024:2048],
                                       OPOSS[:, :], ALU.subtract))
                    with self.nc.allow_low_precision(reason="bf16 softmax recip row"):
                        st(v.reciprocal(R1B[:, :], PA[64:65, 0:1024]))
                    st(v.tensor_copy(R2[:, :], PA[64:65, 0:1024]))
                    st.last(v.tensor_tensor(R2[:, :], PA[64:65, 1024:2048],
                                            R2[:, :], ALU.subtract))

                hcbA_prev = hcbA.total

                @self.block("vector", 3, waits=[], tag=f"l1fh{h}f{half}cb")
                def hcbB(v, st, h=h, half=half):
                    st(v.reciprocal(R2[:, :], R2[:, :]))
                    st(v.tensor_tensor(R1B[:, :], R1B[:, :],
                                       WQROW[:, half * 1024:(half + 1) * 1024],
                                       ALU.mult))
                    st.last(v.tensor_tensor(R2B[:, :], R2[:, :],
                                            WQ1ROW[:, half * 1024:(half + 1) * 1024],
                                            ALU.mult))

                def mk_tail(h=h, half=half, hcbB_total=hcbB.total):
                    @self.block("tensor", 2, waits=[("V", hcbB_total)],
                                tag=f"l1fh{h}f{half}bc")
                    def hbc(t, st):
                        t.matmul(PA[0:64, 0:512], ONESROW[0:1, 0:64],
                                 R1B[:, 0:512], start=True, stop=True)
                        st(t.matmul(PA[0:64, 512:1024], ONESROW[0:1, 0:64],
                                    R1B[:, 512:1024], start=True, stop=True))
                        t.matmul(PA[0:64, 1024:1536], ONESROW[0:1, 0:64],
                                 R2B[:, 0:512], start=True, stop=True)
                        st.last(t.matmul(PA[0:64, 1536:2048],
                                         ONESROW[0:1, 0:64],
                                         R2B[:, 512:1024],
                                         start=True, stop=True))

                    @self.block("vector", 3, waits=[("T", hbc.total)],
                                tag=f"l1fh{h}f{half}cc")
                    def hcc(v, st):
                        st(v.tensor_tensor(OPOSS[:, :], OPOSS[:, :],
                                           PA[0:64, 0:1024], ALU.mult))
                        st(v.tensor_tensor(ONEGS[:, :], ONEGS[:, :],
                                           PA[0:64, 1024:2048], ALU.mult))
                        st.last(v.tensor_tensor(OCH[:, :], OPOSS[:, :],
                                                ONEGS[:, :], ALU.add))
                    self._hcc_last = hcc.total

                    self.semc["SOC"] += 16

                    @self.block("sync", 0, waits=[("V", hcc.total)],
                                tag=f"l1fh{h}f{half}so")
                    def _(e, st, h=h, half=half):
                        e.dma_start(out=oc_d.ap()[64 * h:64 * h + 64,
                                                  half * 1024:(half + 1) * 1024],
                                    in_=OCH[:, :]).then_inc(self.semh["SOC"], 16)
                pending_hbc[0] = mk_tail

        pending_hbc[0]()
        pending_hbc[0] = None

        self.barrier("l1f_s2", extra=[("SOC", self.val("SOC"))])
        sub2.close()
        sub3 = contextlib.ExitStack()
        self.cur_es = sub3
        WPROJ = self.sbuf("WPROJ", [128, 8 * 1024], BF16)
        OCT = self.sbuf("OCT", [128, 1024], BF16)
        BPROW = self.sbuf("BPROW", [1, 1024], BF16)

        self.sem("LG5")
        self.semc["LG5"] += 32
        lg5 = self.semc["LG5"]

        @self.block("gpsimd", 0, waits=[("SOC", self.val("SOC"))],
                    tag="l1f_wpload")
        def _(g, st):
            g.dma_start(out=WPROJ[:, :].rearrange("p (f o) -> p f o", o=1024),
                        in_=self.dram["l1f_wproj"].ap()
                        .rearrange("(f p) o -> p f o", p=128)
                        ).then_inc(self.semh["LG5"], 16)
            g.dma_start(out=BPROW[:, :], in_=self.dram["l1f_bproj"].ap()
                        ).then_inc(self.semh["LG5"], 16)

        for c in range(NT):
            self.sem("LOC")
            self.semc["LOC"] += 16
            loc = self.semc["LOC"]

            @self.block("gpsimd", 0,
                        waits=[("T", self.val("T")), ("SOC", self.val("SOC"))],
                        tag=f"l1fo{c}g")
            def _(g, st, c=c):
                g.dma_start(out=OCT[:, :].rearrange("p (f q) -> p f q", q=128),
                            in_=oc_d.ap()[:, c * 128:(c + 1) * 128]
                            .rearrange("(f p) q -> p f q", p=128)
                            ).then_inc(self.semh["LOC"], 16)

            @self.block("tensor", 1, waits=[("LOC", loc), ("LG5", lg5),
                                            ("V", self.val("V"))],
                        tag=f"l1fo{c}t")
            def op1(t, st, c=c):
                for fc in range(NF):
                    for q in range(2):
                        t.matmul(PA[:, q * 512:(q + 1) * 512],
                                 OCT[:, fc * 128:(fc + 1) * 128],
                                 WPROJ[:, fc * 1024 + q * 512:
                                       fc * 1024 + (q + 1) * 512],
                                 start=(fc == 0), stop=False)
                for q in range(2):
                    ins = t.matmul(PA[:, q * 512:(q + 1) * 512], ONESROW[:, :],
                                   BPROW[:, q * 512:(q + 1) * 512],
                                   start=False, stop=(q == 1))
                st.last(ins)

            self.postln_residual(c, PA, D, op1.total, f"l1fI{c}")

        self.barrier("l1f")
        sub3.close()
        self.cur_es = outer1
        self.pop_scope()

    def phase_store(self):
        outap = self.dram["out"].ap().rearrange("(t p) d -> p t d", p=128)
        self.sem("STO")
        self.semc["STO"] = 16

        @self.block("sync", 0, waits=[("V", self.val("V"))] +
                    ([("SDC", self.val("SDC"))] if "SDC" in self.semh else []),
                    tag="store")
        def _(e, st):
            e.dma_start(out=outap,
                        in_=self.sb["XS"][:, :].rearrange("p (t d) -> p t d", d=D)
                        ).then_inc(self.semh["STO"], 16)
            e.wait_ge(self.semh["STO"], 16)


def build(upto="l0a"):
    return MK(upto).build()


def input_map(inp, core, upto="l0a"):
    m = {
        "x": np.ascontiguousarray(inp["L0_attn_x"][core]),
        "l0a_wqkv": inp["L0_attn_Wqkv_w"],
        "l0a_bqkv": inp["L0_attn_Wqkv_b"].reshape(1, -1),
        "l0a_wo": inp["L0_attn_out_w"],
        "l0a_bo": inp["L0_attn_out_b"].reshape(1, -1),
        "l0a_gamma": inp["L0_attn_gamma_raw"].reshape(1, -1),
    }
    nb = ["l0a", "l0f", "l1a", "l1f"].index(upto) + 1
    if nb >= 2:
        m.update({"l0f_w1": inp["L0_ffn_w1"], "l0f_b1": inp["L0_ffn_b1"].reshape(1, -1),
                  "l0f_w2": inp["L0_ffn_w2"], "l0f_b2": inp["L0_ffn_b2"].reshape(1, -1)})
    if nb >= 3:
        m.update({"l1a_wg": inp["L1_attn_Wg"], "l1a_bg": inp["L1_attn_bg"].reshape(1, -1),
                  "l1a_wv": inp["L1_attn_Wv"], "l1a_bv": inp["L1_attn_bv"].reshape(1, -1),
                  "l1a_wd": inp["L1_attn_Wd"], "l1a_bd": inp["L1_attn_bd"].reshape(1, -1)})
    if nb >= 4:
        m.update({"l1f_wqkv": inp["L1_ffn_W_qkv"],
                  "l1f_g": inp["L1_ffn_g_info"][0, 0].reshape(1, -1),
                  "l1f_wproj": inp["L1_ffn_W_proj"],
                  "l1f_bproj": inp["L1_ffn_b_proj"].reshape(1, -1)})
    return m



def kernel(**inputs):
    from concourse.bass_utils import run_bass_kernel_spmd
    nc = build("l1f")
    in_maps = [input_map(inputs, core, "l1f") for core in range(8)]
    res = run_bass_kernel_spmd(nc, in_maps, core_ids=list(range(8)))
    out = np.stack([np.asarray(r["out"], np.float32) for r in res.results], axis=0)
    return out
